# version 04
# baseline (speedup 1.0000x reference)
from contextlib import ExitStack

import numpy as np
import ml_dtypes
import concourse.bass as bass
import concourse.mybir as mybir
from concourse.bass_utils import run_bass_kernel_spmd

F32 = mybir.dt.float32
BF16 = mybir.dt.bfloat16
I32 = mybir.dt.int32
AF = mybir.ActivationFunctionType
ALU = mybir.AluOpType
AX = mybir.AxisListType

D = 1024
H = 16
DK = 64
DV = 128
C = 128
DH = 4096
EPS = 1e-5
NCORES = 8

ENGS = ("T", "V", "S", "P", "SP")


class MS:
    __slots__ = ("eng", "val")

    def __init__(self, eng, val):
        self.eng, self.val = eng, val


class _EngState:
    def __init__(self, name):
        self.name = name
        self.ops = []
        self.sem_val = 0
        self.dma_val = 0
        self.pend_w = []
        self.pend_r = []


def _ivl(label):
    if isinstance(label, tuple):
        return label
    return (label, None, None)


def _conflict(a, b):
    if a[0] != b[0]:
        return False
    if a[1] is None or b[1] is None:
        return True
    return a[1] < b[2] and b[1] < a[2]


class Plan:
    def __init__(self, nc, sems, carry=None):
        self.nc = nc
        self.sems = sems
        self.e = {n: _EngState(n) for n in ENGS}
        self.dsem = {}
        if carry is not None:
            for n in ENGS:
                self.e[n].sem_val = carry.e[n].sem_val
                self.e[n].dma_val = carry.e[n].dma_val
            self.dsem = dict(carry.dsem)

    def _emit(self, eng, fn):
        self.e[eng].ops.append(fn)

    def _auto_drain(self, eng, reads, writes):
        st = self.e[eng]
        rds = [_ivl(x) for x in reads]
        wrs = [_ivl(x) for x in writes]
        hit = any(_conflict(w, r) for w in st.pend_w for r in rds) \
            or any(_conflict(w, v) for w in st.pend_w for v in wrs) \
            or any(_conflict(r, v) for r in st.pend_r for v in wrs)
        if hit:
            self._emit(eng, lambda obj: obj.drain())
            st.pend_w = []
            st.pend_r = []
        if len(st.pend_w) > 96:
            st.pend_w = [(n, None, None) for n in
                         {t[0] for t in st.pend_w}]
        if len(st.pend_r) > 96:
            st.pend_r = [(n, None, None) for n in
                         {t[0] for t in st.pend_r}]
        st.pend_w += wrs
        st.pend_r += rds

    def wait(self, eng, ms):
        if ms is None:
            return
        if isinstance(ms, MS):
            ms = [ms]
        for m in ms:
            if m is None:
                continue
            sem = self.sems[m.eng]
            val = m.val
            if m.eng in ("SPd", "Pd"):
                val = self.e[m.eng[:-1]].dma_val
            self._emit(eng, lambda obj, s=sem, v=val: obj.wait_ge(s, v))

    def op(self, eng, fn, reads=(), writes=(), wait=None, inc=False,
           nodrain=False):
        self.wait(eng, wait)
        if not nodrain:
            self._auto_drain(eng, reads, writes)
        st = self.e[eng]
        if inc:
            st.sem_val += 1
            sem = self.sems[eng]

            def wrapped(obj, fn=fn, sem=sem):
                fn(obj).then_inc(sem, 1)
            self._emit(eng, wrapped)
            return MS(eng, st.sem_val)
        self._emit(eng, lambda obj, fn=fn: fn(obj))
        return None

    def dma(self, eng, out_ap, in_ap, wait=None, slow=False, sem=None):
        self.wait(eng, wait)
        key = sem if sem is not None else eng + "d"
        if key in (eng + "d",):
            self.e[eng].dma_val += 16
            val = self.e[eng].dma_val
        else:
            self.dsem[key] = self.dsem.get(key, 0) + 16
            val = self.dsem[key]
        h = self.sems[key]
        self._emit(eng, lambda obj, o=out_ap, i=in_ap, s=h, sl=slow:
                   obj.dma_start(out=o, in_=i,
                                 allow_slow_non_contiguous=sl)
                   .then_inc(s, 16))
        return MS(key, val)

    def drain(self, eng):
        self._emit(eng, lambda obj: obj.drain())
        self.e[eng].pend_w = []
        self.e[eng].pend_r = []

    def barrier_ms(self, eng):
        st = self.e[eng]
        st.sem_val += 1
        sem = self.sems[eng]
        self._emit(eng, lambda obj, s=sem: obj.sem_inc(s, 1))
        return MS(eng, st.sem_val)

    def run(self, block):
        def mk(name):
            def body(obj):
                for fn in self.e[name].ops:
                    fn(obj)
            return body
        block.tensor(mk("T"))
        block.vector(mk("V"))
        block.scalar(mk("S"))
        block.gpsimd(mk("P"))
        block.sync(mk("SP"))


def make_consts(T):
    h = np.arange(H, dtype=np.float64)
    gamma = 1.0 - np.exp2(-5.0 - h)
    j = np.arange(C, dtype=np.float64)

    inv = 1.0 / 10000.0 ** (np.arange(0, DK, 2, dtype=np.float64) / DK)
    t = np.arange(T, dtype=np.float64)
    f = np.outer(t, inv)
    costab = np.tile(np.cos(f), (1, H)).astype(np.float32)
    sintab = np.tile(np.sin(f), (1, H)).astype(np.float32)

    dkve = np.repeat(gamma[None, :] ** (C - 1 - j[:, None]), DK,
                     axis=1).astype(np.float32)
    i_ = np.arange(C, dtype=np.float64)
    dq2 = (np.repeat(gamma[None, :] ** (i_[:, None] + 1 - C), DV, axis=1)
           / 8.0).astype(np.float32)
    gct = np.ascontiguousarray(
        np.repeat(np.broadcast_to(gamma[None, :] ** C, (DK, H)), DV,
                  axis=1)).astype(np.float32)
    m = (np.arange(C)[None, :] >= np.arange(C)[:, None]).astype(np.float32)
    mask2 = np.ascontiguousarray(np.concatenate([m, m], axis=1))
    ident = np.eye(128, dtype=ml_dtypes.bfloat16)
    ones1 = np.ones((1, 128), dtype=ml_dtypes.bfloat16)
    return {"costab": costab, "sintab": sintab, "dkve": dkve, "dq2": dq2,
            "gct": gct, "mask2": mask2, "ident": ident, "ones1": ones1}


def build_program(T=4096):
    assert T % 256 == 0
    NB = T // 128
    NW = T // 256

    nc = bass.Bass("TRN2", target_bir_lowering=False)

    din = {}
    for nm, shape, dt in [
            ("x", [T, D], F32), ("wq", [D, D], F32), ("wk", [D, D], F32),
            ("wv", [D, 2 * D], F32), ("wg", [D, 2 * D], F32),
            ("wo", [2 * D, D], F32), ("gnw", [DV, 1], F32),
            ("w1", [DH, D], F32), ("cw", [DH, 3], F32), ("cb", [DH, 1], F32),
            ("w2", [D, DH], F32), ("costab", [T, 512], F32),
            ("sintab", [T, 512], F32), ("dkve", [128, 1024], F32),
            ("dq2", [128, 2048], F32), ("gct", [64, 2048], F32),
            ("mask2", [128, 256], F32), ("ident", [128, 128], BF16),
            ("ones1", [1, 128], BF16)]:
        din[nm] = nc.dram_tensor(nm, shape, dt, kind="ExternalInput").ap()

    dy = nc.dram_tensor("y", [T, D], F32, kind="ExternalOutput").ap()
    dx1 = nc.dram_tensor("x1s", [T, D], F32, kind="Internal").ap()
    dw1t = nc.dram_tensor("w1t", [D, DH], BF16, kind="Internal").ap()
    dw2t = nc.dram_tensor("w2t", [DH, D], BF16, kind="Internal").ap()

    psb = lambda nm, shape, dt=F32: nc.alloc_sbuf_tensor(
        nm, list(shape), dt).ap()
    WBIG = psb("WBIG", (128, 65536), BF16)
    WQ = WBIG[:, 0:8192]
    WK = WBIG[:, 8192:16384]
    WV = WBIG[:, 16384:32768]
    WG = WBIG[:, 32768:49152]
    WO = WBIG[:, 49152:65536]
    W1T = WBIG[:, 0:32768]
    W2T = WBIG[:, 32768:65536]

    IDENT = psb("IDENT", (128, 128), BF16)
    ONE1 = psb("ONE1", (1, 128), BF16)
    ONE1F = psb("ONE1F", (1, 128))
    NEGH = psb("NEGH", (128, 26))
    stats = psb("stats", (128, 96))
    bn6 = psb("bn6", (128, 12))
    bn2 = psb("bn2", (128, 2))
    WSC = psb("WSC", (128, 8))
    CW = psb("CW", (128, 96))
    CB = psb("CB", (128, 32))

    PS = nc.alloc_psum_tensor("PS", [128, 3072], F32).ap()
    PT = nc.alloc_psum_tensor("PT", [128, 2048], BF16).ap()
    ptA = PT[:, 0:1024]
    ptB = PT[:, 1024:2048]

    def slot(i):
        return PS[:, 512 * i:512 * (i + 1)]

    sems = {}
    for n in ["T", "V", "S", "P", "SP", "SPd", "Pd", "xd0", "xd1", "td",
              "x1d", "wd0", "wd1", "cwd", "srd", "od"]:
        sems[n] = nc.semaphore("sem_" + n).__enter__()

    ctx1 = ExitStack()
    s1 = lambda nm, shape, dt=F32: ctx1.enter_context(
        nc.sbuf_tensor(nm, list(shape), dt)).ap()

    xt = s1("xt", (128, 2 * D))
    ln1 = s1("ln1", (128, D), BF16)
    ln1fm = ln1
    qrt = s1("qrt", (128, D), BF16)
    krt = s1("krt", (128, D), BF16)
    ropem = s1("ropem", (128, 1024), BF16)
    qfm = s1("qfm", (64, 2048), BF16)
    kfm = s1("kfm", (64, 2048), BF16)
    vtm = s1("vtm", (128, 2048), BF16)
    gsil = s1("gsil", (128, 2048), BF16)
    Ab = s1("Ab", (128, 512), BF16)
    Sst = s1("Sst", (64, 2048))
    Sbb = s1("Sbb", (64, 2048), BF16)
    otm = s1("otm", (128, 2048), BF16)
    og = s1("og", (128, 2048), BF16)
    ogfm = og
    ln2b = s1("ln2b", (128, D), BF16)
    x1t = s1("x1t", (128, D))
    ctab = s1("ctab", (128, 512))
    stab = s1("stab", (128, 512))
    DKVEB = s1("DKVEB", (128, 1024), BF16)
    DQ2B = s1("DQ2B", (128, 2048), BF16)
    GCTB = s1("GCTB", (64, 2048), BF16)
    MASKB = s1("MASKB", (128, 256), BF16)
    GNCOL = s1("GNCOL", (128, 1))

    p = Plan(nc, sems)

    p.dma("SP", IDENT, din["ident"])
    p.dma("SP", ONE1, din["ones1"])
    p.dma("SP", GNCOL, din["gnw"])
    p.dma("SP", xt[:, 0:1024], din["dkve"])
    p.dma("SP", ctab[:, 0:256], din["mask2"])
    p.dma("SP", xt[:, 1024:2048], din["dq2"][:, 0:1024])
    p.dma("SP", x1t[:, 0:1024], din["dq2"][:, 1024:2048])
    call = p.dma("SP", Sst[0:64, 0:2048], din["gct"])
    p.op("V", lambda v: v.tensor_copy(DKVEB[:], xt[:, 0:1024]),
         reads=("xt",), writes=("DKVEB",), wait=[call])
    p.op("V", lambda v: v.tensor_copy(MASKB[:], ctab[:, 0:256]),
         reads=("ctab",), writes=("MASKB",))
    p.op("V", lambda v: v.tensor_copy(DQ2B[:, 0:1024], xt[:, 1024:2048]),
         reads=("xt",), writes=("DQ2B",))
    p.op("V", lambda v: v.tensor_copy(DQ2B[:, 1024:2048], x1t[:, 0:1024]),
         reads=("x1t",), writes=("DQ2B",))
    cast4 = p.op("V", lambda v: v.tensor_copy(GCTB[:], Sst[0:64, 0:2048]),
                 reads=("Sst",), writes=("GCTB",), inc=True)
    consts_done = cast4

    wms = []
    for dst, src, w in [(WQ, din["wq"], 1024), (WK, din["wk"], 1024),
                        (WV, din["wv"], 2048), (WG, din["wg"], 2048),
                        (WO, din["wo"], 1024)]:
        wms.append(p.dma("P", dst.rearrange("p (c n) -> p c n", n=w),
                         src.rearrange("(c p) n -> p c n", p=128)))
    p.op("V", lambda v: v.memset(NEGH[:, 0:16], -0.5),
         writes=(("NEGH", 0, 16),))
    p.op("V", lambda v: v.memset(NEGH[:, 16:17], -1.0),
         writes=(("NEGH", 16, 17),))
    p.op("V", lambda v: v.memset(NEGH[:, 17:18], 127.0),
         writes=(("NEGH", 17, 18),))
    p.op("V", lambda v: v.memset(NEGH[:, 18:19], 1.0 / 127.0),
         writes=(("NEGH", 18, 19),))
    p.op("V", lambda v: v.memset(NEGH[:, 19:21], -1.0),
         writes=(("NEGH", 19, 21),))
    p.op("V", lambda v: v.memset(NEGH[:, 21:23], 127.0),
         writes=(("NEGH", 21, 23),))
    p.op("V", lambda v: v.memset(NEGH[:, 23:25], 1.0 / 127.0),
         writes=(("NEGH", 23, 25),))
    p.op("V", lambda v: v.memset(Sst[:], 0.0), writes=("Sst",),
         wait=[consts_done])
    p.op("V", lambda v: v.memset(ONE1F[:], 1.0), writes=("ONE1F",))
    wo_sc_ms = None
    for c in range(16):
        wo_sc_ms = p.op("V", lambda v, c=c: v.tensor_scalar_mul(
            WO[:, c * 1024:(c + 1) * 1024], WO[:, c * 1024:(c + 1) * 1024],
            GNCOL[:, 0:1]),
            reads=("WBIG", "GNCOL"), writes=("WBIG",),
            wait=[wms[4], consts_done] if c == 0 else None,
            inc=(c == 15))
    wready = wms[:4] + [wo_sc_ms, consts_done]

    slot_free = [None] * 6
    pt_free = {"A": None, "B": None}
    blk_end_V = {}
    x1_dma = {}
    S_done = {}
    ab_free = {}
    rope_done = {}

    for b in range(NB):
        pb = b % 2
        so = 32 * pb
        xb = xt[:, pb * D:(pb + 1) * D]
        x1b = x1t[:]
        ctb = ctab[:]
        stb = stab[:]

        w = [blk_end_V[b - 2]] if b >= 2 else ([consts_done] if b == 0
                                               else None)
        x_ms = p.dma("SP", xb, din["x"][b * 128:(b + 1) * 128, :], wait=w,
                     sem="xd%d" % pb)
        tw0 = [rope_done[b - 1]] if b >= 1 else None
        ct_ms = p.dma("SP", ctb, din["costab"][b * 128:(b + 1) * 128, :],
                      wait=tw0, sem="td")
        st_ms = p.dma("SP", stb, din["sintab"][b * 128:(b + 1) * 128, :],
                      sem="td")

        p.op("V", lambda v, xb=xb: v.bn_stats(bn6[:, 0:6], xb[:, 0:512]),
             reads=("xt",), writes=("bn6",), wait=[x_ms])
        p.op("V", lambda v, xb=xb: v.bn_stats(bn6[:, 6:12], xb[:, 512:1024]),
             reads=("xt",), writes=("bn6",))
        p.op("V", lambda v: v.bn_aggr(bn2[:], bn6[:]),
             reads=("bn6",), writes=("bn2",))
        ve = p.op("V", lambda v, so=so: v.tensor_scalar(
            stats[:, so:so + 1], bn2[:, 1:2], EPS, None, op0=ALU.add),
            reads=("bn2",), writes=("stats",), inc=True)
        pw = p.op("P", lambda g, so=so: g.tensor_tensor(
            stats[:, so + 1:so + 2], stats[:, so:so + 1], NEGH[:, 0:1],
            op=ALU.pow), wait=[ve], inc=True)
        bc = p.op("V", lambda v, so=so: v.scalar_tensor_tensor(
            stats[:, so + 2:so + 3], bn2[:, 0:1], -1.0,
            stats[:, so + 1:so + 2], op0=ALU.mult, op1=ALU.mult),
            reads=("bn2", "stats"), writes=("stats",), wait=[pw], inc=True)
        ln1_ms = p.op("S", lambda s, xb=xb, so=so: s.activation(
            ln1[:], xb, AF.Identity, bias=stats[:, so + 2:so + 3],
            scale=stats[:, so + 1:so + 2]),
            reads=("xt", "stats"), writes=("ln1",), wait=[bc], inc=True)

        tw = [ln1_ms]
        if pt_free["A"] is not None:
            tw.append(pt_free["A"])
        tms = None
        for c in range(8):
            tms = p.op("T", lambda t, c=c: t.transpose(
                ptA[:, c * 128:(c + 1) * 128], ln1[:, c * 128:(c + 1) * 128],
                IDENT[:]),
                reads=("ln1", "IDENT"), writes=("PT",),
                wait=tw if c == 0 else None, inc=(c == 7))
        ln1fm_ms = p.op("S", lambda s: s.activation(ln1fm[:], ptA,
                                                    AF.Identity),
                        reads=("PT",), writes=("ln1",), wait=[tms],
                        inc=True)
        pt_free["A"] = ln1fm_ms

        group_src = [(WQ, 1024, 0), (WQ, 1024, 512), (WK, 1024, 0),
                     (WK, 1024, 512), (WV, 2048, 0), (WV, 2048, 512),
                     (WV, 2048, 1024), (WV, 2048, 1536), (WG, 2048, 0),
                     (WG, 2048, 512), (WG, 2048, 1024), (WG, 2048, 1536)]
        proj_ms = {}

        def emit_group(gi):
            Wsrc, wmul, coff = group_src[gi]
            si = gi % 6
            wl = [ln1fm_ms]
            if b == 0 and gi == 0:
                wl = wl + wready
            if slot_free[si] is not None:
                wl.append(slot_free[si])
            ms = None
            for c in range(8):
                ms = p.op("T", lambda t, si=si, c=c, Wsrc=Wsrc, wmul=wmul,
                          coff=coff: t.matmul(
                              slot(si), ln1fm[:, c * 128:(c + 1) * 128],
                              Wsrc[:, c * wmul + coff:c * wmul + coff + 512],
                              start=(c == 0), stop=(c == 7)),
                          reads=("ln1", "WBIG"), writes=("PS",),
                          wait=wl if c == 0 else None, inc=(c == 7))
            proj_ms[gi] = ms
            if gi >= 4:
                if gi < 8:
                    dstn, dst, func = "vtm", vtm[:, (gi - 4) * 512:
                                                 (gi - 3) * 512], AF.Identity
                else:
                    dstn, dst, func = "gsil", gsil[:, (gi - 8) * 512:
                                                   (gi - 7) * 512], AF.Silu
                ev = p.op("S", lambda s, si=si, dst=dst, func=func:
                          s.activation(dst, slot(si), func),
                          reads=("PS",), writes=(dstn,), wait=[ms], inc=True)
                slot_free[si] = ev

        for gi in range(6):
            emit_group(gi)

        def emit_rope(psrc, dst, with_dkv, wait_ms, nm_dst):
            s1_ = psrc.rearrange("p (h two d) -> p h two d", two=2, d=32)
            x1s_, x2s_ = s1_[:, :, 0, :], s1_[:, :, 1, :]
            o1 = dst.rearrange("p (h two d) -> p h two d", two=2, d=32)
            o1a, o2a = o1[:, :, 0, :], o1[:, :, 1, :]
            cv = ctb.rearrange("p (h d) -> p h d", d=32)
            sv = stb.rearrange("p (h d) -> p h d", d=32)
            m0 = ropem[:, 0:512].rearrange("p (h d) -> p h d", d=32)
            m1 = ropem[:, 512:1024].rearrange("p (h d) -> p h d", d=32)
            dkv = DKVEB.rearrange("p (h two d) -> p h two d", two=2, d=32)
            p.op("V", lambda v: v.tensor_mul(m0, x1s_, cv),
                 reads=("PS", "ctab"), writes=("ropem",), wait=wait_ms)
            p.op("V", lambda v: v.tensor_mul(m1, x2s_, sv),
                 reads=("PS", "stab"), writes=("ropem",))
            if with_dkv:
                p.op("V", lambda v: v.tensor_sub(m0, m0, m1),
                     reads=("ropem",), writes=("ropem",))
                p.op("V", lambda v: v.tensor_mul(o1a, m0, dkv[:, :, 0, :]),
                     reads=("ropem", "DKVEB"), writes=(nm_dst,))
            else:
                p.op("V", lambda v: v.tensor_sub(o1a, m0, m1),
                     reads=("ropem",), writes=(nm_dst,))
            p.op("V", lambda v: v.tensor_mul(m0, x1s_, sv),
                 reads=("PS", "stab"), writes=("ropem",))
            p.op("V", lambda v: v.tensor_mul(m1, x2s_, cv),
                 reads=("PS", "ctab"), writes=("ropem",))
            if with_dkv:
                p.op("V", lambda v: v.tensor_add(m0, m0, m1),
                     reads=("ropem",), writes=("ropem",))
                return p.op("V", lambda v: v.tensor_mul(
                    o2a, m0, dkv[:, :, 1, :]),
                    reads=("ropem", "DKVEB"), writes=(nm_dst,), inc=True)
            return p.op("V", lambda v: v.tensor_add(o2a, m0, m1),
                        reads=("ropem",), writes=(nm_dst,), inc=True)

        rope_w_q = [proj_ms[0], proj_ms[1], st_ms]
        if b > 0:
            rope_w_q.append(S_done[b - 1])
        qr_ms = emit_rope(PS[:, 0:1024], qrt, False, rope_w_q, "qrt")
        kr_ms = emit_rope(PS[:, 1024:2048], krt, True,
                          [proj_ms[2], proj_ms[3]], "krt")
        rope_done[b] = kr_ms
        slot_free[0] = slot_free[1] = qr_ms
        slot_free[2] = slot_free[3] = kr_ms
        for gi in range(6, 12):
            emit_group(gi)
        vg_ev = slot_free[5]

        if b > 0:
            sg_v = p.op("V", lambda v: v.tensor_mul(Sst[:], Sst[:], GCTB[:]),
                        reads=("Sst", "GCTB"), writes=("Sst",),
                        wait=[S_done[b - 1]], inc=True)
            sb_ms = p.op("S", lambda s: s.activation(Sbb[:], Sst[:],
                                                     AF.Identity),
                         reads=("Sst",), writes=("Sbb",), wait=[sg_v],
                         inc=True)
        else:
            sg_v = sb_ms = None

        ev_fm = {}
        for wv_i, (src, dst, nm_s, nm_d, h0, rms_) in enumerate([
                (qrt, qfm, "qrt", "qfm", 0, None), (qrt, qfm, "qrt", "qfm", 8, None),
                (krt, kfm, "krt", "kfm", 0, None), (krt, kfm, "krt", "kfm", 8, None)]):
            bk = "A" if wv_i % 2 == 0 else "B"
            bank = ptA if bk == "A" else ptB
            wl = [qr_ms if src is qrt else kr_ms]
            if pt_free[bk] is not None:
                wl.append(pt_free[bk])
            tms = None
            for hh in range(8):
                hx = h0 + hh
                tms = p.op("T", lambda t, bank=bank, hh=hh, hx=hx, src=src:
                           t.transpose(bank[0:64, hh * 128:(hh + 1) * 128],
                                       src[:, hx * 64:(hx + 1) * 64],
                                       IDENT[:]),
                           reads=(nm_s, "IDENT"), writes=("PT",),
                           wait=wl if hh == 0 else None, inc=(hh == 7))
            ev_fm[wv_i] = p.op(
                "S", lambda s, bank=bank, dst=dst, h0=h0: s.activation(
                    dst[0:64, h0 * 128:(h0 + 8) * 128], bank[0:64, 0:1024],
                    AF.Identity),
                reads=("PT",), writes=(nm_d,), wait=[tms], inc=True)
            pt_free[bk] = ev_fm[wv_i]

        mask_ms = {}
        o_ev = {}
        for wv_i in range(4):
            si = wv_i % 2
            reg = 0
            wl = [ev_fm[1], ev_fm[3]]
            if slot_free[si] is not None:
                wl.append(slot_free[si])
            ams = None
            for hh in range(4):
                hx = wv_i * 4 + hh
                ams = p.op("T", lambda t, si=si, hh=hh, hx=hx: t.matmul(
                    slot(si)[:, hh * 128:(hh + 1) * 128],
                    kfm[0:64, hx * 128:(hx + 1) * 128],
                    qfm[0:64, hx * 128:(hx + 1) * 128],
                    start=True, stop=True),
                    reads=("kfm", "qfm"), writes=("PS",),
                    wait=wl if hh == 0 else None, inc=(hh == 3))
            mwl = [ams]
            if ab_free.get(0) is not None:
                mwl.append(ab_free[0])
            msk = None
            for pr in range(2):
                msk = p.op("V", lambda v, si=si, reg=reg, pr=pr: v.tensor_mul(
                    Ab[:, reg * 512 + pr * 256:reg * 512 + (pr + 1) * 256],
                    slot(si)[:, pr * 256:(pr + 1) * 256], MASKB[:]),
                    reads=("PS", "MASKB"), writes=("Ab",),
                    wait=mwl if pr == 0 else None, inc=(pr == 1))
            mask_ms[wv_i] = msk
            slot_free[si] = msk

            osi = 2 + wv_i
            owl = [msk, vg_ev]
            if sb_ms is not None:
                owl.append(sb_ms)
            if slot_free[osi] is not None:
                owl.append(slot_free[osi])
            oms = None
            for hh in range(4):
                hx = wv_i * 4 + hh
                oms = p.op("T", lambda t, osi=osi, hh=hh, hx=hx, reg=reg,
                           sf=(b == 0): t.matmul(
                               slot(osi)[:, hh * 128:(hh + 1) * 128],
                               Ab[:, reg * 512 + hh * 128:
                                   reg * 512 + (hh + 1) * 128],
                               vtm[:, hx * 128:(hx + 1) * 128],
                               start=True, stop=sf),
                           reads=("Ab", "vtm"), writes=("PS",),
                           wait=owl if hh == 0 else None,
                           inc=(b == 0 and hh == 3))
                if b > 0:
                    oms = p.op("T", lambda t, osi=osi, hh=hh, hx=hx: t.matmul(
                        slot(osi)[:, hh * 128:(hh + 1) * 128],
                        qfm[0:64, hx * 128:(hx + 1) * 128],
                        Sbb[0:64, hx * 128:(hx + 1) * 128],
                        start=False, stop=True),
                        reads=("qfm", "Sbb"), writes=("PS",), inc=(hh == 3))
            ab_free[0] = oms
            o_ev[wv_i] = p.op("V", lambda v, osi=osi, wv_i=wv_i: v.tensor_mul(
                otm[:, wv_i * 512:(wv_i + 1) * 512], slot(osi),
                DQ2B[:, wv_i * 512:(wv_i + 1) * 512]),
                reads=("PS", "DQ2B"), writes=("otm",), wait=[oms], inc=True)
            slot_free[osi] = o_ev[wv_i]

        kv_upd = []
        for wv_i in range(4):
            si = wv_i % 2
            wl = [kr_ms, vg_ev, slot_free[si]]
            kms = None
            for hh in range(4):
                hx = wv_i * 4 + hh
                kms = p.op("T", lambda t, si=si, hh=hh, hx=hx: t.matmul(
                    slot(si)[0:64, hh * 128:(hh + 1) * 128],
                    krt[:, hx * 64:(hx + 1) * 64],
                    vtm[:, hx * 128:(hx + 1) * 128],
                    start=True, stop=True),
                    reads=("krt", "vtm"), writes=("PS",),
                    wait=wl if hh == 0 else None, inc=(hh == 3))
            if b == 0:
                msv = p.op("V", lambda v, si=si, wv_i=wv_i: v.tensor_copy(
                    Sst[0:64, wv_i * 512:(wv_i + 1) * 512],
                    slot(si)[0:64, :]),
                    reads=("PS",), writes=("Sst",), wait=[kms], inc=True)
            else:
                msv = p.op("V", lambda v, si=si, wv_i=wv_i: v.tensor_add(
                    Sst[0:64, wv_i * 512:(wv_i + 1) * 512],
                    Sst[0:64, wv_i * 512:(wv_i + 1) * 512],
                    slot(si)[0:64, :]),
                    reads=("PS", "Sst"), writes=("Sst",),
                    wait=[kms, sg_v, sb_ms], inc=True)
            kv_upd.append(msv)
            slot_free[si] = msv
        S_done[b] = kv_upd[-1]

        p.op("V", lambda v: v.tensor_mul(og[:], otm[:], otm[:]),
             reads=("otm",), writes=("og",), wait=[o_ev[3]])
        p.op("V", lambda v, so=so: v.reduce_sum(
            stats[:, so + 8:so + 24].rearrange("p (h one) -> p h one",
                                               one=1),
            og[:].rearrange("p (h d) -> p h d", d=128), axis=AX.X),
            reads=("og",), writes=("stats",))
        ve2 = p.op("V", lambda v, so=so: v.tensor_scalar(
            stats[:, so + 8:so + 24], stats[:, so + 8:so + 24], 1.0 / DV,
            EPS, op0=ALU.mult, op1=ALU.add),
            reads=("stats",), writes=("stats",), inc=True)
        pw2 = p.op("P", lambda g, so=so: g.tensor_tensor(
            stats[:, so + 8:so + 24], stats[:, so + 8:so + 24],
            NEGH[:, 0:16], op=ALU.pow), wait=[ve2], inc=True)
        ogm = None
        for hx in range(16):
            ogm = p.op("V", lambda v, hx=hx, so=so: v.tensor_scalar_mul(
                og[:, hx * 128:(hx + 1) * 128],
                otm[:, hx * 128:(hx + 1) * 128],
                stats[:, so + 8 + hx:so + 9 + hx]),
                reads=("otm", "stats"), writes=("og",),
                wait=[pw2] if hx == 0 else None, inc=(hx == 15))
        og_ms = p.op("V", lambda v: v.tensor_mul(og[:], og[:], gsil[:]),
                     reads=("og", "gsil"), writes=("og",), inc=True)

        ogfm_ev = {}
        for wv_i in range(2):
            bk = "A" if wv_i == 0 else "B"
            bank = ptA if wv_i == 0 else ptB
            wl = [og_ms]
            if pt_free[bk] is not None:
                wl.append(pt_free[bk])
            tms = None
            for hh in range(8):
                c = wv_i * 8 + hh
                tms = p.op("T", lambda t, bank=bank, hh=hh, c=c: t.transpose(
                    bank[:, hh * 128:(hh + 1) * 128],
                    og[:, c * 128:(c + 1) * 128], IDENT[:]),
                    reads=("og", "IDENT"), writes=("PT",),
                    wait=wl if hh == 0 else None, inc=(hh == 7))
            ogfm_ev[wv_i] = p.op(
                "S", lambda s, bank=bank, wv_i=wv_i: s.activation(
                    og[:, wv_i * 1024:(wv_i + 1) * 1024], bank[:, 0:1024],
                    AF.Identity),
                reads=("PT",), writes=("og",), wait=[tms], inc=True)
            pt_free[bk] = ogfm_ev[wv_i]

        wo_t = {}
        for g_i in range(2):
            si = g_i
            wl = [ogfm_ev[1], slot_free[si]]
            wms2 = None
            for c in range(16):
                wms2 = p.op("T", lambda t, si=si, c=c, g_i=g_i: t.matmul(
                    slot(si), og[:, c * 128:(c + 1) * 128],
                    WO[:, c * 1024 + g_i * 512:c * 1024 + (g_i + 1) * 512],
                    start=(c == 0), stop=(c == 15)),
                    reads=("og", "WBIG"), writes=("PS",),
                    wait=wl if c == 0 else None, inc=(c == 15))
            wo_t[g_i] = wms2

        p.op("V", lambda v: v.bn_stats(bn6[:, 0:6], PS[:, 0:512]),
             reads=("PS",), writes=("bn6",), wait=[wo_t[0], wo_t[1]])
        p.op("V", lambda v: v.bn_stats(bn6[:, 6:12], PS[:, 512:1024]),
             reads=("PS",), writes=("bn6",))
        p.op("V", lambda v: v.bn_aggr(bn2[:], bn6[:]),
             reads=("bn6",), writes=("bn2",))
        ve3 = p.op("V", lambda v, so=so: v.tensor_scalar(
            stats[:, so + 4:so + 5], bn2[:, 1:2], EPS, None, op0=ALU.add),
            reads=("bn2",), writes=("stats",), inc=True)
        pw3 = p.op("P", lambda g, so=so: g.tensor_tensor(
            stats[:, so + 5:so + 6], stats[:, so + 4:so + 5], NEGH[:, 0:1],
            op=ALU.pow), wait=[ve3], inc=True)
        bc3 = p.op("V", lambda v, so=so: v.scalar_tensor_tensor(
            stats[:, so + 6:so + 7], bn2[:, 0:1], -1.0,
            stats[:, so + 5:so + 6], op0=ALU.mult, op1=ALU.mult),
            reads=("bn2", "stats"), writes=("stats",), wait=[pw3], inc=True)
        ln2_ms = p.op("S", lambda s, so=so: s.activation(
            ln2b[:], PS[:, 0:1024], AF.Identity,
            bias=stats[:, so + 6:so + 7], scale=stats[:, so + 5:so + 6]),
            reads=("PS", "stats"), writes=("ln2b",), wait=[bc3], inc=True)
        slot_free[0] = slot_free[1] = ln2_ms
        xw = [ln2_ms]
        if b >= 1:
            xw.append(x1_dma[b - 1])
        x1_ms = p.op("V", lambda v, x1b=x1b, xb=xb: v.tensor_add(
            x1b, xb, ln2b[:]),
            reads=("xt", "ln2b"), writes=("x1t",), wait=xw, inc=True)
        blk_end_V[b] = x1_ms
        x1_dma[b] = p.dma("SP", dx1[b * 128:(b + 1) * 128, :], x1b,
                          wait=[x1_ms], sem="x1d")

    p.op("SP", lambda s, v=p.e["SP"].dma_val: s.wait_ge(sems["SPd"], v))
    p.op("P", lambda g, v=p.e["P"].dma_val: g.wait_ge(sems["Pd"], v))
    for k in ("xd0", "xd1", "td", "x1d"):
        p.op("SP", lambda sp, k=k, v=p.dsem.get(k, 0): sp.wait_ge(
            sems[k], v))

    with nc.Block() as blk1:
        p.run(blk1)
    ctx1.close()

    ctxp = ExitStack()
    s2_ = lambda nm, shape, dt=F32: ctxp.enter_context(
        nc.sbuf_tensor(nm, list(shape), dt)).ap()
    WSTG = s2_("WSTG", (128, 2 * 4096))
    WTERN = s2_("WTERN", (128, 4096), BF16)
    WTT = s2_("WTT", (128, 4096), BF16)
    WSUM = s2_("WSUM", (128, 64))
    UUW = s2_("UUW", (128, 1024))
    G1W = s2_("G1W", (128, 1024))
    G2W = s2_("G2W", (128, 1024))

    p = Plan(nc, sems, carry=p)
    slot_free = [None] * 6
    pt_free = {"A": None, "B": None}

    sum_last2 = [None, None]
    col = 0
    for dsrc, nchunk, width in [(din["w1"], 32, 1024), (din["w2"], 8, 4096)]:
        for r in range(nchunk):
            pbuf = WSTG[:, (r % 2) * 4096:(r % 2) * 4096 + width]
            wl = [sum_last2[r % 2]] if sum_last2[r % 2] is not None else None
            dm = p.dma("P", pbuf, dsrc[r * 128:(r + 1) * 128, :], wait=wl,
                       sem="wd%d" % (r % 2))
            ms = p.op("V", lambda v, pbuf=pbuf, col=col: v.reduce_sum(
                WSUM[:, col:col + 1], pbuf, axis=AX.X,
                apply_absolute_value=True),
                reads=("WSTG",), writes=("WSUM",), wait=[dm], inc=True)
            sum_last2[r % 2] = ms
            col += 1
    p.op("V", lambda v: v.reduce_sum(WSUM[:, 40:41], WSUM[:, 0:32],
                                     axis=AX.X),
         reads=("WSUM",), writes=("WSUM",),
         wait=[sum_last2[0], sum_last2[1]])
    t12 = p.op("V", lambda v: v.reduce_sum(WSUM[:, 41:42], WSUM[:, 32:40],
                                           axis=AX.X),
               reads=("WSUM",), writes=("WSUM",), inc=True)
    wsb = p.op("V", lambda v: v.tensor_copy(WTERN[:, 0:2], WSUM[:, 40:42]),
               reads=("WSUM",), writes=("WTERN",), wait=[t12], inc=True)
    tpm = p.op("T", lambda t: t.transpose(ptA[0:2, 0:128],
                                          WTERN[:, 0:2], IDENT[:]),
               reads=("WTERN", "IDENT"), writes=("PT",), wait=[wsb],
               inc=True)
    p.op("V", lambda v: v.reduce_sum(stats[0:2, 0:1], ptA[0:2, 0:128],
                                     axis=AX.X),
         reads=("PT",), writes=("stats",), wait=[tpm])
    p.op("V", lambda v: v.tensor_scalar(stats[0:2, 1:2], stats[0:2, 0:1],
                                        1.0 / (DH * D), 1e-5,
                                        op0=ALU.mult, op1=ALU.max),
         reads=("stats",), writes=("stats",))
    rc = p.op("V", lambda v: v.reciprocal(stats[0:2, 2:3], stats[0:2, 1:2]),
              reads=("stats",), writes=("stats",), inc=True)
    rcb = p.op("V", lambda v: v.tensor_copy(WTERN[0:2, 4:6],
                                            stats[0:2, 1:3]),
               reads=("stats",), writes=("WTERN",), wait=[rc], inc=True)
    tq1 = p.op("T", lambda t: t.transpose(ptA[0:1, 0:2], WTERN[0:2, 4:5],
                                          IDENT[0:2, 0:2]),
               reads=("WTERN", "IDENT"), writes=("PT",), wait=[rcb, tpm])
    tq2 = p.op("T", lambda t: t.transpose(ptA[0:1, 2:4], WTERN[0:2, 5:6],
                                          IDENT[0:2, 0:2]),
               reads=("WTERN", "IDENT"), writes=("PT",), inc=True)
    srow = p.op("V", lambda v: v.tensor_copy(WSUM[0:1, 48:52],
                                             ptA[0:1, 0:4]),
                reads=("PT",), writes=("WSUM",), wait=[tq2], inc=True)

    def stat_bcast(src_11, dst_col, wait_ms):
        mm = p.op("T", lambda t, src_11=src_11: t.matmul(
            slot(5)[:, 0:1], ONE1F[:], src_11, start=True, stop=True),
            reads=("ONE1F", "WSUM"), writes=("PS",), wait=wait_ms, inc=True)
        return p.op("V", lambda v, dst_col=dst_col: v.tensor_copy(
            dst_col, slot(5)[:, 0:1]),
            reads=("PS",), writes=("WSC",), wait=[mm], inc=True)

    b0 = stat_bcast(WSUM[0:1, 50:51], WSC[:, 0:1], [srow])
    b1 = stat_bcast(WSUM[0:1, 48:49], WSC[:, 1:2], [b0])
    b2 = stat_bcast(WSUM[0:1, 51:52], WSC[:, 2:3], [b1])
    b3 = stat_bcast(WSUM[0:1, 49:50], WSC[:, 3:4], [b2])

    cw_dm = p.dma("P", CW[:].rearrange("p (c k) -> p c k", k=3),
                  din["cw"].rearrange("(c p) k -> p c k", p=128), sem="cwd")
    cb_dm = p.dma("P", CB[:].rearrange("p (c one) -> p c one", one=1),
                  din["cb"].rearrange("(c p) one -> p c one", p=128),
                  slow=True, sem="cwd")
    p.op("V", lambda v: v.tensor_scalar_mul(CW[:], CW[:], WSC[:, 1:2]),
         reads=("CW", "WSC"), writes=("CW",), wait=[cb_dm, b1])

    pb_free = {}
    tern_T_last = tq2
    last_store = None
    for wi, (dsrc, ddst, nchunk, width, swc) in enumerate([
            (din["w1"], dw1t, 32, 1024, WSC[:, 0:1]),
            (din["w2"], dw2t, 8, 4096, WSC[:, 2:3])]):
        for r in range(nchunk):
            pbuf = WSTG[:, (r % 2) * 4096:(r % 2) * 4096 + width]
            wl = [b3]
            prev = pb_free.get((wi, r - 2))
            if prev is None and wi == 1 and r < 2:
                prev = pb_free.get((0, 30 + r))
            if prev is not None:
                wl.append(prev)
            dm = p.dma("P", pbuf, dsrc[r * 128:(r + 1) * 128, :], wait=wl,
                       sem="wd%d" % (r % 2))
            qt_ms = None
            for st0 in range(0, width, 1024):
                wwl = [dm]
                if tern_T_last is not None and st0 == 0:
                    wwl.append(tern_T_last)
                p.op("V", lambda v, pbuf=pbuf, st0=st0, swc=swc:
                     v.tensor_scalar_mul(UUW[:], pbuf[:, st0:st0 + 1024],
                                         swc),
                     reads=("WSTG", "WSC"), writes=("UUW",), wait=wwl)
                p.op("V", lambda v: v.tensor_single_scalar(
                    G1W[:], UUW[:], 0.5, op=ALU.is_ge),
                    reads=("UUW",), writes=("G1W",))
                p.op("V", lambda v: v.tensor_single_scalar(
                    G2W[:], UUW[:], -0.5, op=ALU.is_le),
                    reads=("UUW",), writes=("G2W",))
                qt_ms = p.op("V", lambda v, st0=st0: v.tensor_sub(
                    WTERN[:, st0:st0 + 1024], G1W[:], G2W[:]),
                    reads=("G1W", "G2W"), writes=("WTERN",),
                    inc=(st0 + 1024 >= width))
            pb_free[(wi, r)] = qt_ms
            ntc = width // 128
            ev_list = []
            for w0 in range(0, ntc, 8):
                bk = "A" if (w0 // 8) % 2 == 0 else "B"
                bank = ptA if bk == "A" else ptB
                nw = min(8, ntc - w0)
                wl2 = [qt_ms]
                if pt_free[bk] is not None:
                    wl2.append(pt_free[bk])
                tms = None
                for k in range(nw):
                    c = w0 + k
                    tms = p.op("T", lambda t, bank=bank, k=k, c=c:
                               t.transpose(bank[:, k * 128:(k + 1) * 128],
                                           WTERN[:, c * 128:(c + 1) * 128],
                                           IDENT[:]),
                               reads=("WTERN", "IDENT"), writes=("PT",),
                               wait=wl2 if k == 0 else None,
                               inc=(k == nw - 1))
                ewl = [tms]
                if last_store is not None and w0 == 0:
                    ewl.append(last_store)
                ev = p.op("S", lambda s, bank=bank, w0=w0, nw=nw:
                          s.activation(WTT[:, w0 * 128:(w0 + nw) * 128],
                                       bank[:, 0:nw * 128], AF.Identity),
                          reads=("PT",), writes=("WTT",), wait=ewl, inc=True)
                ev_list.append(ev)
                pt_free[bk] = ev
            tern_T_last = tms
            for ci in range(ntc):
                last_store = p.dma(
                    "SP",
                    ddst[ci * 128:(ci + 1) * 128, r * 128:(r + 1) * 128],
                    WTT[:, ci * 128:(ci + 1) * 128],
                    wait=[ev_list[ci // 8]] if ci % 8 == 0 else None)

    p.dma("SP", W1T.rearrange("p (c n) -> p c n", n=4096),
          dw1t.rearrange("(c p) n -> p c n", p=128), wait=[last_store])
    p.dma("SP", W2T.rearrange("p (c n) -> p c n", n=1024),
          dw2t.rearrange("(c p) n -> p c n", p=128))
    p.op("SP", lambda s, v=p.e["SP"].dma_val: s.wait_ge(sems["SPd"], v))
    p.op("P", lambda g, v=p.e["P"].dma_val: g.wait_ge(sems["Pd"], v))

    with nc.Block() as blk2:
        p.run(blk2)
    ctxp.close()

    ctx3 = ExitStack()
    s3_ = lambda nm, shape, dt=F32: ctx3.enter_context(
        nc.sbuf_tensor(nm, list(shape), dt)).ap()
    X1T = s3_("X1T", (128, 2 * 2048))
    LN3F = s3_("LN3F", (128, 1024))
    XQTM = s3_("XQTM", (128, 1024), BF16)
    XQFM = s3_("XQFM", (128, 2048), BF16)
    HT0 = s3_("HT0", (128, 32 * 258), BF16)
    HT1 = s3_("HT1", (128, 32 * 258), BF16)
    A0B = s3_("A0B", (128, 2 * 256))
    ABQ = s3_("ABQ", (128, 2 * 512), BF16)
    ABP = ABQ
    HQ = ABQ
    RMA = s3_("RMA", (128, 512), BF16)
    RMB = s3_("RMB", (128, 512), BF16)
    RMF = RMB[:, 0:256]
    SB2S = s3_("SB2S", (128, 2), BF16)
    UU = s3_("UU", (128, 2 * 512))
    IA = s3_("IA", (128, 2 * 512), I32)
    RR = IA
    SBCS = RMA
    SB2 = s3_("SB2", (2, 128), BF16)
    SROWB = s3_("SROWB", (1, 256), BF16)
    FFO = s3_("FFO", (128, 1024))
    LN4B = FFO

    p = Plan(nc, sems, carry=p)
    slot_free = [None] * 6
    pt_free = {"A": None, "B": None}

    HTS = [HT0, HT1]
    HTnm = ["HT0", "HT1"]

    def hmid(ht, c, a, bcol):
        return ht[:, c * 258 + a:c * 258 + bcol]

    mm1_ev = {}
    win_out_dma = {}
    ffo_done = {}
    halo_ms = {}
    xq_evs = {}
    ln3f_free = [None]
    m2done = {}
    evac_ms = {}

    mz_ms = None
    for c in range(32):
        mz_ms = p.op("V", lambda v, c=c: v.memset(hmid(HT0, c, 0, 1), 0.0),
                     writes=("HT0",), inc=(c == 31))
    halo_ms[0] = mz_ms

    def emit_mm1(n):
        pb = n % 2
        X1 = X1T[:, pb * 2048:(pb + 1) * 2048]
        HT = HTS[pb]
        wl = []
        if n >= 2:
            wl.append(win_out_dma.get(n - 2))
        x_ms = p.dma("SP", X1.rearrange("p (jj nn) -> p jj nn", nn=1024),
                     dx1[n * 256:(n + 1) * 256, :].rearrange(
                         "(jj pp) nn -> pp jj nn", pp=128),
                     wait=[m for m in wl if m is not None] or None,
                     sem="xd%d" % pb)
        tp_evs = []
        for jj in range(2):
            so = 48 * pb + 8 * jj
            x1lb = ("X1T", pb * 2048 + jj * 1024, pb * 2048 + (jj + 1) * 1024)
            xb = X1[:, jj * 1024:(jj + 1) * 1024]
            p.op("V", lambda v, xb=xb: v.bn_stats(bn6[:, 0:6],
                                                  xb[:, 0:512]),
                 reads=(x1lb,), writes=(("bn6", 0, 6),), wait=[x_ms])
            p.op("V", lambda v, xb=xb: v.bn_stats(bn6[:, 6:12],
                                                  xb[:, 512:1024]),
                 reads=(x1lb,), writes=(("bn6", 6, 12),))
            p.op("V", lambda v: v.bn_aggr(bn2[:], bn6[:]),
                 reads=("bn6",), writes=("bn2",))
            ve = p.op("V", lambda v, so=so: v.tensor_scalar(
                stats[:, so:so + 1], bn2[:, 1:2], EPS, None, op0=ALU.add),
                reads=("bn2",), writes=(("stats", so, so + 1),), inc=True)
            pw = p.op("P", lambda g, so=so: g.tensor_tensor(
                stats[:, so + 1:so + 2], stats[:, so:so + 1], NEGH[:, 0:1],
                op=ALU.pow),
                reads=(("stats", so, so + 1), ("NEGH", 0, 1)),
                writes=(("stats", so + 1, so + 2),), wait=[ve], inc=True)
            bc = p.op("V", lambda v, so=so: v.scalar_tensor_tensor(
                stats[:, so + 2:so + 3], bn2[:, 0:1], -1.0,
                stats[:, so + 1:so + 2], op0=ALU.mult, op1=ALU.mult),
                reads=("bn2", ("stats", so + 1, so + 2)),
                writes=(("stats", so + 2, so + 3),), wait=[pw], inc=True)
            lnm = p.op("S", lambda s, xb=xb, so=so: s.activation(
                LN3F[:], xb, AF.Identity, bias=stats[:, so + 2:so + 3],
                scale=stats[:, so + 1:so + 2]),
                reads=(x1lb, ("stats", so, so + 3)), writes=("LN3F",),
                wait=[bc, ln3f_free[0]] if ln3f_free[0] else [bc],
                inc=True)
            mx = p.op("V", lambda v, so=so: v.reduce_max(
                stats[:, so + 3:so + 4], LN3F[:], axis=AX.X,
                apply_absolute_value=True),
                reads=("LN3F",), writes=(("stats", so + 3, so + 4),),
                wait=[lnm], inc=True)
            mxc = p.op("V", lambda v, so=so: v.tensor_scalar_max(
                stats[:, so + 4:so + 5], stats[:, so + 3:so + 4], 1e-5),
                reads=(("stats", so + 3, so + 4),),
                writes=(("stats", so + 4, so + 5),), inc=True)
            p.op("P", lambda g, so=so: g.tensor_tensor(
                stats[:, so + 5:so + 6], stats[:, so + 4:so + 5],
                NEGH[:, 16:17], op=ALU.pow),
                reads=(("stats", so + 4, so + 5), ("NEGH", 16, 17)),
                writes=(("stats", so + 5, so + 6),), wait=[mxc])
            p.op("P", lambda g, so=so: g.tensor_tensor(
                stats[:, so + 6:so + 7], stats[:, so + 5:so + 6],
                NEGH[:, 17:18], op=ALU.mult),
                reads=(("stats", so + 5, so + 6), ("NEGH", 17, 18)),
                writes=(("stats", so + 6, so + 7),))
            sc_ms = p.op("P", lambda g, so=so: g.tensor_tensor(
                stats[:, so + 7:so + 8], stats[:, so + 4:so + 5],
                NEGH[:, 18:19], op=ALU.mult),
                reads=(("stats", so + 4, so + 5), ("NEGH", 18, 19)),
                writes=(("stats", so + 7, so + 8),), inc=True)
            xq_ms = None
            sl = lambda st: (("UU", st, st + 512), ("IA", st, st + 512))
            for step in range(5):
                for sp_ in range(2):
                    st = sp_ * 512
                    lf = LN3F[:, st:st + 512]
                    uu = UU[:, st:st + 512]
                    ia = IA[:, st:st + 512]
                    if step == 0:
                        uwl = None
                        if sp_ == 0:
                            uwl = [sc_ms]
                            prev_key = (n, 0) if jj == 1 else (n - 1, 1)
                            if prev_key in xq_evs:
                                uwl.append(xq_evs[prev_key])
                        p.op("V", lambda v, lf=lf, uu=uu, so=so:
                             v.tensor_scalar(
                                 uu, lf, stats[:, so + 6:so + 7], 0.5,
                                 op0=ALU.mult, op1=ALU.add),
                             reads=(("LN3F", st, st + 512),
                                    ("stats", so + 6, so + 7)),
                             writes=(("UU", st, st + 512),), wait=uwl)
                    elif step == 1:
                        p.op("V", lambda v, uu=uu, ia=ia:
                             v.tensor_copy(ia, uu),
                             reads=(("UU", st, st + 512),),
                             writes=(("IA", st, st + 512),))
                    elif step == 2:
                        p.op("V", lambda v, uu=uu, ia=ia: v.tensor_tensor(
                            uu, uu, ia, op=ALU.subtract),
                            reads=sl(st), writes=(("UU", st, st + 512),))
                    elif step == 3:
                        p.op("V", lambda v, uu=uu, ia=ia:
                             v.scalar_tensor_tensor(
                                 ia, uu, 0.0, ia, op0=ALU.is_ge,
                                 op1=ALU.add),
                             reads=sl(st), writes=(("IA", st, st + 512),))
                    else:
                        xq_ms = p.op("V", lambda v, ia=ia, st=st, so=so:
                                     v.tensor_scalar(
                                         XQTM[:, st:st + 512], ia, -1.0,
                                         stats[:, so + 7:so + 8],
                                         op0=ALU.add, op1=ALU.mult),
                                     reads=(("IA", st, st + 512),
                                            ("stats", so + 7, so + 8)),
                                     writes=(("XQTM", st, st + 512),),
                                     inc=(sp_ == 1))
            ln3f_free[0] = xq_ms
            bk = "A" if jj == 0 else "B"
            bank = ptA if jj == 0 else ptB
            wl2 = [xq_ms]
            if pt_free[bk] is not None:
                wl2.append(pt_free[bk])
            tms = None
            for c in range(8):
                tms = p.op("T", lambda t, bank=bank, c=c: t.transpose(
                    bank[:, c * 128:(c + 1) * 128],
                    XQTM[:, c * 128:(c + 1) * 128], IDENT[:]),
                    reads=("XQTM", "IDENT"), writes=("PT",),
                    wait=wl2 if c == 0 else None, inc=(c == 7))
            ev = p.op("S", lambda s, bank=bank, jj=jj: s.activation(
                XQFM[:].rearrange("p (c j2 nn) -> p c j2 nn", j2=2, nn=128)
                [:, :, jj, :],
                bank[:, 0:1024].rearrange("p (c nn) -> p c nn", nn=128),
                AF.Identity),
                reads=("PT",), writes=("XQFM",), wait=[tms], inc=True)
            xq_evs[(n, jj)] = ev
            pt_free[bk] = ev
            tp_evs.append(ev)

        lastev = None
        for hfc in range(32):
            si = hfc % 2
            wl = list(tp_evs) if hfc < 2 else []
            if slot_free[si] is not None:
                wl.append(slot_free[si])
            mms = None
            for c in range(8):
                mms = p.op("T", lambda t, si=si, c=c, hfc=hfc: t.matmul(
                    slot(si)[:, 0:256],
                    W1T[:, c * 4096 + hfc * 128:c * 4096 + (hfc + 1) * 128],
                    XQFM[:, c * 256:(c + 1) * 256],
                    start=(c == 0), stop=(c == 7)),
                    reads=("WBIG", "XQFM"), writes=("PS",),
                    wait=(wl or None) if c == 0 else None, inc=(c == 7))
            lastev = p.op("S", lambda s, si=si, hfc=hfc, HT=HT: s.activation(
                hmid(HT, hfc, 1, 257), slot(si)[:, 0:256], AF.Identity),
                reads=("PS",),
                writes=((HTnm[pb], hfc * 258 + 1, hfc * 258 + 257),),
                wait=[mms], inc=True)
            slot_free[si] = lastev
            evac_ms[(n, hfc)] = lastev
        mm1_ev[n] = lastev

    def emit_tail(n):
        pb = n % 2
        HT = HTS[pb]
        htn = HTnm[pb]
        htv = HT.rearrange("p (c e) -> p c e", e=258)
        X1 = X1T[:, pb * 2048:(pb + 1) * 2048]
        so = 48 * pb + 16
        lb = 48 * pb + 28

        sil_ms = {}
        add2_ms = {}
        nxtv = None
        if n + 1 < NW:
            nxtv = HTS[(n + 1) % 2].rearrange("p (c e) -> p c e", e=258)
        hl_last = None
        for c in range(32):
            if c % 8 == 0:
                g = c // 8
                if nxtv is not None:
                    p.op("V", lambda v, g=g, nxtv=nxtv: v.tensor_copy(
                        htv[:, g * 8:(g + 1) * 8, 257:258],
                        nxtv[:, g * 8:(g + 1) * 8, 1:2]),
                        reads=(HTnm[(n + 1) % 2],), writes=(htn,),
                        wait=[evac_ms[(n + 1, g * 8 + 7)]])
                    hl_last = p.op("V", lambda v, g=g, nxtv=nxtv:
                                   v.tensor_copy(
                                       nxtv[:, g * 8:(g + 1) * 8, 0:1],
                                       htv[:, g * 8:(g + 1) * 8, 256:257]),
                                   reads=(htn,),
                                   writes=(HTnm[(n + 1) % 2],),
                                   wait=[evac_ms[(n, g * 8 + 7)]],
                                   inc=(g == 3))
                elif g == 0:
                    p.op("V", lambda v: v.memset(htv[:, :, 257:258], 0.0),
                         writes=(htn,))
            s4 = c % 2
            a0 = A0B[:, s4 * 256:(s4 + 1) * 256]
            l0 = ("A0B", s4 * 256, (s4 + 1) * 256)
            swl = []
            if c == 0:
                swl.append(halo_ms[n])
            if c >= 2:
                swl.append(add2_ms[c - 2])
            t0 = p.op("S", lambda s, c=c, a0=a0, HT=HT: s.activation(
                a0, hmid(HT, c, 0, 256), AF.Identity,
                scale=CW[:, c * 3:c * 3 + 1], bias=CB[:, c:c + 1]),
                reads=((htn, c * 258, c * 258 + 256), "CW", "CB"),
                writes=(l0,), wait=swl or None, inc=True)
            p.op("V", lambda v, c=c, a0=a0, HT=HT: v.scalar_tensor_tensor(
                a0, hmid(HT, c, 1, 257), CW[:, c * 3 + 1:c * 3 + 2], a0,
                op0=ALU.mult, op1=ALU.add),
                reads=((htn, c * 258 + 1, c * 258 + 257), "CW", l0),
                writes=(l0,), wait=[t0])
            a_ms = p.op("V", lambda v, c=c, a0=a0, HT=HT:
                        v.scalar_tensor_tensor(
                            a0, hmid(HT, c, 2, 258),
                            CW[:, c * 3 + 2:c * 3 + 3], a0,
                            op0=ALU.mult, op1=ALU.add),
                        reads=((htn, c * 258 + 2, c * 258 + 258), "CW", l0),
                        writes=(l0,), inc=True)
            add2_ms[c] = a_ms
            sil_ms[c] = p.op("S", lambda s, c=c, a0=a0, HT=HT: s.activation(
                hmid(HT, c, 1, 257), a0, AF.Silu),
                reads=(l0,),
                writes=((htn, c * 258 + 1, c * 258 + 257),),
                wait=[a_ms], inc=True)

        if nxtv is not None:
            halo_ms[n + 1] = hl_last

        tree_ms = {}
        for pr in range(16):
            s2 = pr % 2
            abp = ABP[:, s2 * 512:(s2 + 1) * 512]
            lab = ("ABQ", s2 * 512, (s2 + 1) * 512)
            awl = []
            if pr >= 2:
                awl.append(tree_ms[pr - 2])
            elif m2done.get(n - 1) is not None:
                awl.append(m2done[n - 1])
            ab = p.op("S", lambda s, pr=pr, abp=abp: s.activation(
                abp, htv[:, 2 * pr:2 * pr + 2, 1:257], AF.Abs),
                reads=((htn, 2 * pr * 258, (2 * pr + 2) * 258),),
                writes=(lab,), wait=awl or None, inc=True)
            lane = pr % 2
            dst = RMA if lane == 0 else RMB
            dn = "RMA" if lane == 0 else "RMB"
            if pr < 2:
                tree_ms[pr] = p.op("V", lambda v, dst=dst, abp=abp:
                                   v.tensor_copy(dst[:], abp),
                                   reads=(lab,), writes=(dn,),
                                   wait=[ab], inc=True)
            else:
                tree_ms[pr] = p.op("V", lambda v, dst=dst, abp=abp:
                                   v.tensor_tensor(dst[:], dst[:], abp,
                                                   op=ALU.max),
                                   reads=(lab, dn), writes=(dn,),
                                   wait=[ab], inc=True)
        p.op("V", lambda v: v.tensor_tensor(RMA[:], RMA[:], RMB[:],
                                            op=ALU.max),
             reads=("RMA", "RMB"), writes=("RMA",),
             wait=[tree_ms[14], tree_ms[15]])
        rb = p.op("V", lambda v: v.tensor_tensor(
            RMF[:], RMA[:, 0:256], RMA[:, 256:512], op=ALU.max),
            reads=("RMA",), writes=(("RMB", 0, 256),), inc=True)

        wl2 = [rb]
        if pt_free["A"] is not None:
            wl2.append(pt_free["A"])
        tms = None
        for jj in range(2):
            tms = p.op("T", lambda t, jj=jj: t.transpose(
                ptA[:, jj * 128:(jj + 1) * 128],
                RMF[:, jj * 128:(jj + 1) * 128], IDENT[:]),
                reads=(("RMB", 0, 256), "IDENT"), writes=("PT",),
                wait=wl2 if jj == 0 else None, inc=(jj == 1))
        mxc = p.op("V", lambda v, so=so: v.reduce_max(
            stats[:, so:so + 2].rearrange("p (j2 one) -> p j2 one", one=1),
            ptA[:, 0:256].rearrange("p (j2 nn) -> p j2 nn", nn=128),
            axis=AX.X),
            reads=("PT",), writes=(("stats", so, so + 2),), wait=[tms],
            inc=True)
        pt_free["A"] = mxc
        clip = p.op("V", lambda v, so=so: v.tensor_scalar_max(
            stats[:, so + 2:so + 4], stats[:, so:so + 2], 1e-5),
            reads=(("stats", so, so + 2),),
            writes=(("stats", so + 2, so + 4),), inc=True)
        p.op("P", lambda g, so=so: g.tensor_tensor(
            stats[:, so + 4:so + 6], stats[:, so + 2:so + 4],
            NEGH[:, 19:21], op=ALU.pow),
            reads=(("stats", so + 2, so + 4), ("NEGH", 19, 21)),
            writes=(("stats", so + 4, so + 6),), wait=[clip])
        s127p = p.op("P", lambda g, so=so: g.tensor_tensor(
            stats[:, so + 6:so + 8], stats[:, so + 4:so + 6],
            NEGH[:, 21:23], op=ALU.mult),
            reads=(("stats", so + 4, so + 6), ("NEGH", 21, 23)),
            writes=(("stats", so + 6, so + 8),), inc=True)
        p.op("P", lambda g, so=so: g.tensor_tensor(
            stats[:, so + 8:so + 10], stats[:, so + 2:so + 4],
            NEGH[:, 23:25], op=ALU.mult),
            reads=(("stats", so + 2, so + 4), ("NEGH", 23, 25)),
            writes=(("stats", so + 8, so + 10),))
        c2a = p.op("P", lambda g, so=so: g.tensor_tensor(
            stats[:, so + 10:so + 11], stats[:, so + 8:so + 9],
            WSC[:, 3:4], op=ALU.mult),
            reads=(("stats", so + 8, so + 9), "WSC"),
            writes=(("stats", so + 10, so + 11),), inc=True)
        c2_ms = p.op("P", lambda g, so=so: g.tensor_tensor(
            stats[:, so + 11:so + 12], stats[:, so + 9:so + 10],
            WSC[:, 3:4], op=ALU.mult),
            reads=(("stats", so + 9, so + 10), "WSC"),
            writes=(("stats", so + 11, so + 12),), inc=True)
        sb2_ms = p.op("S", lambda s, so=so: s.activation(
            SB2S[:], stats[:, so + 6:so + 8], AF.Identity),
            reads=(("stats", so + 6, so + 8),), writes=("SB2S",),
            wait=[s127p], inc=True)
        wl3 = [sb2_ms]
        if pt_free["B"] is not None:
            wl3.append(pt_free["B"])
        st2 = p.op("T", lambda t: t.transpose(ptB[0:2, 0:128], SB2S[:],
                                              IDENT[:]),
                   reads=("SB2S", "IDENT"), writes=("PT",), wait=wl3,
                   inc=True)
        sb2ev = p.op("S", lambda s: s.activation(SB2[:], ptB[0:2, 0:128],
                                                 AF.Identity),
                     reads=("PT",), writes=("SB2",), wait=[st2], inc=True)
        pt_free["B"] = sb2ev
        sr_dma = p.dma("SP", SROWB[0:1, :], SB2[:], wait=[sb2ev],
                       sem="srd")
        bwl = [sr_dma]
        if ffo_done.get(n - 1) is not None:
            bwl.append(ffo_done[n - 1])
        if slot_free[2] is not None:
            bwl.append(slot_free[2])
        p.op("T", lambda t: t.matmul(
            slot(2)[:, 0:256], ONE1[:], SROWB[0:1, :], start=True,
            stop=True),
            reads=("ONE1", "SROWB"), writes=("PS",), wait=bwl)
        bc_mm = p.op("T", lambda t: t.matmul(
            slot(2)[:, 256:512], ONE1[:], SROWB[0:1, :], start=True,
            stop=True),
            reads=("ONE1", "SROWB"), writes=("PS",), inc=True)
        sbc_ms = p.op("S", lambda s: s.activation(SBCS[:], slot(2)[:, :],
                                                  AF.Identity),
                      reads=("PS",), writes=("RMA",), wait=[bc_mm],
                      inc=True)

        m2slots = [3, 4, 5, 2]
        mm_last = {}
        hq_ms = {}
        for pg in range(8):
            for step in range(5):
                for k in range(2):
                    pr = 2 * pg + k
                    st = k * 512
                    uu = UU[:, st:st + 512]
                    ia = IA[:, st:st + 512]
                    luu = ("UU", st, st + 512)
                    lia = ("IA", st, st + 512)
                    lhq = ("ABQ", st, st + 512)
                    if step == 0:
                        wl = [sil_ms[2 * pr + 1]]
                        if pr == 0:
                            wl.append(sbc_ms)
                        if pr >= 2:
                            wl.append(mm_last[pr - 2])
                        p.op("V", lambda v, pr=pr, uu=uu: v.tensor_tensor(
                            uu, htv[:, 2 * pr:2 * pr + 2, 1:257], SBCS[:],
                            op=ALU.mult),
                            reads=((htn, 2 * pr * 258, (2 * pr + 2) * 258),
                                   "RMA"),
                            writes=(luu,), wait=wl)
                    elif step == 1:
                        p.op("V", lambda v, uu=uu, ia=ia: v.tensor_scalar(
                            ia, uu, 0.5, None, op0=ALU.add),
                            reads=(luu,), writes=(lia,))
                    elif step == 2:
                        p.op("V", lambda v, uu=uu, ia=ia:
                             v.scalar_tensor_tensor(
                                 uu, uu, 0.5, ia, op0=ALU.add,
                                 op1=ALU.subtract),
                             reads=(luu, lia), writes=(luu,))
                    elif step == 3:
                        p.op("V", lambda v, uu=uu, ia=ia:
                             v.scalar_tensor_tensor(
                                 ia, uu, 0.0, ia, op0=ALU.is_ge,
                                 op1=ALU.add),
                             reads=(luu, lia), writes=(lia,))
                    else:
                        hq_ms[pr] = p.op("V", lambda v, ia=ia, st=st:
                                         v.tensor_scalar(
                                             HQ[:, st:st + 512], ia, -1.0,
                                             None, op0=ALU.add),
                                         reads=(lia,), writes=(lhq,),
                                         inc=True)
            for k in range(2):
                pr = 2 * pg + k
                st = k * 512
                lhq = ("ABQ", st, st + 512)
                last = None
                for ci in range(2):
                    c = 2 * pr + ci
                    for jj2 in range(2):
                        for g_i in range(2):
                            si = m2slots[jj2 * 2 + g_i]
                            wlm = None
                            if ci == 0 and jj2 == 0 and g_i == 0:
                                wlm = [hq_ms[pr]]
                                if c == 0:
                                    if ffo_done.get(n - 1) is not None:
                                        wlm.append(ffo_done[n - 1])
                                    for s_ in (3, 4, 5):
                                        if slot_free[s_] is not None:
                                            wlm.append(slot_free[s_])
                            if c == 0 and si == 2 and wlm is None:
                                wlm = [sbc_ms]
                            last = p.op("T", lambda t, si=si, c=c, jj2=jj2,
                                        g_i=g_i, st=st: t.matmul(
                                            slot(si),
                                            HQ[:, st + (c % 2) * 256 +
                                               jj2 * 128:
                                               st + (c % 2) * 256 +
                                               (jj2 + 1) * 128],
                                            W2T[:, c * 1024 + g_i * 512:
                                                c * 1024 + (g_i + 1) * 512],
                                            start=(c == 0), stop=(c == 31)),
                                        reads=(lhq, "WBIG"), writes=("PS",),
                                        wait=wlm,
                                        inc=(ci == 1 and jj2 == 1
                                             and g_i == 1))
                mm_last[pr] = last
        m2done[n] = mm_last[15]

        prev_out = None
        out_ms = []
        for jj2 in range(2):
            l4 = lb + 4 * jj2
            evs = []
            for g_i in range(2):
                si = m2slots[jj2 * 2 + g_i]
                evw = [mm_last[15], c2_ms]
                if jj2 == 0 and g_i == 0 and ffo_done.get(n - 1) is not None:
                    evw.append(ffo_done[n - 1])
                if jj2 == 1 and g_i == 0:
                    evw.append(prev_out)
                ev = p.op("S", lambda s, si=si, g_i=g_i, jj2=jj2, so=so:
                          s.activation(
                              FFO[:, g_i * 512:(g_i + 1) * 512], slot(si),
                              AF.Identity,
                              scale=stats[:, so + 10 + jj2:so + 11 + jj2]),
                          reads=("PS", ("stats", so + 10, so + 12)),
                          writes=(("FFO", g_i * 512, (g_i + 1) * 512),),
                          wait=evw, inc=True)
                evs.append(ev)
                slot_free[si] = ev
            p.op("V", lambda v: v.bn_stats(bn6[:, 0:6], FFO[:, 0:512]),
                 reads=(("FFO", 0, 512),), writes=(("bn6", 0, 6),),
                 wait=evs)
            p.op("V", lambda v: v.bn_stats(bn6[:, 6:12], FFO[:, 512:1024]),
                 reads=(("FFO", 512, 1024),), writes=(("bn6", 6, 12),))
            p.op("V", lambda v: v.bn_aggr(bn2[:], bn6[:]),
                 reads=("bn6",), writes=("bn2",))
            ve = p.op("V", lambda v, l4=l4: v.tensor_scalar(
                stats[:, l4:l4 + 1], bn2[:, 1:2], EPS, None, op0=ALU.add),
                reads=("bn2",), writes=(("stats", l4, l4 + 1),), inc=True)
            pw = p.op("P", lambda g, l4=l4: g.tensor_tensor(
                stats[:, l4 + 1:l4 + 2], stats[:, l4:l4 + 1],
                NEGH[:, 0:1], op=ALU.pow),
                reads=(("stats", l4, l4 + 1), ("NEGH", 0, 1)),
                writes=(("stats", l4 + 1, l4 + 2),), wait=[ve], inc=True)
            bc = p.op("V", lambda v, l4=l4: v.scalar_tensor_tensor(
                stats[:, l4 + 2:l4 + 3], bn2[:, 0:1], -1.0,
                stats[:, l4 + 1:l4 + 2], op0=ALU.mult, op1=ALU.mult),
                reads=("bn2", ("stats", l4 + 1, l4 + 2)),
                writes=(("stats", l4 + 2, l4 + 3),), wait=[pw], inc=True)
            lnm = p.op("S", lambda s, l4=l4: s.activation(
                FFO[:], FFO[:], AF.Identity,
                bias=stats[:, l4 + 2:l4 + 3],
                scale=stats[:, l4 + 1:l4 + 2]),
                reads=("FFO", ("stats", l4, l4 + 3)), writes=("FFO",),
                wait=[bc], inc=True)
            om = p.op("V", lambda v, jj2=jj2, X1=X1: v.tensor_tensor(
                X1[:, jj2 * 1024:(jj2 + 1) * 1024],
                X1[:, jj2 * 1024:(jj2 + 1) * 1024], FFO[:], op=ALU.add),
                reads=(("X1T", pb * 2048 + jj2 * 1024,
                        pb * 2048 + (jj2 + 1) * 1024), "FFO"),
                writes=(("X1T", pb * 2048 + jj2 * 1024,
                         pb * 2048 + (jj2 + 1) * 1024),),
                wait=[lnm], inc=True)
            out_ms.append(om)
            prev_out = om
        ffo_done[n] = out_ms[-1]
        win_out_dma[n] = p.dma(
            "SP", dy[n * 256:(n + 1) * 256, :].rearrange(
                "(jj pp) nn -> pp jj nn", pp=128),
            X1.rearrange("p (jj nn) -> p jj nn", nn=1024),
            wait=out_ms, sem="od")

    emit_mm1(0)
    for n in range(1, NW):
        emit_mm1(n)
        emit_tail(n - 1)
    emit_tail(NW - 1)

    p.op("SP", lambda s, v=p.e["SP"].dma_val: s.wait_ge(sems["SPd"], v))
    p.op("P", lambda g, v=p.e["P"].dma_val: g.wait_ge(sems["Pd"], v))
    for k in ("xd0", "xd1", "td", "x1d", "wd0", "wd1", "cwd", "srd", "od"):
        p.op("SP", lambda sp, k=k, v=p.dsem.get(k, 0): sp.wait_ge(
            sems[k], v))

    with nc.Block() as blk3:
        p.run(blk3)
    ctx3.close()

    return nc


def kernel(L0_attn_x, L0_attn_Wq, L0_attn_Wk, L0_attn_Wv, L0_attn_Wg,
           L0_attn_Wo, L0_attn_g_norm_w, L0_ffn_w1, L0_ffn_conv_w,
           L0_ffn_conv_b, L0_ffn_w2):
    x = np.ascontiguousarray(np.asarray(L0_attn_x, dtype=np.float32))
    B, T, D_ = x.shape
    assert (B, D_) == (NCORES, D)

    consts = make_consts(T)
    nc = build_program(T)

    common = {
        "wq": np.ascontiguousarray(L0_attn_Wq, np.float32),
        "wk": np.ascontiguousarray(L0_attn_Wk, np.float32),
        "wv": np.ascontiguousarray(L0_attn_Wv, np.float32),
        "wg": np.ascontiguousarray(L0_attn_Wg, np.float32),
        "wo": np.ascontiguousarray(L0_attn_Wo, np.float32),
        "gnw": np.ascontiguousarray(
            np.asarray(L0_attn_g_norm_w, np.float32).reshape(DV, 1)),
        "w1": np.ascontiguousarray(L0_ffn_w1, np.float32),
        "cw": np.ascontiguousarray(
            np.asarray(L0_ffn_conv_w, np.float32).reshape(DH, 3)),
        "cb": np.ascontiguousarray(
            np.asarray(L0_ffn_conv_b, np.float32).reshape(DH, 1)),
        "w2": np.ascontiguousarray(L0_ffn_w2, np.float32),
        **consts,
    }
    in_maps = []
    for i in range(NCORES):
        m = dict(common)
        m["x"] = np.ascontiguousarray(x[i])
        in_maps.append(m)

    res = run_bass_kernel_spmd(nc, in_maps, core_ids=list(range(NCORES)))
    out = np.stack([np.asarray(r["y"], np.float32) for r in res.results])
    return out


# version 05
# speedup vs baseline: 1.0477x; 1.0477x over previous
from contextlib import ExitStack

import numpy as np
import ml_dtypes
import concourse.bass as bass
import concourse.mybir as mybir
from concourse.bass_utils import run_bass_kernel_spmd

F32 = mybir.dt.float32
BF16 = mybir.dt.bfloat16
I32 = mybir.dt.int32
AF = mybir.ActivationFunctionType
ALU = mybir.AluOpType
AX = mybir.AxisListType

D = 1024
H = 16
DK = 64
DV = 128
C = 128
DH = 4096
EPS = 1e-5
NCORES = 8

ENGS = ("T", "V", "S", "P", "SP")


class MS:
    __slots__ = ("eng", "val")

    def __init__(self, eng, val):
        self.eng, self.val = eng, val


class _EngState:
    def __init__(self, name):
        self.name = name
        self.ops = []
        self.sem_val = 0
        self.dma_val = 0
        self.pend_w = []
        self.pend_r = []


def _ivl(label):
    if isinstance(label, tuple):
        return label
    return (label, None, None)


def _conflict(a, b):
    if a[0] != b[0]:
        return False
    if a[1] is None or b[1] is None:
        return True
    return a[1] < b[2] and b[1] < a[2]


class Plan:
    def __init__(self, nc, sems, carry=None):
        self.nc = nc
        self.sems = sems
        self.e = {n: _EngState(n) for n in ENGS}
        self.dsem = {}
        if carry is not None:
            for n in ENGS:
                self.e[n].sem_val = carry.e[n].sem_val
                self.e[n].dma_val = carry.e[n].dma_val
            self.dsem = dict(carry.dsem)

    def _emit(self, eng, fn):
        self.e[eng].ops.append(fn)

    def _auto_drain(self, eng, reads, writes):
        st = self.e[eng]
        rds = [_ivl(x) for x in reads]
        wrs = [_ivl(x) for x in writes]
        hit = any(_conflict(w, r) for w in st.pend_w for r in rds) \
            or any(_conflict(w, v) for w in st.pend_w for v in wrs) \
            or any(_conflict(r, v) for r in st.pend_r for v in wrs)
        if hit:
            self._emit(eng, lambda obj: obj.drain())
            st.pend_w = []
            st.pend_r = []
        if len(st.pend_w) > 96:
            st.pend_w = [(n, None, None) for n in
                         {t[0] for t in st.pend_w}]
        if len(st.pend_r) > 96:
            st.pend_r = [(n, None, None) for n in
                         {t[0] for t in st.pend_r}]
        st.pend_w += wrs
        st.pend_r += rds

    def wait(self, eng, ms):
        if ms is None:
            return
        if isinstance(ms, MS):
            ms = [ms]
        for m in ms:
            if m is None:
                continue
            sem = self.sems[m.eng]
            val = m.val
            if m.eng in ("SPd", "Pd"):
                val = self.e[m.eng[:-1]].dma_val
            self._emit(eng, lambda obj, s=sem, v=val: obj.wait_ge(s, v))

    def op(self, eng, fn, reads=(), writes=(), wait=None, inc=False,
           nodrain=False):
        self.wait(eng, wait)
        if not nodrain:
            self._auto_drain(eng, reads, writes)
        st = self.e[eng]
        if inc:
            st.sem_val += 1
            sem = self.sems[eng]

            def wrapped(obj, fn=fn, sem=sem):
                fn(obj).then_inc(sem, 1)
            self._emit(eng, wrapped)
            return MS(eng, st.sem_val)
        self._emit(eng, lambda obj, fn=fn: fn(obj))
        return None

    def dma(self, eng, out_ap, in_ap, wait=None, slow=False, sem=None):
        self.wait(eng, wait)
        key = sem if sem is not None else eng + "d"
        if key in (eng + "d",):
            self.e[eng].dma_val += 16
            val = self.e[eng].dma_val
        else:
            self.dsem[key] = self.dsem.get(key, 0) + 16
            val = self.dsem[key]
        h = self.sems[key]
        self._emit(eng, lambda obj, o=out_ap, i=in_ap, s=h, sl=slow:
                   obj.dma_start(out=o, in_=i,
                                 allow_slow_non_contiguous=sl)
                   .then_inc(s, 16))
        return MS(key, val)

    def drain(self, eng):
        self._emit(eng, lambda obj: obj.drain())
        self.e[eng].pend_w = []
        self.e[eng].pend_r = []

    def barrier_ms(self, eng):
        st = self.e[eng]
        st.sem_val += 1
        sem = self.sems[eng]
        self._emit(eng, lambda obj, s=sem: obj.sem_inc(s, 1))
        return MS(eng, st.sem_val)

    def run(self, block):
        def mk(name):
            def body(obj):
                for fn in self.e[name].ops:
                    fn(obj)
            return body
        block.tensor(mk("T"))
        block.vector(mk("V"))
        block.scalar(mk("S"))
        block.gpsimd(mk("P"))
        block.sync(mk("SP"))


def make_consts(T):
    h = np.arange(H, dtype=np.float64)
    gamma = 1.0 - np.exp2(-5.0 - h)
    j = np.arange(C, dtype=np.float64)

    inv = 1.0 / 10000.0 ** (np.arange(0, DK, 2, dtype=np.float64) / DK)
    t = np.arange(T, dtype=np.float64)
    f = np.outer(t, inv)
    costab = np.tile(np.cos(f), (1, H)).astype(np.float32)
    sintab = np.tile(np.sin(f), (1, H)).astype(np.float32)

    dkve = np.repeat(gamma[None, :] ** (C - 1 - j[:, None]), DK,
                     axis=1).astype(np.float32)
    i_ = np.arange(C, dtype=np.float64)
    dq2 = (np.repeat(gamma[None, :] ** (i_[:, None] + 1 - C), DV, axis=1)
           / 8.0).astype(np.float32)
    gct = np.ascontiguousarray(
        np.repeat(np.broadcast_to(gamma[None, :] ** C, (DK, H)), DV,
                  axis=1)).astype(np.float32)
    m = (np.arange(C)[None, :] >= np.arange(C)[:, None]).astype(np.float32)
    mask2 = np.ascontiguousarray(np.concatenate([m, m], axis=1))
    ident = np.eye(128, dtype=ml_dtypes.bfloat16)
    ones1 = np.ones((1, 128), dtype=ml_dtypes.bfloat16)
    return {"costab": costab, "sintab": sintab, "dkve": dkve, "dq2": dq2,
            "gct": gct, "mask2": mask2, "ident": ident, "ones1": ones1}


def build_program(T=4096):
    assert T % 256 == 0
    NB = T // 128
    NW = T // 256

    nc = bass.Bass("TRN2", target_bir_lowering=False)

    din = {}
    for nm, shape, dt in [
            ("x", [T, D], F32), ("wq", [D, D], F32), ("wk", [D, D], F32),
            ("wv", [D, 2 * D], F32), ("wg", [D, 2 * D], F32),
            ("wo", [2 * D, D], F32), ("gnw", [DV, 1], F32),
            ("w1", [DH, D], F32), ("cw", [DH, 3], F32), ("cb", [DH, 1], F32),
            ("w2", [D, DH], F32), ("costab", [T, 512], F32),
            ("sintab", [T, 512], F32), ("dkve", [128, 1024], F32),
            ("dq2", [128, 2048], F32), ("gct", [64, 2048], F32),
            ("mask2", [128, 256], F32), ("ident", [128, 128], BF16),
            ("ones1", [1, 128], BF16)]:
        din[nm] = nc.dram_tensor(nm, shape, dt, kind="ExternalInput").ap()

    dy = nc.dram_tensor("y", [T, D], F32, kind="ExternalOutput").ap()
    dx1 = nc.dram_tensor("x1s", [T, D], F32, kind="Internal").ap()
    dw1t = nc.dram_tensor("w1t", [D, DH], BF16, kind="Internal").ap()
    dw2t = nc.dram_tensor("w2t", [DH, D], BF16, kind="Internal").ap()

    psb = lambda nm, shape, dt=F32: nc.alloc_sbuf_tensor(
        nm, list(shape), dt).ap()
    WBIG = psb("WBIG", (128, 65536), BF16)
    WQ = WBIG[:, 0:8192]
    WK = WBIG[:, 8192:16384]
    WV = WBIG[:, 16384:32768]
    WG = WBIG[:, 32768:49152]
    WO = WBIG[:, 49152:65536]
    W1T = WBIG[:, 0:32768]
    W2T = WBIG[:, 32768:65536]

    IDENT = psb("IDENT", (128, 128), BF16)
    ONE1 = psb("ONE1", (1, 128), BF16)
    ONE1F = psb("ONE1F", (1, 128))
    NEGH = psb("NEGH", (128, 26))
    stats = psb("stats", (128, 96))
    bn6 = psb("bn6", (128, 12))
    bn2 = psb("bn2", (128, 2))
    WSC = psb("WSC", (128, 8))
    CW = psb("CW", (128, 96))
    CB = psb("CB", (128, 32))

    PS = nc.alloc_psum_tensor("PS", [128, 3072], F32).ap()
    PT = nc.alloc_psum_tensor("PT", [128, 2048], BF16).ap()
    ptA = PT[:, 0:1024]
    ptB = PT[:, 1024:2048]

    def slot(i):
        return PS[:, 512 * i:512 * (i + 1)]

    sems = {}
    for n in ["T", "V", "S", "P", "SP", "SPd", "Pd", "xd0", "xd1", "td",
              "x1d", "wd0", "wd1", "cwd", "srd", "od"]:
        sems[n] = nc.semaphore("sem_" + n).__enter__()

    ctx1 = ExitStack()
    s1 = lambda nm, shape, dt=F32: ctx1.enter_context(
        nc.sbuf_tensor(nm, list(shape), dt)).ap()

    xt = s1("xt", (128, 2 * D))
    ln1 = s1("ln1", (128, D), BF16)
    ln1fm = ln1
    qrt = s1("qrt", (128, D), BF16)
    krt = s1("krt", (128, D), BF16)
    ropem = s1("ropem", (128, 1024), BF16)
    qfm = s1("qfm", (64, 2048), BF16)
    kfm = s1("kfm", (64, 2048), BF16)
    vtm = s1("vtm", (128, 2048), BF16)
    gsil = s1("gsil", (128, 2048), BF16)
    Ab = s1("Ab", (128, 512), BF16)
    Sst = s1("Sst", (64, 2048))
    Sbb = s1("Sbb", (64, 2048), BF16)
    otm = s1("otm", (128, 2048), BF16)
    og = s1("og", (128, 2048), BF16)
    ogfm = og
    ln2b = s1("ln2b", (128, D), BF16)
    x1t = s1("x1t", (128, D))
    ctab = s1("ctab", (128, 512))
    stab = s1("stab", (128, 512))
    DKVEB = s1("DKVEB", (128, 1024), BF16)
    DQ2B = s1("DQ2B", (128, 2048), BF16)
    GCTB = s1("GCTB", (64, 2048), BF16)
    MASKB = s1("MASKB", (128, 256), BF16)
    GNCOL = s1("GNCOL", (128, 1))

    p = Plan(nc, sems)

    p.dma("SP", IDENT, din["ident"])
    p.dma("SP", ONE1, din["ones1"])
    p.dma("SP", GNCOL, din["gnw"])
    p.dma("SP", xt[:, 0:1024], din["dkve"])
    p.dma("SP", ctab[:, 0:256], din["mask2"])
    p.dma("SP", xt[:, 1024:2048], din["dq2"][:, 0:1024])
    p.dma("SP", x1t[:, 0:1024], din["dq2"][:, 1024:2048])
    call = p.dma("SP", Sst[0:64, 0:2048], din["gct"])
    p.op("V", lambda v: v.tensor_copy(DKVEB[:], xt[:, 0:1024]),
         reads=("xt",), writes=("DKVEB",), wait=[call])
    p.op("V", lambda v: v.tensor_copy(MASKB[:], ctab[:, 0:256]),
         reads=("ctab",), writes=("MASKB",))
    p.op("V", lambda v: v.tensor_copy(DQ2B[:, 0:1024], xt[:, 1024:2048]),
         reads=("xt",), writes=("DQ2B",))
    p.op("V", lambda v: v.tensor_copy(DQ2B[:, 1024:2048], x1t[:, 0:1024]),
         reads=("x1t",), writes=("DQ2B",))
    cast4 = p.op("V", lambda v: v.tensor_copy(GCTB[:], Sst[0:64, 0:2048]),
                 reads=("Sst",), writes=("GCTB",), inc=True)
    consts_done = cast4

    wms = []
    for dst, src, w in [(WQ, din["wq"], 1024), (WK, din["wk"], 1024),
                        (WV, din["wv"], 2048), (WG, din["wg"], 2048),
                        (WO, din["wo"], 1024)]:
        wms.append(p.dma("P", dst.rearrange("p (c n) -> p c n", n=w),
                         src.rearrange("(c p) n -> p c n", p=128)))
    p.op("V", lambda v: v.memset(NEGH[:, 0:16], -0.5),
         writes=(("NEGH", 0, 16),))
    p.op("V", lambda v: v.memset(NEGH[:, 16:17], -1.0),
         writes=(("NEGH", 16, 17),))
    p.op("V", lambda v: v.memset(NEGH[:, 17:18], 127.0),
         writes=(("NEGH", 17, 18),))
    p.op("V", lambda v: v.memset(NEGH[:, 18:19], 1.0 / 127.0),
         writes=(("NEGH", 18, 19),))
    p.op("V", lambda v: v.memset(NEGH[:, 19:21], -1.0),
         writes=(("NEGH", 19, 21),))
    p.op("V", lambda v: v.memset(NEGH[:, 21:23], 127.0),
         writes=(("NEGH", 21, 23),))
    p.op("V", lambda v: v.memset(NEGH[:, 23:25], 1.0 / 127.0),
         writes=(("NEGH", 23, 25),))
    p.op("V", lambda v: v.memset(Sst[:], 0.0), writes=("Sst",),
         wait=[consts_done])
    p.op("V", lambda v: v.memset(ONE1F[:], 1.0), writes=("ONE1F",))
    wo_sc_ms = None
    for c in range(16):
        wo_sc_ms = p.op("V", lambda v, c=c: v.tensor_scalar_mul(
            WO[:, c * 1024:(c + 1) * 1024], WO[:, c * 1024:(c + 1) * 1024],
            GNCOL[:, 0:1]),
            reads=("WBIG", "GNCOL"), writes=("WBIG",),
            wait=[wms[4], consts_done] if c == 0 else None,
            inc=(c == 15))
    wready = wms[:4] + [wo_sc_ms, consts_done]

    slot_free = [None] * 6
    pt_free = {"A": None, "B": None}
    blk_end_V = {}
    x1_dma = {}
    S_done = {}
    ab_free = {}
    rope_done = {}

    for b in range(NB):
        pb = b % 2
        so = 32 * pb
        xb = xt[:, pb * D:(pb + 1) * D]
        x1b = x1t[:]
        ctb = ctab[:]
        stb = stab[:]

        w = [blk_end_V[b - 2]] if b >= 2 else ([consts_done] if b == 0
                                               else None)
        x_ms = p.dma("SP", xb, din["x"][b * 128:(b + 1) * 128, :], wait=w,
                     sem="xd%d" % pb)
        tw0 = [rope_done[b - 1]] if b >= 1 else None
        ct_ms = p.dma("SP", ctb, din["costab"][b * 128:(b + 1) * 128, :],
                      wait=tw0, sem="td")
        st_ms = p.dma("SP", stb, din["sintab"][b * 128:(b + 1) * 128, :],
                      sem="td")

        p.op("V", lambda v, xb=xb: v.bn_stats(bn6[:, 0:6], xb[:, 0:512]),
             reads=("xt",), writes=("bn6",), wait=[x_ms])
        p.op("V", lambda v, xb=xb: v.bn_stats(bn6[:, 6:12], xb[:, 512:1024]),
             reads=("xt",), writes=("bn6",))
        p.op("V", lambda v: v.bn_aggr(bn2[:], bn6[:]),
             reads=("bn6",), writes=("bn2",))
        ve = p.op("V", lambda v, so=so: v.tensor_scalar(
            stats[:, so:so + 1], bn2[:, 1:2], EPS, None, op0=ALU.add),
            reads=("bn2",), writes=("stats",), inc=True)
        pw = p.op("P", lambda g, so=so: g.tensor_tensor(
            stats[:, so + 1:so + 2], stats[:, so:so + 1], NEGH[:, 0:1],
            op=ALU.pow), wait=[ve], inc=True)
        bc = p.op("V", lambda v, so=so: v.scalar_tensor_tensor(
            stats[:, so + 2:so + 3], bn2[:, 0:1], -1.0,
            stats[:, so + 1:so + 2], op0=ALU.mult, op1=ALU.mult),
            reads=("bn2", "stats"), writes=("stats",), wait=[pw], inc=True)
        ln1_ms = p.op("S", lambda s, xb=xb, so=so: s.activation(
            ln1[:], xb, AF.Identity, bias=stats[:, so + 2:so + 3],
            scale=stats[:, so + 1:so + 2]),
            reads=("xt", "stats"), writes=("ln1",), wait=[bc], inc=True)

        tw = [ln1_ms]
        if pt_free["A"] is not None:
            tw.append(pt_free["A"])
        tms = None
        for c in range(8):
            tms = p.op("T", lambda t, c=c: t.transpose(
                ptA[:, c * 128:(c + 1) * 128], ln1[:, c * 128:(c + 1) * 128],
                IDENT[:]),
                reads=("ln1", "IDENT"), writes=("PT",),
                wait=tw if c == 0 else None, inc=(c == 7))
        ln1fm_ms = p.op("S", lambda s: s.activation(ln1fm[:], ptA,
                                                    AF.Identity),
                        reads=("PT",), writes=("ln1",), wait=[tms],
                        inc=True)
        pt_free["A"] = ln1fm_ms

        group_src = [(WQ, 1024, 0), (WQ, 1024, 512), (WK, 1024, 0),
                     (WK, 1024, 512), (WV, 2048, 0), (WV, 2048, 512),
                     (WV, 2048, 1024), (WV, 2048, 1536), (WG, 2048, 0),
                     (WG, 2048, 512), (WG, 2048, 1024), (WG, 2048, 1536)]
        proj_ms = {}

        def emit_group(gi):
            Wsrc, wmul, coff = group_src[gi]
            si = gi % 6
            wl = [ln1fm_ms]
            if b == 0 and gi == 0:
                wl = wl + wready
            if slot_free[si] is not None:
                wl.append(slot_free[si])
            ms = None
            for c in range(8):
                ms = p.op("T", lambda t, si=si, c=c, Wsrc=Wsrc, wmul=wmul,
                          coff=coff: t.matmul(
                              slot(si), ln1fm[:, c * 128:(c + 1) * 128],
                              Wsrc[:, c * wmul + coff:c * wmul + coff + 512],
                              start=(c == 0), stop=(c == 7)),
                          reads=("ln1", "WBIG"), writes=("PS",),
                          wait=wl if c == 0 else None, inc=(c == 7))
            proj_ms[gi] = ms
            if gi >= 4:
                if gi < 8:
                    dstn, dst, func = "vtm", vtm[:, (gi - 4) * 512:
                                                 (gi - 3) * 512], AF.Identity
                else:
                    dstn, dst, func = "gsil", gsil[:, (gi - 8) * 512:
                                                   (gi - 7) * 512], AF.Silu
                ev = p.op("S", lambda s, si=si, dst=dst, func=func:
                          s.activation(dst, slot(si), func),
                          reads=("PS",), writes=(dstn,), wait=[ms], inc=True)
                slot_free[si] = ev

        for gi in range(6):
            emit_group(gi)

        def emit_rope(psrc, dst, with_dkv, wait_ms, nm_dst):
            s1_ = psrc.rearrange("p (h two d) -> p h two d", two=2, d=32)
            x1s_, x2s_ = s1_[:, :, 0, :], s1_[:, :, 1, :]
            o1 = dst.rearrange("p (h two d) -> p h two d", two=2, d=32)
            o1a, o2a = o1[:, :, 0, :], o1[:, :, 1, :]
            cv = ctb.rearrange("p (h d) -> p h d", d=32)
            sv = stb.rearrange("p (h d) -> p h d", d=32)
            m0 = ropem[:, 0:512].rearrange("p (h d) -> p h d", d=32)
            m1 = ropem[:, 512:1024].rearrange("p (h d) -> p h d", d=32)
            dkv = DKVEB.rearrange("p (h two d) -> p h two d", two=2, d=32)
            p.op("V", lambda v: v.tensor_mul(m0, x1s_, cv),
                 reads=("PS", "ctab"), writes=("ropem",), wait=wait_ms)
            p.op("V", lambda v: v.tensor_mul(m1, x2s_, sv),
                 reads=("PS", "stab"), writes=("ropem",))
            if with_dkv:
                p.op("V", lambda v: v.tensor_sub(m0, m0, m1),
                     reads=("ropem",), writes=("ropem",))
                p.op("V", lambda v: v.tensor_mul(o1a, m0, dkv[:, :, 0, :]),
                     reads=("ropem", "DKVEB"), writes=(nm_dst,))
            else:
                p.op("V", lambda v: v.tensor_sub(o1a, m0, m1),
                     reads=("ropem",), writes=(nm_dst,))
            p.op("V", lambda v: v.tensor_mul(m0, x1s_, sv),
                 reads=("PS", "stab"), writes=("ropem",))
            p.op("V", lambda v: v.tensor_mul(m1, x2s_, cv),
                 reads=("PS", "ctab"), writes=("ropem",))
            if with_dkv:
                p.op("V", lambda v: v.tensor_add(m0, m0, m1),
                     reads=("ropem",), writes=("ropem",))
                return p.op("V", lambda v: v.tensor_mul(
                    o2a, m0, dkv[:, :, 1, :]),
                    reads=("ropem", "DKVEB"), writes=(nm_dst,), inc=True)
            return p.op("V", lambda v: v.tensor_add(o2a, m0, m1),
                        reads=("ropem",), writes=(nm_dst,), inc=True)

        rope_w_q = [proj_ms[0], proj_ms[1], st_ms]
        if b > 0:
            rope_w_q.append(S_done[b - 1])
        qr_ms = emit_rope(PS[:, 0:1024], qrt, False, rope_w_q, "qrt")
        kr_ms = emit_rope(PS[:, 1024:2048], krt, True,
                          [proj_ms[2], proj_ms[3]], "krt")
        rope_done[b] = kr_ms
        slot_free[0] = slot_free[1] = qr_ms
        slot_free[2] = slot_free[3] = kr_ms
        for gi in range(6, 12):
            emit_group(gi)
        vg_ev = slot_free[5]

        if b > 0:
            sg_v = p.op("V", lambda v: v.tensor_mul(Sst[:], Sst[:], GCTB[:]),
                        reads=("Sst", "GCTB"), writes=("Sst",),
                        wait=[S_done[b - 1]], inc=True)
            sb_ms = p.op("S", lambda s: s.activation(Sbb[:], Sst[:],
                                                     AF.Identity),
                         reads=("Sst",), writes=("Sbb",), wait=[sg_v],
                         inc=True)
        else:
            sg_v = sb_ms = None

        ev_fm = {}
        for wv_i, (src, dst, nm_s, nm_d, h0, rms_) in enumerate([
                (qrt, qfm, "qrt", "qfm", 0, None), (qrt, qfm, "qrt", "qfm", 8, None),
                (krt, kfm, "krt", "kfm", 0, None), (krt, kfm, "krt", "kfm", 8, None)]):
            bk = "A" if wv_i % 2 == 0 else "B"
            bank = ptA if bk == "A" else ptB
            wl = [qr_ms if src is qrt else kr_ms]
            if pt_free[bk] is not None:
                wl.append(pt_free[bk])
            tms = None
            for hh in range(8):
                hx = h0 + hh
                tms = p.op("T", lambda t, bank=bank, hh=hh, hx=hx, src=src:
                           t.transpose(bank[0:64, hh * 128:(hh + 1) * 128],
                                       src[:, hx * 64:(hx + 1) * 64],
                                       IDENT[:]),
                           reads=(nm_s, "IDENT"), writes=("PT",),
                           wait=wl if hh == 0 else None, inc=(hh == 7))
            ev_fm[wv_i] = p.op(
                "S", lambda s, bank=bank, dst=dst, h0=h0: s.activation(
                    dst[0:64, h0 * 128:(h0 + 8) * 128], bank[0:64, 0:1024],
                    AF.Identity),
                reads=("PT",), writes=(nm_d,), wait=[tms], inc=True)
            pt_free[bk] = ev_fm[wv_i]

        mask_ms = {}
        o_ev = {}
        for wv_i in range(4):
            si = wv_i % 2
            reg = 0
            wl = [ev_fm[1], ev_fm[3]]
            if slot_free[si] is not None:
                wl.append(slot_free[si])
            ams = None
            for hh in range(4):
                hx = wv_i * 4 + hh
                ams = p.op("T", lambda t, si=si, hh=hh, hx=hx: t.matmul(
                    slot(si)[:, hh * 128:(hh + 1) * 128],
                    kfm[0:64, hx * 128:(hx + 1) * 128],
                    qfm[0:64, hx * 128:(hx + 1) * 128],
                    start=True, stop=True),
                    reads=("kfm", "qfm"), writes=("PS",),
                    wait=wl if hh == 0 else None, inc=(hh == 3))
            mwl = [ams]
            if ab_free.get(0) is not None:
                mwl.append(ab_free[0])
            msk = None
            for pr in range(2):
                msk = p.op("V", lambda v, si=si, reg=reg, pr=pr: v.tensor_mul(
                    Ab[:, reg * 512 + pr * 256:reg * 512 + (pr + 1) * 256],
                    slot(si)[:, pr * 256:(pr + 1) * 256], MASKB[:]),
                    reads=("PS", "MASKB"), writes=("Ab",),
                    wait=mwl if pr == 0 else None, inc=(pr == 1))
            mask_ms[wv_i] = msk
            slot_free[si] = msk

            osi = 2 + wv_i
            owl = [msk, vg_ev]
            if sb_ms is not None:
                owl.append(sb_ms)
            if slot_free[osi] is not None:
                owl.append(slot_free[osi])
            oms = None
            for hh in range(4):
                hx = wv_i * 4 + hh
                oms = p.op("T", lambda t, osi=osi, hh=hh, hx=hx, reg=reg,
                           sf=(b == 0): t.matmul(
                               slot(osi)[:, hh * 128:(hh + 1) * 128],
                               Ab[:, reg * 512 + hh * 128:
                                   reg * 512 + (hh + 1) * 128],
                               vtm[:, hx * 128:(hx + 1) * 128],
                               start=True, stop=sf),
                           reads=("Ab", "vtm"), writes=("PS",),
                           wait=owl if hh == 0 else None,
                           inc=(b == 0 and hh == 3))
                if b > 0:
                    oms = p.op("T", lambda t, osi=osi, hh=hh, hx=hx: t.matmul(
                        slot(osi)[:, hh * 128:(hh + 1) * 128],
                        qfm[0:64, hx * 128:(hx + 1) * 128],
                        Sbb[0:64, hx * 128:(hx + 1) * 128],
                        start=False, stop=True),
                        reads=("qfm", "Sbb"), writes=("PS",), inc=(hh == 3))
            ab_free[0] = oms
            o_ev[wv_i] = p.op("V", lambda v, osi=osi, wv_i=wv_i: v.tensor_mul(
                otm[:, wv_i * 512:(wv_i + 1) * 512], slot(osi),
                DQ2B[:, wv_i * 512:(wv_i + 1) * 512]),
                reads=("PS", "DQ2B"), writes=("otm",), wait=[oms], inc=True)
            slot_free[osi] = o_ev[wv_i]

        kv_upd = []
        for wv_i in range(4):
            si = wv_i % 2
            wl = [kr_ms, vg_ev, slot_free[si]]
            kms = None
            for hh in range(4):
                hx = wv_i * 4 + hh
                kms = p.op("T", lambda t, si=si, hh=hh, hx=hx: t.matmul(
                    slot(si)[0:64, hh * 128:(hh + 1) * 128],
                    krt[:, hx * 64:(hx + 1) * 64],
                    vtm[:, hx * 128:(hx + 1) * 128],
                    start=True, stop=True),
                    reads=("krt", "vtm"), writes=("PS",),
                    wait=wl if hh == 0 else None, inc=(hh == 3))
            if b == 0:
                msv = p.op("V", lambda v, si=si, wv_i=wv_i: v.tensor_copy(
                    Sst[0:64, wv_i * 512:(wv_i + 1) * 512],
                    slot(si)[0:64, :]),
                    reads=("PS",), writes=("Sst",), wait=[kms], inc=True)
            else:
                msv = p.op("V", lambda v, si=si, wv_i=wv_i: v.tensor_add(
                    Sst[0:64, wv_i * 512:(wv_i + 1) * 512],
                    Sst[0:64, wv_i * 512:(wv_i + 1) * 512],
                    slot(si)[0:64, :]),
                    reads=("PS", "Sst"), writes=("Sst",),
                    wait=[kms, sg_v, sb_ms], inc=True)
            kv_upd.append(msv)
            slot_free[si] = msv
        S_done[b] = kv_upd[-1]

        p.op("V", lambda v: v.tensor_mul(og[:], otm[:], otm[:]),
             reads=("otm",), writes=("og",), wait=[o_ev[3]])
        p.op("V", lambda v, so=so: v.reduce_sum(
            stats[:, so + 8:so + 24].rearrange("p (h one) -> p h one",
                                               one=1),
            og[:].rearrange("p (h d) -> p h d", d=128), axis=AX.X),
            reads=("og",), writes=("stats",))
        ve2 = p.op("V", lambda v, so=so: v.tensor_scalar(
            stats[:, so + 8:so + 24], stats[:, so + 8:so + 24], 1.0 / DV,
            EPS, op0=ALU.mult, op1=ALU.add),
            reads=("stats",), writes=("stats",), inc=True)
        pw2 = p.op("P", lambda g, so=so: g.tensor_tensor(
            stats[:, so + 8:so + 24], stats[:, so + 8:so + 24],
            NEGH[:, 0:16], op=ALU.pow), wait=[ve2], inc=True)
        ogm = None
        for hx in range(16):
            ogm = p.op("V", lambda v, hx=hx, so=so: v.tensor_scalar_mul(
                og[:, hx * 128:(hx + 1) * 128],
                otm[:, hx * 128:(hx + 1) * 128],
                stats[:, so + 8 + hx:so + 9 + hx]),
                reads=("otm", "stats"), writes=("og",),
                wait=[pw2] if hx == 0 else None, inc=(hx == 15))
        og_ms = p.op("V", lambda v: v.tensor_mul(og[:], og[:], gsil[:]),
                     reads=("og", "gsil"), writes=("og",), inc=True)

        ogfm_ev = {}
        for wv_i in range(2):
            bk = "A" if wv_i == 0 else "B"
            bank = ptA if wv_i == 0 else ptB
            wl = [og_ms]
            if pt_free[bk] is not None:
                wl.append(pt_free[bk])
            tms = None
            for hh in range(8):
                c = wv_i * 8 + hh
                tms = p.op("T", lambda t, bank=bank, hh=hh, c=c: t.transpose(
                    bank[:, hh * 128:(hh + 1) * 128],
                    og[:, c * 128:(c + 1) * 128], IDENT[:]),
                    reads=("og", "IDENT"), writes=("PT",),
                    wait=wl if hh == 0 else None, inc=(hh == 7))
            ogfm_ev[wv_i] = p.op(
                "S", lambda s, bank=bank, wv_i=wv_i: s.activation(
                    og[:, wv_i * 1024:(wv_i + 1) * 1024], bank[:, 0:1024],
                    AF.Identity),
                reads=("PT",), writes=("og",), wait=[tms], inc=True)
            pt_free[bk] = ogfm_ev[wv_i]

        wo_t = {}
        for g_i in range(2):
            si = g_i
            wl = [ogfm_ev[1], slot_free[si]]
            wms2 = None
            for c in range(16):
                wms2 = p.op("T", lambda t, si=si, c=c, g_i=g_i: t.matmul(
                    slot(si), og[:, c * 128:(c + 1) * 128],
                    WO[:, c * 1024 + g_i * 512:c * 1024 + (g_i + 1) * 512],
                    start=(c == 0), stop=(c == 15)),
                    reads=("og", "WBIG"), writes=("PS",),
                    wait=wl if c == 0 else None, inc=(c == 15))
            wo_t[g_i] = wms2

        p.op("V", lambda v: v.bn_stats(bn6[:, 0:6], PS[:, 0:512]),
             reads=("PS",), writes=("bn6",), wait=[wo_t[0], wo_t[1]])
        p.op("V", lambda v: v.bn_stats(bn6[:, 6:12], PS[:, 512:1024]),
             reads=("PS",), writes=("bn6",))
        p.op("V", lambda v: v.bn_aggr(bn2[:], bn6[:]),
             reads=("bn6",), writes=("bn2",))
        ve3 = p.op("V", lambda v, so=so: v.tensor_scalar(
            stats[:, so + 4:so + 5], bn2[:, 1:2], EPS, None, op0=ALU.add),
            reads=("bn2",), writes=("stats",), inc=True)
        pw3 = p.op("P", lambda g, so=so: g.tensor_tensor(
            stats[:, so + 5:so + 6], stats[:, so + 4:so + 5], NEGH[:, 0:1],
            op=ALU.pow), wait=[ve3], inc=True)
        bc3 = p.op("V", lambda v, so=so: v.scalar_tensor_tensor(
            stats[:, so + 6:so + 7], bn2[:, 0:1], -1.0,
            stats[:, so + 5:so + 6], op0=ALU.mult, op1=ALU.mult),
            reads=("bn2", "stats"), writes=("stats",), wait=[pw3], inc=True)
        ln2_ms = p.op("S", lambda s, so=so: s.activation(
            ln2b[:], PS[:, 0:1024], AF.Identity,
            bias=stats[:, so + 6:so + 7], scale=stats[:, so + 5:so + 6]),
            reads=("PS", "stats"), writes=("ln2b",), wait=[bc3], inc=True)
        slot_free[0] = slot_free[1] = ln2_ms
        xw = [ln2_ms]
        if b >= 1:
            xw.append(x1_dma[b - 1])
        x1_ms = p.op("V", lambda v, x1b=x1b, xb=xb: v.tensor_add(
            x1b, xb, ln2b[:]),
            reads=("xt", "ln2b"), writes=("x1t",), wait=xw, inc=True)
        blk_end_V[b] = x1_ms
        x1_dma[b] = p.dma("SP", dx1[b * 128:(b + 1) * 128, :], x1b,
                          wait=[x1_ms], sem="x1d")

    p.op("SP", lambda s, v=p.e["SP"].dma_val: s.wait_ge(sems["SPd"], v))
    p.op("P", lambda g, v=p.e["P"].dma_val: g.wait_ge(sems["Pd"], v))
    for k in ("xd0", "xd1", "td", "x1d"):
        p.op("SP", lambda sp, k=k, v=p.dsem.get(k, 0): sp.wait_ge(
            sems[k], v))

    with nc.Block() as blk1:
        p.run(blk1)
    ctx1.close()

    ctxp = ExitStack()
    s2_ = lambda nm, shape, dt=F32: ctxp.enter_context(
        nc.sbuf_tensor(nm, list(shape), dt)).ap()
    WSTG = s2_("WSTG", (128, 2 * 4096))
    WTERN = s2_("WTERN", (128, 4096), BF16)
    WTT = s2_("WTT", (128, 4096), BF16)
    WSUM = s2_("WSUM", (128, 64))
    UUW = s2_("UUW", (128, 1024))
    G1W = s2_("G1W", (128, 1024))
    G2W = s2_("G2W", (128, 1024))

    p = Plan(nc, sems, carry=p)
    slot_free = [None] * 6
    pt_free = {"A": None, "B": None}

    sum_last2 = [None, None]
    col = 0
    for dsrc, nchunk, width in [(din["w1"], 32, 1024), (din["w2"], 8, 4096)]:
        for r in range(nchunk):
            pbuf = WSTG[:, (r % 2) * 4096:(r % 2) * 4096 + width]
            wl = [sum_last2[r % 2]] if sum_last2[r % 2] is not None else None
            dm = p.dma("P", pbuf, dsrc[r * 128:(r + 1) * 128, :], wait=wl,
                       sem="wd%d" % (r % 2))
            ms = p.op("V", lambda v, pbuf=pbuf, col=col: v.reduce_sum(
                WSUM[:, col:col + 1], pbuf, axis=AX.X,
                apply_absolute_value=True),
                reads=("WSTG",), writes=("WSUM",), wait=[dm], inc=True)
            sum_last2[r % 2] = ms
            col += 1
    p.op("V", lambda v: v.reduce_sum(WSUM[:, 40:41], WSUM[:, 0:32],
                                     axis=AX.X),
         reads=("WSUM",), writes=("WSUM",),
         wait=[sum_last2[0], sum_last2[1]])
    t12 = p.op("V", lambda v: v.reduce_sum(WSUM[:, 41:42], WSUM[:, 32:40],
                                           axis=AX.X),
               reads=("WSUM",), writes=("WSUM",), inc=True)
    wsb = p.op("V", lambda v: v.tensor_copy(WTERN[:, 0:2], WSUM[:, 40:42]),
               reads=("WSUM",), writes=("WTERN",), wait=[t12], inc=True)
    tpm = p.op("T", lambda t: t.transpose(ptA[0:2, 0:128],
                                          WTERN[:, 0:2], IDENT[:]),
               reads=("WTERN", "IDENT"), writes=("PT",), wait=[wsb],
               inc=True)
    p.op("V", lambda v: v.reduce_sum(stats[0:2, 0:1], ptA[0:2, 0:128],
                                     axis=AX.X),
         reads=("PT",), writes=("stats",), wait=[tpm])
    p.op("V", lambda v: v.tensor_scalar(stats[0:2, 1:2], stats[0:2, 0:1],
                                        1.0 / (DH * D), 1e-5,
                                        op0=ALU.mult, op1=ALU.max),
         reads=("stats",), writes=("stats",))
    rc = p.op("V", lambda v: v.reciprocal(stats[0:2, 2:3], stats[0:2, 1:2]),
              reads=("stats",), writes=("stats",), inc=True)
    rcb = p.op("V", lambda v: v.tensor_copy(WTERN[0:2, 4:6],
                                            stats[0:2, 1:3]),
               reads=("stats",), writes=("WTERN",), wait=[rc], inc=True)
    tq1 = p.op("T", lambda t: t.transpose(ptA[0:1, 0:2], WTERN[0:2, 4:5],
                                          IDENT[0:2, 0:2]),
               reads=("WTERN", "IDENT"), writes=("PT",), wait=[rcb, tpm])
    tq2 = p.op("T", lambda t: t.transpose(ptA[0:1, 2:4], WTERN[0:2, 5:6],
                                          IDENT[0:2, 0:2]),
               reads=("WTERN", "IDENT"), writes=("PT",), inc=True)
    srow = p.op("V", lambda v: v.tensor_copy(WSUM[0:1, 48:52],
                                             ptA[0:1, 0:4]),
                reads=("PT",), writes=("WSUM",), wait=[tq2], inc=True)

    def stat_bcast(src_11, dst_col, wait_ms):
        mm = p.op("T", lambda t, src_11=src_11: t.matmul(
            slot(5)[:, 0:1], ONE1F[:], src_11, start=True, stop=True),
            reads=("ONE1F", "WSUM"), writes=("PS",), wait=wait_ms, inc=True)
        return p.op("V", lambda v, dst_col=dst_col: v.tensor_copy(
            dst_col, slot(5)[:, 0:1]),
            reads=("PS",), writes=("WSC",), wait=[mm], inc=True)

    b0 = stat_bcast(WSUM[0:1, 50:51], WSC[:, 0:1], [srow])
    b1 = stat_bcast(WSUM[0:1, 48:49], WSC[:, 1:2], [b0])
    b2 = stat_bcast(WSUM[0:1, 51:52], WSC[:, 2:3], [b1])
    b3 = stat_bcast(WSUM[0:1, 49:50], WSC[:, 3:4], [b2])

    cw_dm = p.dma("P", CW[:].rearrange("p (c k) -> p c k", k=3),
                  din["cw"].rearrange("(c p) k -> p c k", p=128), sem="cwd")
    cb_dm = p.dma("P", CB[:].rearrange("p (c one) -> p c one", one=1),
                  din["cb"].rearrange("(c p) one -> p c one", p=128),
                  slow=True, sem="cwd")
    p.op("V", lambda v: v.tensor_scalar_mul(CW[:], CW[:], WSC[:, 1:2]),
         reads=("CW", "WSC"), writes=("CW",), wait=[cb_dm, b1])

    pb_free = {}
    tern_T_last = tq2
    last_store = None
    for wi, (dsrc, ddst, nchunk, width, swc) in enumerate([
            (din["w1"], dw1t, 32, 1024, WSC[:, 0:1]),
            (din["w2"], dw2t, 8, 4096, WSC[:, 2:3])]):
        for r in range(nchunk):
            pbuf = WSTG[:, (r % 2) * 4096:(r % 2) * 4096 + width]
            wl = [b3]
            prev = pb_free.get((wi, r - 2))
            if prev is None and wi == 1 and r < 2:
                prev = pb_free.get((0, 30 + r))
            if prev is not None:
                wl.append(prev)
            dm = p.dma("P", pbuf, dsrc[r * 128:(r + 1) * 128, :], wait=wl,
                       sem="wd%d" % (r % 2))
            qt_ms = None
            for st0 in range(0, width, 1024):
                wwl = [dm]
                if tern_T_last is not None and st0 == 0:
                    wwl.append(tern_T_last)
                p.op("V", lambda v, pbuf=pbuf, st0=st0, swc=swc:
                     v.tensor_scalar_mul(UUW[:], pbuf[:, st0:st0 + 1024],
                                         swc),
                     reads=("WSTG", "WSC"), writes=("UUW",), wait=wwl)
                p.op("V", lambda v: v.tensor_single_scalar(
                    G1W[:], UUW[:], 0.5, op=ALU.is_ge),
                    reads=("UUW",), writes=("G1W",))
                p.op("V", lambda v: v.tensor_single_scalar(
                    G2W[:], UUW[:], -0.5, op=ALU.is_le),
                    reads=("UUW",), writes=("G2W",))
                qt_ms = p.op("V", lambda v, st0=st0: v.tensor_sub(
                    WTERN[:, st0:st0 + 1024], G1W[:], G2W[:]),
                    reads=("G1W", "G2W"), writes=("WTERN",),
                    inc=(st0 + 1024 >= width))
            pb_free[(wi, r)] = qt_ms
            ntc = width // 128
            ev_list = []
            for w0 in range(0, ntc, 8):
                bk = "A" if (w0 // 8) % 2 == 0 else "B"
                bank = ptA if bk == "A" else ptB
                nw = min(8, ntc - w0)
                wl2 = [qt_ms]
                if pt_free[bk] is not None:
                    wl2.append(pt_free[bk])
                tms = None
                for k in range(nw):
                    c = w0 + k
                    tms = p.op("T", lambda t, bank=bank, k=k, c=c:
                               t.transpose(bank[:, k * 128:(k + 1) * 128],
                                           WTERN[:, c * 128:(c + 1) * 128],
                                           IDENT[:]),
                               reads=("WTERN", "IDENT"), writes=("PT",),
                               wait=wl2 if k == 0 else None,
                               inc=(k == nw - 1))
                ewl = [tms]
                if last_store is not None and w0 == 0:
                    ewl.append(last_store)
                ev = p.op("S", lambda s, bank=bank, w0=w0, nw=nw:
                          s.activation(WTT[:, w0 * 128:(w0 + nw) * 128],
                                       bank[:, 0:nw * 128], AF.Identity),
                          reads=("PT",), writes=("WTT",), wait=ewl, inc=True)
                ev_list.append(ev)
                pt_free[bk] = ev
            tern_T_last = tms
            for ci in range(ntc):
                last_store = p.dma(
                    "SP",
                    ddst[ci * 128:(ci + 1) * 128, r * 128:(r + 1) * 128],
                    WTT[:, ci * 128:(ci + 1) * 128],
                    wait=[ev_list[ci // 8]] if ci % 8 == 0 else None)

    p.dma("SP", W1T.rearrange("p (c n) -> p c n", n=4096),
          dw1t.rearrange("(c p) n -> p c n", p=128), wait=[last_store])
    p.dma("SP", W2T.rearrange("p (c n) -> p c n", n=1024),
          dw2t.rearrange("(c p) n -> p c n", p=128))
    p.op("SP", lambda s, v=p.e["SP"].dma_val: s.wait_ge(sems["SPd"], v))
    p.op("P", lambda g, v=p.e["P"].dma_val: g.wait_ge(sems["Pd"], v))

    with nc.Block() as blk2:
        p.run(blk2)
    ctxp.close()

    ctx3 = ExitStack()
    s3_ = lambda nm, shape, dt=F32: ctx3.enter_context(
        nc.sbuf_tensor(nm, list(shape), dt)).ap()
    X1T = s3_("X1T", (128, 2 * 2048))
    LN3F = s3_("LN3F", (128, 1024))
    XQTM = s3_("XQTM", (128, 1024), BF16)
    XQFM = s3_("XQFM", (128, 2048), BF16)
    HT0 = s3_("HT0", (128, 32 * 258), BF16)
    HT1 = s3_("HT1", (128, 32 * 258), BF16)
    A0B = s3_("A0B", (128, 2 * 256))
    ABQ = s3_("ABQ", (128, 2 * 512), BF16)
    ABP = ABQ
    HQ = ABQ
    RMA = s3_("RMA", (128, 512), BF16)
    RMB = s3_("RMB", (128, 512), BF16)
    RMF = RMB[:, 0:256]
    SB2S = s3_("SB2S", (128, 2), BF16)
    UU = s3_("UU", (128, 2 * 512))
    IA = s3_("IA", (128, 2 * 512), I32)
    RR = IA
    SBCS = RMA
    SB2 = s3_("SB2", (2, 128), BF16)
    SROWB = s3_("SROWB", (1, 256), BF16)
    FFO = s3_("FFO", (128, 1024))
    LN4B = FFO

    p = Plan(nc, sems, carry=p)
    slot_free = [None] * 6
    pt_free = {"A": None, "B": None}

    HTS = [HT0, HT1]
    HTnm = ["HT0", "HT1"]

    def hmid(ht, c, a, bcol):
        return ht[:, c * 258 + a:c * 258 + bcol]

    mm1_ev = {}
    win_out_dma = {}
    ffo_done = {}
    halo_ms = {}
    xq_evs = {}
    ln3f_free = [None]
    m2done = {}
    evac_ms = {}

    mz_ms = None
    for c in range(32):
        mz_ms = p.op("V", lambda v, c=c: v.memset(hmid(HT0, c, 0, 1), 0.0),
                     writes=("HT0",), inc=(c == 31))
    halo_ms[0] = mz_ms

    def emit_mm1(n):
        pb = n % 2
        X1 = X1T[:, pb * 2048:(pb + 1) * 2048]
        HT = HTS[pb]
        wl = []
        if n >= 2:
            wl.append(win_out_dma.get(n - 2))
        x_ms = p.dma("SP", X1.rearrange("p (jj nn) -> p jj nn", nn=1024),
                     dx1[n * 256:(n + 1) * 256, :].rearrange(
                         "(jj pp) nn -> pp jj nn", pp=128),
                     wait=[m for m in wl if m is not None] or None,
                     sem="xd%d" % pb)
        tp_evs = []
        for jj in range(2):
            so = 48 * pb + 8 * jj
            x1lb = ("X1T", pb * 2048 + jj * 1024, pb * 2048 + (jj + 1) * 1024)
            xb = X1[:, jj * 1024:(jj + 1) * 1024]
            p.op("V", lambda v, xb=xb: v.bn_stats(bn6[:, 0:6],
                                                  xb[:, 0:512]),
                 reads=(x1lb,), writes=(("bn6", 0, 6),), wait=[x_ms])
            p.op("V", lambda v, xb=xb: v.bn_stats(bn6[:, 6:12],
                                                  xb[:, 512:1024]),
                 reads=(x1lb,), writes=(("bn6", 6, 12),))
            p.op("V", lambda v: v.bn_aggr(bn2[:], bn6[:]),
                 reads=("bn6",), writes=("bn2",))
            ve = p.op("V", lambda v, so=so: v.tensor_scalar(
                stats[:, so:so + 1], bn2[:, 1:2], EPS, None, op0=ALU.add),
                reads=("bn2",), writes=(("stats", so, so + 1),), inc=True)
            pw = p.op("P", lambda g, so=so: g.tensor_tensor(
                stats[:, so + 1:so + 2], stats[:, so:so + 1], NEGH[:, 0:1],
                op=ALU.pow),
                reads=(("stats", so, so + 1), ("NEGH", 0, 1)),
                writes=(("stats", so + 1, so + 2),), wait=[ve], inc=True)
            bc = p.op("V", lambda v, so=so: v.scalar_tensor_tensor(
                stats[:, so + 2:so + 3], bn2[:, 0:1], -1.0,
                stats[:, so + 1:so + 2], op0=ALU.mult, op1=ALU.mult),
                reads=("bn2", ("stats", so + 1, so + 2)),
                writes=(("stats", so + 2, so + 3),), wait=[pw], inc=True)
            lnm = p.op("S", lambda s, xb=xb, so=so: s.activation(
                LN3F[:], xb, AF.Identity, bias=stats[:, so + 2:so + 3],
                scale=stats[:, so + 1:so + 2]),
                reads=(x1lb, ("stats", so, so + 3)), writes=("LN3F",),
                wait=[bc, ln3f_free[0]] if ln3f_free[0] else [bc],
                inc=True)
            mx = p.op("V", lambda v, so=so: v.reduce_max(
                stats[:, so + 3:so + 4], LN3F[:], axis=AX.X,
                apply_absolute_value=True),
                reads=("LN3F",), writes=(("stats", so + 3, so + 4),),
                wait=[lnm], inc=True)
            mxc = p.op("V", lambda v, so=so: v.tensor_scalar_max(
                stats[:, so + 4:so + 5], stats[:, so + 3:so + 4], 1e-5),
                reads=(("stats", so + 3, so + 4),),
                writes=(("stats", so + 4, so + 5),), inc=True)
            p.op("P", lambda g, so=so: g.tensor_tensor(
                stats[:, so + 5:so + 6], stats[:, so + 4:so + 5],
                NEGH[:, 16:17], op=ALU.pow),
                reads=(("stats", so + 4, so + 5), ("NEGH", 16, 17)),
                writes=(("stats", so + 5, so + 6),), wait=[mxc])
            p.op("P", lambda g, so=so: g.tensor_tensor(
                stats[:, so + 6:so + 7], stats[:, so + 5:so + 6],
                NEGH[:, 17:18], op=ALU.mult),
                reads=(("stats", so + 5, so + 6), ("NEGH", 17, 18)),
                writes=(("stats", so + 6, so + 7),))
            sc_ms = p.op("P", lambda g, so=so: g.tensor_tensor(
                stats[:, so + 7:so + 8], stats[:, so + 4:so + 5],
                NEGH[:, 18:19], op=ALU.mult),
                reads=(("stats", so + 4, so + 5), ("NEGH", 18, 19)),
                writes=(("stats", so + 7, so + 8),), inc=True)
            xq_ms = None
            sl = lambda st: (("UU", st, st + 512), ("IA", st, st + 512))
            for step in range(5):
                for sp_ in range(2):
                    st = sp_ * 512
                    lf = LN3F[:, st:st + 512]
                    uu = UU[:, st:st + 512]
                    ia = IA[:, st:st + 512]
                    if step == 0:
                        uwl = None
                        if sp_ == 0:
                            uwl = [sc_ms]
                        p.op("V", lambda v, lf=lf, uu=uu, so=so:
                             v.tensor_scalar(
                                 uu, lf, stats[:, so + 6:so + 7], 0.5,
                                 op0=ALU.mult, op1=ALU.add),
                             reads=(("LN3F", st, st + 512),
                                    ("stats", so + 6, so + 7)),
                             writes=(("UU", st, st + 512),), wait=uwl)
                    elif step == 1:
                        p.op("V", lambda v, uu=uu, ia=ia:
                             v.tensor_copy(ia, uu),
                             reads=(("UU", st, st + 512),),
                             writes=(("IA", st, st + 512),))
                    elif step == 2:
                        p.op("V", lambda v, uu=uu, ia=ia: v.tensor_tensor(
                            uu, uu, ia, op=ALU.subtract),
                            reads=sl(st), writes=(("UU", st, st + 512),))
                    elif step == 3:
                        p.op("V", lambda v, uu=uu, ia=ia:
                             v.scalar_tensor_tensor(
                                 ia, uu, 0.0, ia, op0=ALU.is_ge,
                                 op1=ALU.add),
                             reads=sl(st), writes=(("IA", st, st + 512),))
                    else:
                        xwl = None
                        if sp_ == 0:
                            prev_key = (n, 0) if jj == 1 else (n - 1, 1)
                            if prev_key in xq_evs:
                                xwl = [xq_evs[prev_key]]
                        xq_ms = p.op("V", lambda v, ia=ia, st=st, so=so:
                                     v.tensor_scalar(
                                         XQTM[:, st:st + 512], ia, -1.0,
                                         stats[:, so + 7:so + 8],
                                         op0=ALU.add, op1=ALU.mult),
                                     reads=(("IA", st, st + 512),
                                            ("stats", so + 7, so + 8)),
                                     writes=(("XQTM", st, st + 512),),
                                     wait=xwl, inc=(sp_ == 1))
            ln3f_free[0] = xq_ms
            bk = "A" if jj == 0 else "B"
            bank = ptA if jj == 0 else ptB
            wl2 = [xq_ms]
            if pt_free[bk] is not None:
                wl2.append(pt_free[bk])
            tms = None
            for c in range(8):
                tms = p.op("T", lambda t, bank=bank, c=c: t.transpose(
                    bank[:, c * 128:(c + 1) * 128],
                    XQTM[:, c * 128:(c + 1) * 128], IDENT[:]),
                    reads=("XQTM", "IDENT"), writes=("PT",),
                    wait=wl2 if c == 0 else None, inc=(c == 7))
            ev = p.op("S", lambda s, bank=bank, jj=jj: s.activation(
                XQFM[:].rearrange("p (c j2 nn) -> p c j2 nn", j2=2, nn=128)
                [:, :, jj, :],
                bank[:, 0:1024].rearrange("p (c nn) -> p c nn", nn=128),
                AF.Identity),
                reads=("PT",), writes=("XQFM",), wait=[tms], inc=True)
            xq_evs[(n, jj)] = ev
            pt_free[bk] = ev
            tp_evs.append(ev)

        lastev = None
        for hfc in range(32):
            si = hfc % 2
            wl = list(tp_evs) if hfc < 2 else []
            if slot_free[si] is not None:
                wl.append(slot_free[si])
            mms = None
            for c in range(8):
                mms = p.op("T", lambda t, si=si, c=c, hfc=hfc: t.matmul(
                    slot(si)[:, 0:256],
                    W1T[:, c * 4096 + hfc * 128:c * 4096 + (hfc + 1) * 128],
                    XQFM[:, c * 256:(c + 1) * 256],
                    start=(c == 0), stop=(c == 7)),
                    reads=("WBIG", "XQFM"), writes=("PS",),
                    wait=(wl or None) if c == 0 else None, inc=(c == 7))
            lastev = p.op("S", lambda s, si=si, hfc=hfc, HT=HT: s.activation(
                hmid(HT, hfc, 1, 257), slot(si)[:, 0:256], AF.Identity),
                reads=("PS",),
                writes=((HTnm[pb], hfc * 258 + 1, hfc * 258 + 257),),
                wait=[mms], inc=True)
            slot_free[si] = lastev
            evac_ms[(n, hfc)] = lastev
        mm1_ev[n] = lastev

    def emit_tail(n):
        pb = n % 2
        HT = HTS[pb]
        htn = HTnm[pb]
        htv = HT.rearrange("p (c e) -> p c e", e=258)
        X1 = X1T[:, pb * 2048:(pb + 1) * 2048]
        so = 48 * pb + 16
        lb = 48 * pb + 28

        sil_ms = {}
        add2_ms = {}
        nxtv = None
        if n + 1 < NW:
            nxtv = HTS[(n + 1) % 2].rearrange("p (c e) -> p c e", e=258)
        hl_last = None
        for c in range(32):
            if c % 8 == 0:
                g = c // 8
                if nxtv is not None:
                    p.op("V", lambda v, g=g, nxtv=nxtv: v.tensor_copy(
                        htv[:, g * 8:(g + 1) * 8, 257:258],
                        nxtv[:, g * 8:(g + 1) * 8, 1:2]),
                        reads=(HTnm[(n + 1) % 2],), writes=(htn,),
                        wait=[evac_ms[(n + 1, g * 8 + 7)]])
                    hl_last = p.op("V", lambda v, g=g, nxtv=nxtv:
                                   v.tensor_copy(
                                       nxtv[:, g * 8:(g + 1) * 8, 0:1],
                                       htv[:, g * 8:(g + 1) * 8, 256:257]),
                                   reads=(htn,),
                                   writes=(HTnm[(n + 1) % 2],),
                                   wait=[evac_ms[(n, g * 8 + 7)]],
                                   inc=(g == 3))
                elif g == 0:
                    p.op("V", lambda v: v.memset(htv[:, :, 257:258], 0.0),
                         writes=(htn,))
            s4 = c % 2
            a0 = A0B[:, s4 * 256:(s4 + 1) * 256]
            l0 = ("A0B", s4 * 256, (s4 + 1) * 256)
            swl = []
            if c == 0:
                swl.append(halo_ms[n])
            if c >= 2:
                swl.append(add2_ms[c - 2])
            t0 = p.op("S", lambda s, c=c, a0=a0, HT=HT: s.activation(
                a0, hmid(HT, c, 0, 256), AF.Identity,
                scale=CW[:, c * 3:c * 3 + 1], bias=CB[:, c:c + 1]),
                reads=((htn, c * 258, c * 258 + 256), "CW", "CB"),
                writes=(l0,), wait=swl or None, inc=True)
            p.op("V", lambda v, c=c, a0=a0, HT=HT: v.scalar_tensor_tensor(
                a0, hmid(HT, c, 1, 257), CW[:, c * 3 + 1:c * 3 + 2], a0,
                op0=ALU.mult, op1=ALU.add),
                reads=((htn, c * 258 + 1, c * 258 + 257), "CW", l0),
                writes=(l0,), wait=[t0])
            a_ms = p.op("V", lambda v, c=c, a0=a0, HT=HT:
                        v.scalar_tensor_tensor(
                            a0, hmid(HT, c, 2, 258),
                            CW[:, c * 3 + 2:c * 3 + 3], a0,
                            op0=ALU.mult, op1=ALU.add),
                        reads=((htn, c * 258 + 2, c * 258 + 258), "CW", l0),
                        writes=(l0,), inc=True)
            add2_ms[c] = a_ms
            sil_ms[c] = p.op("S", lambda s, c=c, a0=a0, HT=HT: s.activation(
                hmid(HT, c, 1, 257), a0, AF.Silu),
                reads=(l0,),
                writes=((htn, c * 258 + 1, c * 258 + 257),),
                wait=[a_ms], inc=True)

        if nxtv is not None:
            halo_ms[n + 1] = hl_last

        tree_ms = {}
        for pr in range(16):
            s2 = pr % 2
            abp = ABP[:, s2 * 512:(s2 + 1) * 512]
            lab = ("ABQ", s2 * 512, (s2 + 1) * 512)
            awl = []
            if pr >= 2:
                awl.append(tree_ms[pr - 2])
            elif m2done.get(n - 1) is not None:
                awl.append(m2done[n - 1])
            ab = p.op("S", lambda s, pr=pr, abp=abp: s.activation(
                abp, htv[:, 2 * pr:2 * pr + 2, 1:257], AF.Abs),
                reads=((htn, 2 * pr * 258, (2 * pr + 2) * 258),),
                writes=(lab,), wait=awl or None, inc=True)
            lane = pr % 2
            dst = RMA if lane == 0 else RMB
            dn = "RMA" if lane == 0 else "RMB"
            if pr < 2:
                tree_ms[pr] = p.op("V", lambda v, dst=dst, abp=abp:
                                   v.tensor_copy(dst[:], abp),
                                   reads=(lab,), writes=(dn,),
                                   wait=[ab], inc=True)
            else:
                tree_ms[pr] = p.op("V", lambda v, dst=dst, abp=abp:
                                   v.tensor_tensor(dst[:], dst[:], abp,
                                                   op=ALU.max),
                                   reads=(lab, dn), writes=(dn,),
                                   wait=[ab], inc=True)
        p.op("V", lambda v: v.tensor_tensor(RMA[:], RMA[:], RMB[:],
                                            op=ALU.max),
             reads=("RMA", "RMB"), writes=("RMA",),
             wait=[tree_ms[14], tree_ms[15]])
        rb = p.op("V", lambda v: v.tensor_tensor(
            RMF[:], RMA[:, 0:256], RMA[:, 256:512], op=ALU.max),
            reads=("RMA",), writes=(("RMB", 0, 256),), inc=True)

        wl2 = [rb]
        if pt_free["A"] is not None:
            wl2.append(pt_free["A"])
        tms = None
        for jj in range(2):
            tms = p.op("T", lambda t, jj=jj: t.transpose(
                ptA[:, jj * 128:(jj + 1) * 128],
                RMF[:, jj * 128:(jj + 1) * 128], IDENT[:]),
                reads=(("RMB", 0, 256), "IDENT"), writes=("PT",),
                wait=wl2 if jj == 0 else None, inc=(jj == 1))
        mxc = p.op("V", lambda v, so=so: v.reduce_max(
            stats[:, so:so + 2].rearrange("p (j2 one) -> p j2 one", one=1),
            ptA[:, 0:256].rearrange("p (j2 nn) -> p j2 nn", nn=128),
            axis=AX.X),
            reads=("PT",), writes=(("stats", so, so + 2),), wait=[tms],
            inc=True)
        pt_free["A"] = mxc
        clip = p.op("V", lambda v, so=so: v.tensor_scalar_max(
            stats[:, so + 2:so + 4], stats[:, so:so + 2], 1e-5),
            reads=(("stats", so, so + 2),),
            writes=(("stats", so + 2, so + 4),), inc=True)
        p.op("P", lambda g, so=so: g.tensor_tensor(
            stats[:, so + 4:so + 6], stats[:, so + 2:so + 4],
            NEGH[:, 19:21], op=ALU.pow),
            reads=(("stats", so + 2, so + 4), ("NEGH", 19, 21)),
            writes=(("stats", so + 4, so + 6),), wait=[clip])
        s127p = p.op("P", lambda g, so=so: g.tensor_tensor(
            stats[:, so + 6:so + 8], stats[:, so + 4:so + 6],
            NEGH[:, 21:23], op=ALU.mult),
            reads=(("stats", so + 4, so + 6), ("NEGH", 21, 23)),
            writes=(("stats", so + 6, so + 8),), inc=True)
        p.op("P", lambda g, so=so: g.tensor_tensor(
            stats[:, so + 8:so + 10], stats[:, so + 2:so + 4],
            NEGH[:, 23:25], op=ALU.mult),
            reads=(("stats", so + 2, so + 4), ("NEGH", 23, 25)),
            writes=(("stats", so + 8, so + 10),))
        c2a = p.op("P", lambda g, so=so: g.tensor_tensor(
            stats[:, so + 10:so + 11], stats[:, so + 8:so + 9],
            WSC[:, 3:4], op=ALU.mult),
            reads=(("stats", so + 8, so + 9), "WSC"),
            writes=(("stats", so + 10, so + 11),), inc=True)
        c2_ms = p.op("P", lambda g, so=so: g.tensor_tensor(
            stats[:, so + 11:so + 12], stats[:, so + 9:so + 10],
            WSC[:, 3:4], op=ALU.mult),
            reads=(("stats", so + 9, so + 10), "WSC"),
            writes=(("stats", so + 11, so + 12),), inc=True)
        sb2_ms = p.op("S", lambda s, so=so: s.activation(
            SB2S[:], stats[:, so + 6:so + 8], AF.Identity),
            reads=(("stats", so + 6, so + 8),), writes=("SB2S",),
            wait=[s127p], inc=True)
        wl3 = [sb2_ms]
        if pt_free["B"] is not None:
            wl3.append(pt_free["B"])
        st2 = p.op("T", lambda t: t.transpose(ptB[0:2, 0:128], SB2S[:],
                                              IDENT[:]),
                   reads=("SB2S", "IDENT"), writes=("PT",), wait=wl3,
                   inc=True)
        sb2ev = p.op("S", lambda s: s.activation(SB2[:], ptB[0:2, 0:128],
                                                 AF.Identity),
                     reads=("PT",), writes=("SB2",), wait=[st2], inc=True)
        pt_free["B"] = sb2ev
        sr_dma = p.dma("SP", SROWB[0:1, :], SB2[:], wait=[sb2ev],
                       sem="srd")
        bwl = [sr_dma]
        if ffo_done.get(n - 1) is not None:
            bwl.append(ffo_done[n - 1])
        if slot_free[2] is not None:
            bwl.append(slot_free[2])
        p.op("T", lambda t: t.matmul(
            slot(2)[:, 0:256], ONE1[:], SROWB[0:1, :], start=True,
            stop=True),
            reads=("ONE1", "SROWB"), writes=("PS",), wait=bwl)
        bc_mm = p.op("T", lambda t: t.matmul(
            slot(2)[:, 256:512], ONE1[:], SROWB[0:1, :], start=True,
            stop=True),
            reads=("ONE1", "SROWB"), writes=("PS",), inc=True)
        sbc_ms = p.op("S", lambda s: s.activation(SBCS[:], slot(2)[:, :],
                                                  AF.Identity),
                      reads=("PS",), writes=("RMA",), wait=[bc_mm],
                      inc=True)

        m2slots = [3, 4, 5, 2]
        mm_last = {}
        hq_ms = {}
        for pg in range(8):
            for step in range(5):
                for k in range(2):
                    pr = 2 * pg + k
                    st = k * 512
                    uu = UU[:, st:st + 512]
                    ia = IA[:, st:st + 512]
                    luu = ("UU", st, st + 512)
                    lia = ("IA", st, st + 512)
                    lhq = ("ABQ", st, st + 512)
                    if step == 0:
                        wl = [sil_ms[2 * pr + 1]]
                        if pr == 0:
                            wl.append(sbc_ms)
                        p.op("V", lambda v, pr=pr, uu=uu: v.tensor_tensor(
                            uu, htv[:, 2 * pr:2 * pr + 2, 1:257], SBCS[:],
                            op=ALU.mult),
                            reads=((htn, 2 * pr * 258, (2 * pr + 2) * 258),
                                   "RMA"),
                            writes=(luu,), wait=wl)
                    elif step == 1:
                        p.op("V", lambda v, uu=uu, ia=ia: v.tensor_scalar(
                            ia, uu, 0.5, None, op0=ALU.add),
                            reads=(luu,), writes=(lia,))
                    elif step == 2:
                        p.op("V", lambda v, uu=uu, ia=ia:
                             v.scalar_tensor_tensor(
                                 uu, uu, 0.5, ia, op0=ALU.add,
                                 op1=ALU.subtract),
                             reads=(luu, lia), writes=(luu,))
                    elif step == 3:
                        p.op("V", lambda v, uu=uu, ia=ia:
                             v.scalar_tensor_tensor(
                                 ia, uu, 0.0, ia, op0=ALU.is_ge,
                                 op1=ALU.add),
                             reads=(luu, lia), writes=(lia,))
                    else:
                        hwl = [mm_last[pr - 2]] if pr >= 2 else None
                        hq_ms[pr] = p.op("V", lambda v, ia=ia, st=st:
                                         v.tensor_scalar(
                                             HQ[:, st:st + 512], ia, -1.0,
                                             None, op0=ALU.add),
                                         reads=(lia,), writes=(lhq,),
                                         wait=hwl, inc=True)
            for k in range(2):
                pr = 2 * pg + k
                st = k * 512
                lhq = ("ABQ", st, st + 512)
                last = None
                for ci in range(2):
                    c = 2 * pr + ci
                    for jj2 in range(2):
                        for g_i in range(2):
                            si = m2slots[jj2 * 2 + g_i]
                            wlm = None
                            if ci == 0 and jj2 == 0 and g_i == 0:
                                wlm = [hq_ms[pr]]
                                if c == 0:
                                    if ffo_done.get(n - 1) is not None:
                                        wlm.append(ffo_done[n - 1])
                                    for s_ in (3, 4, 5):
                                        if slot_free[s_] is not None:
                                            wlm.append(slot_free[s_])
                            if c == 0 and si == 2 and wlm is None:
                                wlm = [sbc_ms]
                            last = p.op("T", lambda t, si=si, c=c, jj2=jj2,
                                        g_i=g_i, st=st: t.matmul(
                                            slot(si),
                                            HQ[:, st + (c % 2) * 256 +
                                               jj2 * 128:
                                               st + (c % 2) * 256 +
                                               (jj2 + 1) * 128],
                                            W2T[:, c * 1024 + g_i * 512:
                                                c * 1024 + (g_i + 1) * 512],
                                            start=(c == 0), stop=(c == 31)),
                                        reads=(lhq, "WBIG"), writes=("PS",),
                                        wait=wlm,
                                        inc=(ci == 1 and jj2 == 1
                                             and g_i == 1))
                mm_last[pr] = last
        m2done[n] = mm_last[15]

        prev_out = None
        out_ms = []
        for jj2 in range(2):
            l4 = lb + 4 * jj2
            evs = []
            for g_i in range(2):
                si = m2slots[jj2 * 2 + g_i]
                evw = [mm_last[15], c2_ms]
                if jj2 == 0 and g_i == 0 and ffo_done.get(n - 1) is not None:
                    evw.append(ffo_done[n - 1])
                if jj2 == 1 and g_i == 0:
                    evw.append(prev_out)
                ev = p.op("S", lambda s, si=si, g_i=g_i, jj2=jj2, so=so:
                          s.activation(
                              FFO[:, g_i * 512:(g_i + 1) * 512], slot(si),
                              AF.Identity,
                              scale=stats[:, so + 10 + jj2:so + 11 + jj2]),
                          reads=("PS", ("stats", so + 10, so + 12)),
                          writes=(("FFO", g_i * 512, (g_i + 1) * 512),),
                          wait=evw, inc=True)
                evs.append(ev)
                slot_free[si] = ev
            p.op("V", lambda v: v.bn_stats(bn6[:, 0:6], FFO[:, 0:512]),
                 reads=(("FFO", 0, 512),), writes=(("bn6", 0, 6),),
                 wait=evs)
            p.op("V", lambda v: v.bn_stats(bn6[:, 6:12], FFO[:, 512:1024]),
                 reads=(("FFO", 512, 1024),), writes=(("bn6", 6, 12),))
            p.op("V", lambda v: v.bn_aggr(bn2[:], bn6[:]),
                 reads=("bn6",), writes=("bn2",))
            ve = p.op("V", lambda v, l4=l4: v.tensor_scalar(
                stats[:, l4:l4 + 1], bn2[:, 1:2], EPS, None, op0=ALU.add),
                reads=("bn2",), writes=(("stats", l4, l4 + 1),), inc=True)
            pw = p.op("P", lambda g, l4=l4: g.tensor_tensor(
                stats[:, l4 + 1:l4 + 2], stats[:, l4:l4 + 1],
                NEGH[:, 0:1], op=ALU.pow),
                reads=(("stats", l4, l4 + 1), ("NEGH", 0, 1)),
                writes=(("stats", l4 + 1, l4 + 2),), wait=[ve], inc=True)
            bc = p.op("V", lambda v, l4=l4: v.scalar_tensor_tensor(
                stats[:, l4 + 2:l4 + 3], bn2[:, 0:1], -1.0,
                stats[:, l4 + 1:l4 + 2], op0=ALU.mult, op1=ALU.mult),
                reads=("bn2", ("stats", l4 + 1, l4 + 2)),
                writes=(("stats", l4 + 2, l4 + 3),), wait=[pw], inc=True)
            lnm = p.op("S", lambda s, l4=l4: s.activation(
                FFO[:], FFO[:], AF.Identity,
                bias=stats[:, l4 + 2:l4 + 3],
                scale=stats[:, l4 + 1:l4 + 2]),
                reads=("FFO", ("stats", l4, l4 + 3)), writes=("FFO",),
                wait=[bc], inc=True)
            om = p.op("V", lambda v, jj2=jj2, X1=X1: v.tensor_tensor(
                X1[:, jj2 * 1024:(jj2 + 1) * 1024],
                X1[:, jj2 * 1024:(jj2 + 1) * 1024], FFO[:], op=ALU.add),
                reads=(("X1T", pb * 2048 + jj2 * 1024,
                        pb * 2048 + (jj2 + 1) * 1024), "FFO"),
                writes=(("X1T", pb * 2048 + jj2 * 1024,
                         pb * 2048 + (jj2 + 1) * 1024),),
                wait=[lnm], inc=True)
            out_ms.append(om)
            prev_out = om
        ffo_done[n] = out_ms[-1]
        win_out_dma[n] = p.dma(
            "SP", dy[n * 256:(n + 1) * 256, :].rearrange(
                "(jj pp) nn -> pp jj nn", pp=128),
            X1.rearrange("p (jj nn) -> p jj nn", nn=1024),
            wait=out_ms, sem="od")

    emit_mm1(0)
    for n in range(1, NW):
        emit_mm1(n)
        emit_tail(n - 1)
    emit_tail(NW - 1)

    p.op("SP", lambda s, v=p.e["SP"].dma_val: s.wait_ge(sems["SPd"], v))
    p.op("P", lambda g, v=p.e["P"].dma_val: g.wait_ge(sems["Pd"], v))
    for k in ("xd0", "xd1", "td", "x1d", "wd0", "wd1", "cwd", "srd", "od"):
        p.op("SP", lambda sp, k=k, v=p.dsem.get(k, 0): sp.wait_ge(
            sems[k], v))

    with nc.Block() as blk3:
        p.run(blk3)
    ctx3.close()

    return nc


def kernel(L0_attn_x, L0_attn_Wq, L0_attn_Wk, L0_attn_Wv, L0_attn_Wg,
           L0_attn_Wo, L0_attn_g_norm_w, L0_ffn_w1, L0_ffn_conv_w,
           L0_ffn_conv_b, L0_ffn_w2):
    x = np.ascontiguousarray(np.asarray(L0_attn_x, dtype=np.float32))
    B, T, D_ = x.shape
    assert (B, D_) == (NCORES, D)

    consts = make_consts(T)
    nc = build_program(T)

    common = {
        "wq": np.ascontiguousarray(L0_attn_Wq, np.float32),
        "wk": np.ascontiguousarray(L0_attn_Wk, np.float32),
        "wv": np.ascontiguousarray(L0_attn_Wv, np.float32),
        "wg": np.ascontiguousarray(L0_attn_Wg, np.float32),
        "wo": np.ascontiguousarray(L0_attn_Wo, np.float32),
        "gnw": np.ascontiguousarray(
            np.asarray(L0_attn_g_norm_w, np.float32).reshape(DV, 1)),
        "w1": np.ascontiguousarray(L0_ffn_w1, np.float32),
        "cw": np.ascontiguousarray(
            np.asarray(L0_ffn_conv_w, np.float32).reshape(DH, 3)),
        "cb": np.ascontiguousarray(
            np.asarray(L0_ffn_conv_b, np.float32).reshape(DH, 1)),
        "w2": np.ascontiguousarray(L0_ffn_w2, np.float32),
        **consts,
    }
    in_maps = []
    for i in range(NCORES):
        m = dict(common)
        m["x"] = np.ascontiguousarray(x[i])
        in_maps.append(m)

    res = run_bass_kernel_spmd(nc, in_maps, core_ids=list(range(NCORES)))
    out = np.stack([np.asarray(r["y"], np.float32) for r in res.results])
    return out


# version 06
# speedup vs baseline: 1.0696x; 1.0210x over previous
from contextlib import ExitStack

import numpy as np
import ml_dtypes
import concourse.bass as bass
import concourse.mybir as mybir
from concourse.bass_utils import run_bass_kernel_spmd

F32 = mybir.dt.float32
BF16 = mybir.dt.bfloat16
I32 = mybir.dt.int32
AF = mybir.ActivationFunctionType
ALU = mybir.AluOpType
AX = mybir.AxisListType

D = 1024
H = 16
DK = 64
DV = 128
C = 128
DH = 4096
EPS = 1e-5
NCORES = 8

ENGS = ("T", "V", "S", "P", "SP")


class MS:
    __slots__ = ("eng", "val")

    def __init__(self, eng, val):
        self.eng, self.val = eng, val


class _EngState:
    def __init__(self, name):
        self.name = name
        self.ops = []
        self.sem_val = 0
        self.dma_val = 0
        self.pend_w = []
        self.pend_r = []


def _ivl(label):
    if isinstance(label, tuple):
        return label
    return (label, None, None)


def _conflict(a, b):
    if a[0] != b[0]:
        return False
    if a[1] is None or b[1] is None:
        return True
    return a[1] < b[2] and b[1] < a[2]


class Plan:
    def __init__(self, nc, sems, carry=None):
        self.nc = nc
        self.sems = sems
        self.e = {n: _EngState(n) for n in ENGS}
        self.dsem = {}
        if carry is not None:
            for n in ENGS:
                self.e[n].sem_val = carry.e[n].sem_val
                self.e[n].dma_val = carry.e[n].dma_val
            self.dsem = dict(carry.dsem)

    def _emit(self, eng, fn):
        self.e[eng].ops.append(fn)

    def _auto_drain(self, eng, reads, writes):
        st = self.e[eng]
        rds = [_ivl(x) for x in reads]
        wrs = [_ivl(x) for x in writes]
        hit = any(_conflict(w, r) for w in st.pend_w for r in rds) \
            or any(_conflict(w, v) for w in st.pend_w for v in wrs) \
            or any(_conflict(r, v) for r in st.pend_r for v in wrs)
        if hit:
            self._emit(eng, lambda obj: obj.drain())
            st.pend_w = []
            st.pend_r = []
        if len(st.pend_w) > 96:
            st.pend_w = [(n, None, None) for n in
                         {t[0] for t in st.pend_w}]
        if len(st.pend_r) > 96:
            st.pend_r = [(n, None, None) for n in
                         {t[0] for t in st.pend_r}]
        st.pend_w += wrs
        st.pend_r += rds

    def wait(self, eng, ms):
        if ms is None:
            return
        if isinstance(ms, MS):
            ms = [ms]
        for m in ms:
            if m is None:
                continue
            sem = self.sems[m.eng]
            val = m.val
            if m.eng in ("SPd", "Pd"):
                val = self.e[m.eng[:-1]].dma_val
            self._emit(eng, lambda obj, s=sem, v=val: obj.wait_ge(s, v))

    def op(self, eng, fn, reads=(), writes=(), wait=None, inc=False,
           nodrain=False):
        self.wait(eng, wait)
        if not nodrain:
            self._auto_drain(eng, reads, writes)
        st = self.e[eng]
        if inc:
            st.sem_val += 1
            sem = self.sems[eng]

            def wrapped(obj, fn=fn, sem=sem):
                fn(obj).then_inc(sem, 1)
            self._emit(eng, wrapped)
            return MS(eng, st.sem_val)
        self._emit(eng, lambda obj, fn=fn: fn(obj))
        return None

    def dma(self, eng, out_ap, in_ap, wait=None, slow=False, sem=None):
        self.wait(eng, wait)
        key = sem if sem is not None else eng + "d"
        if key in (eng + "d",):
            self.e[eng].dma_val += 16
            val = self.e[eng].dma_val
        else:
            self.dsem[key] = self.dsem.get(key, 0) + 16
            val = self.dsem[key]
        h = self.sems[key]
        self._emit(eng, lambda obj, o=out_ap, i=in_ap, s=h, sl=slow:
                   obj.dma_start(out=o, in_=i,
                                 allow_slow_non_contiguous=sl)
                   .then_inc(s, 16))
        return MS(key, val)

    def drain(self, eng):
        self._emit(eng, lambda obj: obj.drain())
        self.e[eng].pend_w = []
        self.e[eng].pend_r = []

    def barrier_ms(self, eng):
        st = self.e[eng]
        st.sem_val += 1
        sem = self.sems[eng]
        self._emit(eng, lambda obj, s=sem: obj.sem_inc(s, 1))
        return MS(eng, st.sem_val)

    def run(self, block):
        def mk(name):
            def body(obj):
                for fn in self.e[name].ops:
                    fn(obj)
            return body
        block.tensor(mk("T"))
        block.vector(mk("V"))
        block.scalar(mk("S"))
        block.gpsimd(mk("P"))
        block.sync(mk("SP"))


def make_consts(T):
    h = np.arange(H, dtype=np.float64)
    gamma = 1.0 - np.exp2(-5.0 - h)
    j = np.arange(C, dtype=np.float64)

    inv = 1.0 / 10000.0 ** (np.arange(0, DK, 2, dtype=np.float64) / DK)
    t = np.arange(T, dtype=np.float64)
    f = np.outer(t, inv)
    costab = np.tile(np.cos(f), (1, H)).astype(np.float32)
    sintab = np.tile(np.sin(f), (1, H)).astype(np.float32)

    dkve = np.repeat(gamma[None, :] ** (C - 1 - j[:, None]), DK,
                     axis=1).astype(np.float32)
    i_ = np.arange(C, dtype=np.float64)
    dq2 = (np.repeat(gamma[None, :] ** (i_[:, None] + 1 - C), DV, axis=1)
           / 8.0).astype(np.float32)
    gct = np.ascontiguousarray(
        np.repeat(np.broadcast_to(gamma[None, :] ** C, (DK, H)), DV,
                  axis=1)).astype(np.float32)
    m = (np.arange(C)[None, :] >= np.arange(C)[:, None]).astype(np.float32)
    mask2 = np.ascontiguousarray(np.concatenate([m, m], axis=1))
    ident = np.eye(128, dtype=ml_dtypes.bfloat16)
    ones1 = np.ones((1, 128), dtype=ml_dtypes.bfloat16)
    return {"costab": costab, "sintab": sintab, "dkve": dkve, "dq2": dq2,
            "gct": gct, "mask2": mask2, "ident": ident, "ones1": ones1}


def build_program(T=4096):
    assert T % 256 == 0
    NB = T // 128
    NW = T // 256

    nc = bass.Bass("TRN2", target_bir_lowering=False)

    din = {}
    for nm, shape, dt in [
            ("x", [T, D], F32), ("wq", [D, D], F32), ("wk", [D, D], F32),
            ("wv", [D, 2 * D], F32), ("wg", [D, 2 * D], F32),
            ("wo", [2 * D, D], F32), ("gnw", [DV, 1], F32),
            ("w1", [DH, D], F32), ("cw", [DH, 3], F32), ("cb", [DH, 1], F32),
            ("w2", [D, DH], F32), ("costab", [T, 512], F32),
            ("sintab", [T, 512], F32), ("dkve", [128, 1024], F32),
            ("dq2", [128, 2048], F32), ("gct", [64, 2048], F32),
            ("mask2", [128, 256], F32), ("ident", [128, 128], BF16),
            ("ones1", [1, 128], BF16)]:
        din[nm] = nc.dram_tensor(nm, shape, dt, kind="ExternalInput").ap()

    dy = nc.dram_tensor("y", [T, D], F32, kind="ExternalOutput").ap()
    dx1 = nc.dram_tensor("x1s", [T, D], F32, kind="Internal").ap()
    dw1t = nc.dram_tensor("w1t", [D, DH], BF16, kind="Internal").ap()
    dw2t = nc.dram_tensor("w2t", [DH, D], BF16, kind="Internal").ap()

    psb = lambda nm, shape, dt=F32: nc.alloc_sbuf_tensor(
        nm, list(shape), dt).ap()
    WBIG = psb("WBIG", (128, 65536), BF16)
    WQ = WBIG[:, 0:8192]
    WK = WBIG[:, 8192:16384]
    WV = WBIG[:, 16384:32768]
    WG = WBIG[:, 32768:49152]
    WO = WBIG[:, 49152:65536]
    W1T = WBIG[:, 0:32768]
    W2T = WBIG[:, 32768:65536]

    IDENT = psb("IDENT", (128, 128), BF16)
    ONE1 = psb("ONE1", (1, 128), BF16)
    ONE1F = psb("ONE1F", (1, 128))
    NEGH = psb("NEGH", (128, 26))
    stats = psb("stats", (128, 96))
    bn6 = psb("bn6", (128, 12))
    bn2 = psb("bn2", (128, 2))
    WSC = psb("WSC", (128, 8))
    CW = psb("CW", (128, 96))
    CB = psb("CB", (128, 32))

    PS = nc.alloc_psum_tensor("PS", [128, 3072], F32).ap()
    PT = nc.alloc_psum_tensor("PT", [128, 2048], BF16).ap()
    ptA = PT[:, 0:1024]
    ptB = PT[:, 1024:2048]

    def slot(i):
        return PS[:, 512 * i:512 * (i + 1)]

    sems = {}
    for n in ["T", "V", "S", "P", "SP", "SPd", "Pd", "xd0", "xd1", "td",
              "x1d", "wd0", "wd1", "cwd", "srd", "od"]:
        sems[n] = nc.semaphore("sem_" + n).__enter__()

    ctx1 = ExitStack()
    s1 = lambda nm, shape, dt=F32: ctx1.enter_context(
        nc.sbuf_tensor(nm, list(shape), dt)).ap()

    xt = s1("xt", (128, 2 * D))
    ln1 = s1("ln1", (128, D), BF16)
    ln1fm = ln1
    qrt = s1("qrt", (128, D), BF16)
    krt = s1("krt", (128, D), BF16)
    ropem = s1("ropem", (128, 1024), BF16)
    qfm = s1("qfm", (64, 2048), BF16)
    kfm = s1("kfm", (64, 2048), BF16)
    vtm = s1("vtm", (128, 2048), BF16)
    gsil = s1("gsil", (128, 2048), BF16)
    Ab = s1("Ab", (128, 512), BF16)
    Sst = s1("Sst", (64, 2048))
    Sbb = s1("Sbb", (64, 2048), BF16)
    otm = s1("otm", (128, 2048), BF16)
    og = s1("og", (128, 2048), BF16)
    ogfm = og
    ln2b = s1("ln2b", (128, D), BF16)
    x1t = s1("x1t", (128, D))
    ctab = s1("ctab", (128, 512))
    stab = s1("stab", (128, 512))
    DKVEB = s1("DKVEB", (128, 1024), BF16)
    DQ2B = s1("DQ2B", (128, 2048), BF16)
    GCTB = s1("GCTB", (64, 2048), BF16)
    MASKB = s1("MASKB", (128, 256), BF16)
    GNCOL = s1("GNCOL", (128, 1))

    p = Plan(nc, sems)

    p.dma("SP", IDENT, din["ident"])
    p.dma("SP", ONE1, din["ones1"])
    p.dma("SP", GNCOL, din["gnw"])
    p.dma("SP", xt[:, 0:1024], din["dkve"])
    p.dma("SP", ctab[:, 0:256], din["mask2"])
    p.dma("SP", xt[:, 1024:2048], din["dq2"][:, 0:1024])
    p.dma("SP", x1t[:, 0:1024], din["dq2"][:, 1024:2048])
    call = p.dma("SP", Sst[0:64, 0:2048], din["gct"])
    p.op("V", lambda v: v.tensor_copy(DKVEB[:], xt[:, 0:1024]),
         reads=("xt",), writes=("DKVEB",), wait=[call])
    p.op("V", lambda v: v.tensor_copy(MASKB[:], ctab[:, 0:256]),
         reads=("ctab",), writes=("MASKB",))
    p.op("V", lambda v: v.tensor_copy(DQ2B[:, 0:1024], xt[:, 1024:2048]),
         reads=("xt",), writes=("DQ2B",))
    p.op("V", lambda v: v.tensor_copy(DQ2B[:, 1024:2048], x1t[:, 0:1024]),
         reads=("x1t",), writes=("DQ2B",))
    cast4 = p.op("V", lambda v: v.tensor_copy(GCTB[:], Sst[0:64, 0:2048]),
                 reads=("Sst",), writes=("GCTB",), inc=True)
    consts_done = cast4

    wms = []
    for dst, src, w in [(WQ, din["wq"], 1024), (WK, din["wk"], 1024),
                        (WV, din["wv"], 2048), (WG, din["wg"], 2048),
                        (WO, din["wo"], 1024)]:
        wms.append(p.dma("P", dst.rearrange("p (c n) -> p c n", n=w),
                         src.rearrange("(c p) n -> p c n", p=128)))
    p.op("V", lambda v: v.memset(NEGH[:, 0:16], -0.5),
         writes=(("NEGH", 0, 16),))
    p.op("V", lambda v: v.memset(NEGH[:, 16:17], -1.0),
         writes=(("NEGH", 16, 17),))
    p.op("V", lambda v: v.memset(NEGH[:, 17:18], 127.0),
         writes=(("NEGH", 17, 18),))
    p.op("V", lambda v: v.memset(NEGH[:, 18:19], 1.0 / 127.0),
         writes=(("NEGH", 18, 19),))
    p.op("V", lambda v: v.memset(NEGH[:, 19:21], -1.0),
         writes=(("NEGH", 19, 21),))
    p.op("V", lambda v: v.memset(NEGH[:, 21:23], 127.0),
         writes=(("NEGH", 21, 23),))
    p.op("V", lambda v: v.memset(NEGH[:, 23:25], 1.0 / 127.0),
         writes=(("NEGH", 23, 25),))
    p.op("V", lambda v: v.memset(Sst[:], 0.0), writes=("Sst",),
         wait=[consts_done])
    p.op("V", lambda v: v.memset(ONE1F[:], 1.0), writes=("ONE1F",))
    wo_sc_ms = None
    for c in range(16):
        wo_sc_ms = p.op("V", lambda v, c=c: v.tensor_scalar_mul(
            WO[:, c * 1024:(c + 1) * 1024], WO[:, c * 1024:(c + 1) * 1024],
            GNCOL[:, 0:1]),
            reads=("WBIG", "GNCOL"), writes=("WBIG",),
            wait=[wms[4], consts_done] if c == 0 else None,
            inc=(c == 15))
    wready = wms[:4] + [wo_sc_ms, consts_done]

    slot_free = [None] * 6
    pt_free = {"A": None, "B": None}
    blk_end_V = {}
    x1_dma = {}
    S_done = {}
    ab_free = {}
    rope_done = {}

    for b in range(NB):
        pb = b % 2
        so = 32 * pb
        xb = xt[:, pb * D:(pb + 1) * D]
        x1b = x1t[:]
        ctb = ctab[:]
        stb = stab[:]

        w = [blk_end_V[b - 2]] if b >= 2 else ([consts_done] if b == 0
                                               else None)
        x_ms = p.dma("SP", xb, din["x"][b * 128:(b + 1) * 128, :], wait=w,
                     sem="xd%d" % pb)
        tw0 = [rope_done[b - 1]] if b >= 1 else None
        ct_ms = p.dma("SP", ctb, din["costab"][b * 128:(b + 1) * 128, :],
                      wait=tw0, sem="td")
        st_ms = p.dma("SP", stb, din["sintab"][b * 128:(b + 1) * 128, :],
                      sem="td")

        p.op("V", lambda v, xb=xb: v.bn_stats(bn6[:, 0:6], xb[:, 0:512]),
             reads=("xt",), writes=("bn6",), wait=[x_ms])
        p.op("V", lambda v, xb=xb: v.bn_stats(bn6[:, 6:12], xb[:, 512:1024]),
             reads=("xt",), writes=("bn6",))
        p.op("V", lambda v: v.bn_aggr(bn2[:], bn6[:]),
             reads=("bn6",), writes=("bn2",))
        ve = p.op("V", lambda v, so=so: v.tensor_scalar(
            stats[:, so:so + 1], bn2[:, 1:2], EPS, None, op0=ALU.add),
            reads=("bn2",), writes=("stats",), inc=True)
        pw = p.op("P", lambda g, so=so: g.tensor_tensor(
            stats[:, so + 1:so + 2], stats[:, so:so + 1], NEGH[:, 0:1],
            op=ALU.pow), wait=[ve], inc=True)
        bc = p.op("V", lambda v, so=so: v.scalar_tensor_tensor(
            stats[:, so + 2:so + 3], bn2[:, 0:1], -1.0,
            stats[:, so + 1:so + 2], op0=ALU.mult, op1=ALU.mult),
            reads=("bn2", "stats"), writes=("stats",), wait=[pw], inc=True)
        ln1_ms = p.op("S", lambda s, xb=xb, so=so: s.activation(
            ln1[:], xb, AF.Identity, bias=stats[:, so + 2:so + 3],
            scale=stats[:, so + 1:so + 2]),
            reads=("xt", "stats"), writes=("ln1",), wait=[bc], inc=True)

        tw = [ln1_ms]
        if pt_free["A"] is not None:
            tw.append(pt_free["A"])
        tms = None
        for c in range(8):
            tms = p.op("T", lambda t, c=c: t.transpose(
                ptA[:, c * 128:(c + 1) * 128], ln1[:, c * 128:(c + 1) * 128],
                IDENT[:]),
                reads=("ln1", "IDENT"), writes=("PT",),
                wait=tw if c == 0 else None, inc=(c == 7))
        ln1fm_ms = p.op("S", lambda s: s.activation(ln1fm[:], ptA,
                                                    AF.Identity),
                        reads=("PT",), writes=("ln1",), wait=[tms],
                        inc=True)
        pt_free["A"] = ln1fm_ms

        group_src = [(WQ, 1024, 0), (WQ, 1024, 512), (WK, 1024, 0),
                     (WK, 1024, 512), (WV, 2048, 0), (WV, 2048, 512),
                     (WV, 2048, 1024), (WV, 2048, 1536), (WG, 2048, 0),
                     (WG, 2048, 512), (WG, 2048, 1024), (WG, 2048, 1536)]
        proj_ms = {}

        def emit_group(gi):
            Wsrc, wmul, coff = group_src[gi]
            si = gi % 6
            wl = [ln1fm_ms]
            if b == 0 and gi == 0:
                wl = wl + wready
            if slot_free[si] is not None:
                wl.append(slot_free[si])
            ms = None
            for c in range(8):
                ms = p.op("T", lambda t, si=si, c=c, Wsrc=Wsrc, wmul=wmul,
                          coff=coff: t.matmul(
                              slot(si), ln1fm[:, c * 128:(c + 1) * 128],
                              Wsrc[:, c * wmul + coff:c * wmul + coff + 512],
                              start=(c == 0), stop=(c == 7)),
                          reads=("ln1", "WBIG"), writes=("PS",),
                          wait=wl if c == 0 else None, inc=(c == 7))
            proj_ms[gi] = ms
            if gi >= 4:
                if gi < 8:
                    dstn, dst, func = "vtm", vtm[:, (gi - 4) * 512:
                                                 (gi - 3) * 512], AF.Identity
                else:
                    dstn, dst, func = "gsil", gsil[:, (gi - 8) * 512:
                                                   (gi - 7) * 512], AF.Silu
                ev = p.op("S", lambda s, si=si, dst=dst, func=func:
                          s.activation(dst, slot(si), func),
                          reads=("PS",), writes=(dstn,), wait=[ms], inc=True)
                slot_free[si] = ev

        for gi in range(6):
            emit_group(gi)

        def emit_rope(psrc, dst, with_dkv, wait_ms, nm_dst):
            s1_ = psrc.rearrange("p (h two d) -> p h two d", two=2, d=32)
            x1s_, x2s_ = s1_[:, :, 0, :], s1_[:, :, 1, :]
            o1 = dst.rearrange("p (h two d) -> p h two d", two=2, d=32)
            o1a, o2a = o1[:, :, 0, :], o1[:, :, 1, :]
            cv = ctb.rearrange("p (h d) -> p h d", d=32)
            sv = stb.rearrange("p (h d) -> p h d", d=32)
            m0 = ropem[:, 0:512].rearrange("p (h d) -> p h d", d=32)
            m1 = ropem[:, 512:1024].rearrange("p (h d) -> p h d", d=32)
            dkv = DKVEB.rearrange("p (h two d) -> p h two d", two=2, d=32)
            p.op("V", lambda v: v.tensor_mul(m0, x1s_, cv),
                 reads=("PS", "ctab"), writes=("ropem",), wait=wait_ms)
            p.op("V", lambda v: v.tensor_mul(m1, x2s_, sv),
                 reads=("PS", "stab"), writes=("ropem",))
            if with_dkv:
                p.op("V", lambda v: v.tensor_sub(m0, m0, m1),
                     reads=("ropem",), writes=("ropem",))
                p.op("V", lambda v: v.tensor_mul(o1a, m0, dkv[:, :, 0, :]),
                     reads=("ropem", "DKVEB"), writes=(nm_dst,))
            else:
                p.op("V", lambda v: v.tensor_sub(o1a, m0, m1),
                     reads=("ropem",), writes=(nm_dst,))
            p.op("V", lambda v: v.tensor_mul(m0, x1s_, sv),
                 reads=("PS", "stab"), writes=("ropem",))
            p.op("V", lambda v: v.tensor_mul(m1, x2s_, cv),
                 reads=("PS", "ctab"), writes=("ropem",))
            if with_dkv:
                p.op("V", lambda v: v.tensor_add(m0, m0, m1),
                     reads=("ropem",), writes=("ropem",))
                return p.op("V", lambda v: v.tensor_mul(
                    o2a, m0, dkv[:, :, 1, :]),
                    reads=("ropem", "DKVEB"), writes=(nm_dst,), inc=True)
            return p.op("V", lambda v: v.tensor_add(o2a, m0, m1),
                        reads=("ropem",), writes=(nm_dst,), inc=True)

        rope_w_q = [proj_ms[0], proj_ms[1], st_ms]
        if b > 0:
            rope_w_q.append(S_done[b - 1])
        qr_ms = emit_rope(PS[:, 0:1024], qrt, False, rope_w_q, "qrt")
        kr_ms = emit_rope(PS[:, 1024:2048], krt, True,
                          [proj_ms[2], proj_ms[3]], "krt")
        rope_done[b] = kr_ms
        slot_free[0] = slot_free[1] = qr_ms
        slot_free[2] = slot_free[3] = kr_ms
        for gi in range(6, 12):
            emit_group(gi)
        vg_ev = slot_free[5]

        if b > 0:
            sg_v = p.op("V", lambda v: v.tensor_mul(Sst[:], Sst[:], GCTB[:]),
                        reads=("Sst", "GCTB"), writes=("Sst",),
                        wait=[S_done[b - 1]], inc=True)
            sb_ms = p.op("S", lambda s: s.activation(Sbb[:], Sst[:],
                                                     AF.Identity),
                         reads=("Sst",), writes=("Sbb",), wait=[sg_v],
                         inc=True)
        else:
            sg_v = sb_ms = None

        ev_fm = {}
        for wv_i, (src, dst, nm_s, nm_d, h0, rms_) in enumerate([
                (qrt, qfm, "qrt", "qfm", 0, None), (qrt, qfm, "qrt", "qfm", 8, None),
                (krt, kfm, "krt", "kfm", 0, None), (krt, kfm, "krt", "kfm", 8, None)]):
            bk = "A" if wv_i % 2 == 0 else "B"
            bank = ptA if bk == "A" else ptB
            wl = [qr_ms if src is qrt else kr_ms]
            if pt_free[bk] is not None:
                wl.append(pt_free[bk])
            tms = None
            for hh in range(8):
                hx = h0 + hh
                tms = p.op("T", lambda t, bank=bank, hh=hh, hx=hx, src=src:
                           t.transpose(bank[0:64, hh * 128:(hh + 1) * 128],
                                       src[:, hx * 64:(hx + 1) * 64],
                                       IDENT[:]),
                           reads=(nm_s, "IDENT"), writes=("PT",),
                           wait=wl if hh == 0 else None, inc=(hh == 7))
            ev_fm[wv_i] = p.op(
                "S", lambda s, bank=bank, dst=dst, h0=h0: s.activation(
                    dst[0:64, h0 * 128:(h0 + 8) * 128], bank[0:64, 0:1024],
                    AF.Identity),
                reads=("PT",), writes=(nm_d,), wait=[tms], inc=True)
            pt_free[bk] = ev_fm[wv_i]

        mask_ms = {}
        o_ev = {}
        for wv_i in range(4):
            si = wv_i % 2
            reg = 0
            wl = [ev_fm[1], ev_fm[3]]
            if slot_free[si] is not None:
                wl.append(slot_free[si])
            ams = None
            for hh in range(4):
                hx = wv_i * 4 + hh
                ams = p.op("T", lambda t, si=si, hh=hh, hx=hx: t.matmul(
                    slot(si)[:, hh * 128:(hh + 1) * 128],
                    kfm[0:64, hx * 128:(hx + 1) * 128],
                    qfm[0:64, hx * 128:(hx + 1) * 128],
                    start=True, stop=True),
                    reads=("kfm", "qfm"), writes=("PS",),
                    wait=wl if hh == 0 else None, inc=(hh == 3))
            mwl = [ams]
            if ab_free.get(0) is not None:
                mwl.append(ab_free[0])
            msk = None
            for pr in range(2):
                msk = p.op("V", lambda v, si=si, reg=reg, pr=pr: v.tensor_mul(
                    Ab[:, reg * 512 + pr * 256:reg * 512 + (pr + 1) * 256],
                    slot(si)[:, pr * 256:(pr + 1) * 256], MASKB[:]),
                    reads=("PS", "MASKB"), writes=("Ab",),
                    wait=mwl if pr == 0 else None, inc=(pr == 1))
            mask_ms[wv_i] = msk
            slot_free[si] = msk

            osi = 2 + wv_i
            owl = [msk, vg_ev]
            if sb_ms is not None:
                owl.append(sb_ms)
            if slot_free[osi] is not None:
                owl.append(slot_free[osi])
            oms = None
            for hh in range(4):
                hx = wv_i * 4 + hh
                oms = p.op("T", lambda t, osi=osi, hh=hh, hx=hx, reg=reg,
                           sf=(b == 0): t.matmul(
                               slot(osi)[:, hh * 128:(hh + 1) * 128],
                               Ab[:, reg * 512 + hh * 128:
                                   reg * 512 + (hh + 1) * 128],
                               vtm[:, hx * 128:(hx + 1) * 128],
                               start=True, stop=sf),
                           reads=("Ab", "vtm"), writes=("PS",),
                           wait=owl if hh == 0 else None,
                           inc=(b == 0 and hh == 3))
                if b > 0:
                    oms = p.op("T", lambda t, osi=osi, hh=hh, hx=hx: t.matmul(
                        slot(osi)[:, hh * 128:(hh + 1) * 128],
                        qfm[0:64, hx * 128:(hx + 1) * 128],
                        Sbb[0:64, hx * 128:(hx + 1) * 128],
                        start=False, stop=True),
                        reads=("qfm", "Sbb"), writes=("PS",), inc=(hh == 3))
            ab_free[0] = oms
            o_ev[wv_i] = p.op("V", lambda v, osi=osi, wv_i=wv_i: v.tensor_mul(
                otm[:, wv_i * 512:(wv_i + 1) * 512], slot(osi),
                DQ2B[:, wv_i * 512:(wv_i + 1) * 512]),
                reads=("PS", "DQ2B"), writes=("otm",), wait=[oms], inc=True)
            slot_free[osi] = o_ev[wv_i]

        kv_upd = []
        for wv_i in range(4):
            si = wv_i % 2
            wl = [kr_ms, vg_ev, slot_free[si]]
            kms = None
            for hh in range(4):
                hx = wv_i * 4 + hh
                kms = p.op("T", lambda t, si=si, hh=hh, hx=hx: t.matmul(
                    slot(si)[0:64, hh * 128:(hh + 1) * 128],
                    krt[:, hx * 64:(hx + 1) * 64],
                    vtm[:, hx * 128:(hx + 1) * 128],
                    start=True, stop=True),
                    reads=("krt", "vtm"), writes=("PS",),
                    wait=wl if hh == 0 else None, inc=(hh == 3))
            if b == 0:
                msv = p.op("V", lambda v, si=si, wv_i=wv_i: v.tensor_copy(
                    Sst[0:64, wv_i * 512:(wv_i + 1) * 512],
                    slot(si)[0:64, :]),
                    reads=("PS",), writes=("Sst",), wait=[kms], inc=True)
            else:
                msv = p.op("V", lambda v, si=si, wv_i=wv_i: v.tensor_add(
                    Sst[0:64, wv_i * 512:(wv_i + 1) * 512],
                    Sst[0:64, wv_i * 512:(wv_i + 1) * 512],
                    slot(si)[0:64, :]),
                    reads=("PS", "Sst"), writes=("Sst",),
                    wait=[kms, sg_v, sb_ms], inc=True)
            kv_upd.append(msv)
            slot_free[si] = msv
        S_done[b] = kv_upd[-1]

        p.op("V", lambda v: v.tensor_mul(og[:], otm[:], otm[:]),
             reads=("otm",), writes=("og",), wait=[o_ev[3]])
        p.op("V", lambda v, so=so: v.reduce_sum(
            stats[:, so + 8:so + 24].rearrange("p (h one) -> p h one",
                                               one=1),
            og[:].rearrange("p (h d) -> p h d", d=128), axis=AX.X),
            reads=("og",), writes=("stats",))
        ve2 = p.op("V", lambda v, so=so: v.tensor_scalar(
            stats[:, so + 8:so + 24], stats[:, so + 8:so + 24], 1.0 / DV,
            EPS, op0=ALU.mult, op1=ALU.add),
            reads=("stats",), writes=("stats",), inc=True)
        pw2 = p.op("P", lambda g, so=so: g.tensor_tensor(
            stats[:, so + 8:so + 24], stats[:, so + 8:so + 24],
            NEGH[:, 0:16], op=ALU.pow), wait=[ve2], inc=True)
        ogm = None
        for hx in range(16):
            ogm = p.op("V", lambda v, hx=hx, so=so: v.tensor_scalar_mul(
                og[:, hx * 128:(hx + 1) * 128],
                otm[:, hx * 128:(hx + 1) * 128],
                stats[:, so + 8 + hx:so + 9 + hx]),
                reads=("otm", "stats"), writes=("og",),
                wait=[pw2] if hx == 0 else None, inc=(hx == 15))
        og_ms = p.op("V", lambda v: v.tensor_mul(og[:], og[:], gsil[:]),
                     reads=("og", "gsil"), writes=("og",), inc=True)

        ogfm_ev = {}
        for wv_i in range(2):
            bk = "A" if wv_i == 0 else "B"
            bank = ptA if wv_i == 0 else ptB
            wl = [og_ms]
            if pt_free[bk] is not None:
                wl.append(pt_free[bk])
            tms = None
            for hh in range(8):
                c = wv_i * 8 + hh
                tms = p.op("T", lambda t, bank=bank, hh=hh, c=c: t.transpose(
                    bank[:, hh * 128:(hh + 1) * 128],
                    og[:, c * 128:(c + 1) * 128], IDENT[:]),
                    reads=("og", "IDENT"), writes=("PT",),
                    wait=wl if hh == 0 else None, inc=(hh == 7))
            ogfm_ev[wv_i] = p.op(
                "S", lambda s, bank=bank, wv_i=wv_i: s.activation(
                    og[:, wv_i * 1024:(wv_i + 1) * 1024], bank[:, 0:1024],
                    AF.Identity),
                reads=("PT",), writes=("og",), wait=[tms], inc=True)
            pt_free[bk] = ogfm_ev[wv_i]

        wo_t = {}
        for g_i in range(2):
            si = g_i
            wl = [ogfm_ev[1], slot_free[si]]
            wms2 = None
            for c in range(16):
                wms2 = p.op("T", lambda t, si=si, c=c, g_i=g_i: t.matmul(
                    slot(si), og[:, c * 128:(c + 1) * 128],
                    WO[:, c * 1024 + g_i * 512:c * 1024 + (g_i + 1) * 512],
                    start=(c == 0), stop=(c == 15)),
                    reads=("og", "WBIG"), writes=("PS",),
                    wait=wl if c == 0 else None, inc=(c == 15))
            wo_t[g_i] = wms2

        p.op("V", lambda v: v.bn_stats(bn6[:, 0:6], PS[:, 0:512]),
             reads=("PS",), writes=("bn6",), wait=[wo_t[0], wo_t[1]])
        p.op("V", lambda v: v.bn_stats(bn6[:, 6:12], PS[:, 512:1024]),
             reads=("PS",), writes=("bn6",))
        p.op("V", lambda v: v.bn_aggr(bn2[:], bn6[:]),
             reads=("bn6",), writes=("bn2",))
        ve3 = p.op("V", lambda v, so=so: v.tensor_scalar(
            stats[:, so + 4:so + 5], bn2[:, 1:2], EPS, None, op0=ALU.add),
            reads=("bn2",), writes=("stats",), inc=True)
        pw3 = p.op("P", lambda g, so=so: g.tensor_tensor(
            stats[:, so + 5:so + 6], stats[:, so + 4:so + 5], NEGH[:, 0:1],
            op=ALU.pow), wait=[ve3], inc=True)
        bc3 = p.op("V", lambda v, so=so: v.scalar_tensor_tensor(
            stats[:, so + 6:so + 7], bn2[:, 0:1], -1.0,
            stats[:, so + 5:so + 6], op0=ALU.mult, op1=ALU.mult),
            reads=("bn2", "stats"), writes=("stats",), wait=[pw3], inc=True)
        ln2_ms = p.op("S", lambda s, so=so: s.activation(
            ln2b[:], PS[:, 0:1024], AF.Identity,
            bias=stats[:, so + 6:so + 7], scale=stats[:, so + 5:so + 6]),
            reads=("PS", "stats"), writes=("ln2b",), wait=[bc3], inc=True)
        slot_free[0] = slot_free[1] = ln2_ms
        xw = [ln2_ms]
        if b >= 1:
            xw.append(x1_dma[b - 1])
        x1_ms = p.op("V", lambda v, x1b=x1b, xb=xb: v.tensor_add(
            x1b, xb, ln2b[:]),
            reads=("xt", "ln2b"), writes=("x1t",), wait=xw, inc=True)
        blk_end_V[b] = x1_ms
        x1_dma[b] = p.dma("SP", dx1[b * 128:(b + 1) * 128, :], x1b,
                          wait=[x1_ms], sem="x1d")

    p.op("SP", lambda s, v=p.e["SP"].dma_val: s.wait_ge(sems["SPd"], v))
    p.op("P", lambda g, v=p.e["P"].dma_val: g.wait_ge(sems["Pd"], v))
    for k in ("xd0", "xd1", "td", "x1d"):
        p.op("SP", lambda sp, k=k, v=p.dsem.get(k, 0): sp.wait_ge(
            sems[k], v))

    with nc.Block() as blk1:
        p.run(blk1)
    ctx1.close()

    ctxp = ExitStack()
    s2_ = lambda nm, shape, dt=F32: ctxp.enter_context(
        nc.sbuf_tensor(nm, list(shape), dt)).ap()
    WSTG = s2_("WSTG", (128, 2 * 4096))
    WTERN = s2_("WTERN", (128, 4096), BF16)
    WTT = s2_("WTT", (128, 4096), BF16)
    WSUM = s2_("WSUM", (128, 64))
    UUW = s2_("UUW", (128, 1024))
    G1W = s2_("G1W", (128, 1024))
    G2W = s2_("G2W", (128, 1024))

    p = Plan(nc, sems, carry=p)
    slot_free = [None] * 6
    pt_free = {"A": None, "B": None}

    sum_last2 = [None, None]
    col = 0
    for dsrc, nchunk, width in [(din["w1"], 32, 1024), (din["w2"], 8, 4096)]:
        for r in range(nchunk):
            pbuf = WSTG[:, (r % 2) * 4096:(r % 2) * 4096 + width]
            wl = [sum_last2[r % 2]] if sum_last2[r % 2] is not None else None
            dm = p.dma("P", pbuf, dsrc[r * 128:(r + 1) * 128, :], wait=wl,
                       sem="wd%d" % (r % 2))
            ms = p.op("V", lambda v, pbuf=pbuf, col=col: v.reduce_sum(
                WSUM[:, col:col + 1], pbuf, axis=AX.X,
                apply_absolute_value=True),
                reads=("WSTG",), writes=("WSUM",), wait=[dm], inc=True)
            sum_last2[r % 2] = ms
            col += 1
    p.op("V", lambda v: v.reduce_sum(WSUM[:, 40:41], WSUM[:, 0:32],
                                     axis=AX.X),
         reads=("WSUM",), writes=("WSUM",),
         wait=[sum_last2[0], sum_last2[1]])
    t12 = p.op("V", lambda v: v.reduce_sum(WSUM[:, 41:42], WSUM[:, 32:40],
                                           axis=AX.X),
               reads=("WSUM",), writes=("WSUM",), inc=True)
    wsb = p.op("V", lambda v: v.tensor_copy(WTERN[:, 0:2], WSUM[:, 40:42]),
               reads=("WSUM",), writes=("WTERN",), wait=[t12], inc=True)
    tpm = p.op("T", lambda t: t.transpose(ptA[0:2, 0:128],
                                          WTERN[:, 0:2], IDENT[:]),
               reads=("WTERN", "IDENT"), writes=("PT",), wait=[wsb],
               inc=True)
    p.op("V", lambda v: v.reduce_sum(stats[0:2, 0:1], ptA[0:2, 0:128],
                                     axis=AX.X),
         reads=("PT",), writes=("stats",), wait=[tpm])
    p.op("V", lambda v: v.tensor_scalar(stats[0:2, 1:2], stats[0:2, 0:1],
                                        1.0 / (DH * D), 1e-5,
                                        op0=ALU.mult, op1=ALU.max),
         reads=("stats",), writes=("stats",))
    rc = p.op("V", lambda v: v.reciprocal(stats[0:2, 2:3], stats[0:2, 1:2]),
              reads=("stats",), writes=("stats",), inc=True)
    rcb = p.op("V", lambda v: v.tensor_copy(WTERN[0:2, 4:6],
                                            stats[0:2, 1:3]),
               reads=("stats",), writes=("WTERN",), wait=[rc], inc=True)
    tq1 = p.op("T", lambda t: t.transpose(ptA[0:1, 0:2], WTERN[0:2, 4:5],
                                          IDENT[0:2, 0:2]),
               reads=("WTERN", "IDENT"), writes=("PT",), wait=[rcb, tpm])
    tq2 = p.op("T", lambda t: t.transpose(ptA[0:1, 2:4], WTERN[0:2, 5:6],
                                          IDENT[0:2, 0:2]),
               reads=("WTERN", "IDENT"), writes=("PT",), inc=True)
    srow = p.op("V", lambda v: v.tensor_copy(WSUM[0:1, 48:52],
                                             ptA[0:1, 0:4]),
                reads=("PT",), writes=("WSUM",), wait=[tq2], inc=True)

    def stat_bcast(src_11, dst_col, wait_ms):
        mm = p.op("T", lambda t, src_11=src_11: t.matmul(
            slot(5)[:, 0:1], ONE1F[:], src_11, start=True, stop=True),
            reads=("ONE1F", "WSUM"), writes=("PS",), wait=wait_ms, inc=True)
        return p.op("V", lambda v, dst_col=dst_col: v.tensor_copy(
            dst_col, slot(5)[:, 0:1]),
            reads=("PS",), writes=("WSC",), wait=[mm], inc=True)

    b0 = stat_bcast(WSUM[0:1, 50:51], WSC[:, 0:1], [srow])
    b1 = stat_bcast(WSUM[0:1, 48:49], WSC[:, 1:2], [b0])
    b2 = stat_bcast(WSUM[0:1, 51:52], WSC[:, 2:3], [b1])
    b3 = stat_bcast(WSUM[0:1, 49:50], WSC[:, 3:4], [b2])

    cw_dm = p.dma("P", CW[:].rearrange("p (c k) -> p c k", k=3),
                  din["cw"].rearrange("(c p) k -> p c k", p=128), sem="cwd")
    cb_dm = p.dma("P", CB[:].rearrange("p (c one) -> p c one", one=1),
                  din["cb"].rearrange("(c p) one -> p c one", p=128),
                  slow=True, sem="cwd")
    p.op("V", lambda v: v.tensor_scalar_mul(CW[:], CW[:], WSC[:, 1:2]),
         reads=("CW", "WSC"), writes=("CW",), wait=[cb_dm, b1])

    pb_free = {}
    tern_T_last = tq2
    last_store = None
    for wi, (dsrc, ddst, nchunk, width, swc) in enumerate([
            (din["w1"], dw1t, 32, 1024, WSC[:, 0:1]),
            (din["w2"], dw2t, 8, 4096, WSC[:, 2:3])]):
        for r in range(nchunk):
            pbuf = WSTG[:, (r % 2) * 4096:(r % 2) * 4096 + width]
            wl = [b3]
            prev = pb_free.get((wi, r - 2))
            if prev is None and wi == 1 and r < 2:
                prev = pb_free.get((0, 30 + r))
            if prev is not None:
                wl.append(prev)
            dm = p.dma("P", pbuf, dsrc[r * 128:(r + 1) * 128, :], wait=wl,
                       sem="wd%d" % (r % 2))
            qt_ms = None
            for st0 in range(0, width, 1024):
                wwl = [dm]
                if tern_T_last is not None and st0 == 0:
                    wwl.append(tern_T_last)
                p.op("V", lambda v, pbuf=pbuf, st0=st0, swc=swc:
                     v.tensor_scalar_mul(UUW[:], pbuf[:, st0:st0 + 1024],
                                         swc),
                     reads=("WSTG", "WSC"), writes=("UUW",), wait=wwl)
                p.op("V", lambda v: v.tensor_single_scalar(
                    G1W[:], UUW[:], 0.5, op=ALU.is_ge),
                    reads=("UUW",), writes=("G1W",))
                p.op("V", lambda v: v.tensor_single_scalar(
                    G2W[:], UUW[:], -0.5, op=ALU.is_le),
                    reads=("UUW",), writes=("G2W",))
                qt_ms = p.op("V", lambda v, st0=st0: v.tensor_sub(
                    WTERN[:, st0:st0 + 1024], G1W[:], G2W[:]),
                    reads=("G1W", "G2W"), writes=("WTERN",),
                    inc=(st0 + 1024 >= width))
            pb_free[(wi, r)] = qt_ms
            ntc = width // 128
            ev_list = []
            for w0 in range(0, ntc, 8):
                bk = "A" if (w0 // 8) % 2 == 0 else "B"
                bank = ptA if bk == "A" else ptB
                nw = min(8, ntc - w0)
                wl2 = [qt_ms]
                if pt_free[bk] is not None:
                    wl2.append(pt_free[bk])
                tms = None
                for k in range(nw):
                    c = w0 + k
                    tms = p.op("T", lambda t, bank=bank, k=k, c=c:
                               t.transpose(bank[:, k * 128:(k + 1) * 128],
                                           WTERN[:, c * 128:(c + 1) * 128],
                                           IDENT[:]),
                               reads=("WTERN", "IDENT"), writes=("PT",),
                               wait=wl2 if k == 0 else None,
                               inc=(k == nw - 1))
                ewl = [tms]
                if last_store is not None and w0 == 0:
                    ewl.append(last_store)
                ev = p.op("S", lambda s, bank=bank, w0=w0, nw=nw:
                          s.activation(WTT[:, w0 * 128:(w0 + nw) * 128],
                                       bank[:, 0:nw * 128], AF.Identity),
                          reads=("PT",), writes=("WTT",), wait=ewl, inc=True)
                ev_list.append(ev)
                pt_free[bk] = ev
            tern_T_last = tms
            for ci in range(ntc):
                last_store = p.dma(
                    "SP",
                    ddst[ci * 128:(ci + 1) * 128, r * 128:(r + 1) * 128],
                    WTT[:, ci * 128:(ci + 1) * 128],
                    wait=[ev_list[ci // 8]] if ci % 8 == 0 else None)

    p.dma("SP", W1T.rearrange("p (c n) -> p c n", n=4096),
          dw1t.rearrange("(c p) n -> p c n", p=128), wait=[last_store])
    p.dma("SP", W2T.rearrange("p (c n) -> p c n", n=1024),
          dw2t.rearrange("(c p) n -> p c n", p=128))
    p.op("SP", lambda s, v=p.e["SP"].dma_val: s.wait_ge(sems["SPd"], v))
    p.op("P", lambda g, v=p.e["P"].dma_val: g.wait_ge(sems["Pd"], v))

    with nc.Block() as blk2:
        p.run(blk2)
    ctxp.close()

    ctx3 = ExitStack()
    s3_ = lambda nm, shape, dt=F32: ctx3.enter_context(
        nc.sbuf_tensor(nm, list(shape), dt)).ap()
    X1T = s3_("X1T", (128, 2 * 2048))
    LN3F = s3_("LN3F", (128, 1024))
    XQTM = s3_("XQTM", (128, 1024), BF16)
    XQFM = s3_("XQFM", (128, 2048), BF16)
    HT0 = s3_("HT0", (128, 32 * 258), BF16)
    HT1 = s3_("HT1", (128, 32 * 258), BF16)
    A0B = s3_("A0B", (128, 2 * 256))
    ABQ = s3_("ABQ", (128, 2 * 512), BF16)
    ABP = ABQ
    HQ = ABQ
    RMA = s3_("RMA", (128, 512), BF16)
    RMB = s3_("RMB", (128, 512), BF16)
    RMF = RMB[:, 0:256]
    SB2S = s3_("SB2S", (128, 2), BF16)
    UU = s3_("UU", (128, 2 * 512))
    IA = s3_("IA", (128, 2 * 512), I32)
    RR = IA
    SBCS = RMA
    SB2 = s3_("SB2", (2, 128), BF16)
    SROWB = s3_("SROWB", (1, 256), BF16)
    FFO = s3_("FFO", (128, 1024))
    LN4B = FFO

    p = Plan(nc, sems, carry=p)
    slot_free = [None] * 6
    pt_free = {"A": None, "B": None}

    HTS = [HT0, HT1]
    HTnm = ["HT0", "HT1"]

    def hmid(ht, c, a, bcol):
        return ht[:, c * 258 + a:c * 258 + bcol]

    mm1_ev = {}
    win_out_dma = {}
    ffo_done = {}
    halo_ms = {}
    xq_evs = {}
    ln3f_free = [None]
    m2done = {}
    evac_ms = {}

    mz_ms = None
    for c in range(32):
        mz_ms = p.op("V", lambda v, c=c: v.memset(hmid(HT0, c, 0, 1), 0.0),
                     writes=("HT0",), inc=(c == 31))
    halo_ms[0] = mz_ms

    WST = {}
    m2v1_last = {}

    def emit_quant(n):
        pb = n % 2
        pb = n % 2
        X1 = X1T[:, pb * 2048:(pb + 1) * 2048]
        HT = HTS[pb]
        wl = []
        if n >= 2:
            wl.append(win_out_dma.get(n - 2))
        x_ms = p.dma("SP", X1.rearrange("p (jj nn) -> p jj nn", nn=1024),
                     dx1[n * 256:(n + 1) * 256, :].rearrange(
                         "(jj pp) nn -> pp jj nn", pp=128),
                     wait=[m for m in wl if m is not None] or None,
                     sem="xd%d" % pb)
        tp_evs = []
        for jj in range(2):
            so = 48 * pb + 8 * jj
            x1lb = ("X1T", pb * 2048 + jj * 1024, pb * 2048 + (jj + 1) * 1024)
            xb = X1[:, jj * 1024:(jj + 1) * 1024]
            p.op("V", lambda v, xb=xb: v.bn_stats(bn6[:, 0:6],
                                                  xb[:, 0:512]),
                 reads=(x1lb,), writes=(("bn6", 0, 6),), wait=[x_ms])
            p.op("V", lambda v, xb=xb: v.bn_stats(bn6[:, 6:12],
                                                  xb[:, 512:1024]),
                 reads=(x1lb,), writes=(("bn6", 6, 12),))
            p.op("V", lambda v: v.bn_aggr(bn2[:], bn6[:]),
                 reads=("bn6",), writes=("bn2",))
            ve = p.op("V", lambda v, so=so: v.tensor_scalar(
                stats[:, so:so + 1], bn2[:, 1:2], EPS, None, op0=ALU.add),
                reads=("bn2",), writes=(("stats", so, so + 1),), inc=True)
            pw = p.op("P", lambda g, so=so: g.tensor_tensor(
                stats[:, so + 1:so + 2], stats[:, so:so + 1], NEGH[:, 0:1],
                op=ALU.pow),
                reads=(("stats", so, so + 1), ("NEGH", 0, 1)),
                writes=(("stats", so + 1, so + 2),), wait=[ve], inc=True)
            bc = p.op("V", lambda v, so=so: v.scalar_tensor_tensor(
                stats[:, so + 2:so + 3], bn2[:, 0:1], -1.0,
                stats[:, so + 1:so + 2], op0=ALU.mult, op1=ALU.mult),
                reads=("bn2", ("stats", so + 1, so + 2)),
                writes=(("stats", so + 2, so + 3),), wait=[pw], inc=True)
            lnm = p.op("S", lambda s, xb=xb, so=so: s.activation(
                LN3F[:], xb, AF.Identity, bias=stats[:, so + 2:so + 3],
                scale=stats[:, so + 1:so + 2]),
                reads=(x1lb, ("stats", so, so + 3)), writes=("LN3F",),
                wait=[bc, ln3f_free[0]] if ln3f_free[0] else [bc],
                inc=True)
            mx = p.op("V", lambda v, so=so: v.reduce_max(
                stats[:, so + 3:so + 4], LN3F[:], axis=AX.X,
                apply_absolute_value=True),
                reads=("LN3F",), writes=(("stats", so + 3, so + 4),),
                wait=[lnm], inc=True)
            mxc = p.op("V", lambda v, so=so: v.tensor_scalar_max(
                stats[:, so + 4:so + 5], stats[:, so + 3:so + 4], 1e-5),
                reads=(("stats", so + 3, so + 4),),
                writes=(("stats", so + 4, so + 5),), inc=True)
            p.op("P", lambda g, so=so: g.tensor_tensor(
                stats[:, so + 5:so + 6], stats[:, so + 4:so + 5],
                NEGH[:, 16:17], op=ALU.pow),
                reads=(("stats", so + 4, so + 5), ("NEGH", 16, 17)),
                writes=(("stats", so + 5, so + 6),), wait=[mxc])
            p.op("P", lambda g, so=so: g.tensor_tensor(
                stats[:, so + 6:so + 7], stats[:, so + 5:so + 6],
                NEGH[:, 17:18], op=ALU.mult),
                reads=(("stats", so + 5, so + 6), ("NEGH", 17, 18)),
                writes=(("stats", so + 6, so + 7),))
            sc_ms = p.op("P", lambda g, so=so: g.tensor_tensor(
                stats[:, so + 7:so + 8], stats[:, so + 4:so + 5],
                NEGH[:, 18:19], op=ALU.mult),
                reads=(("stats", so + 4, so + 5), ("NEGH", 18, 19)),
                writes=(("stats", so + 7, so + 8),), inc=True)
            xq_ms = None
            sl = lambda st: (("UU", st, st + 512), ("IA", st, st + 512))
            for step in range(5):
                for sp_ in range(2):
                    st = sp_ * 512
                    lf = LN3F[:, st:st + 512]
                    uu = UU[:, st:st + 512]
                    ia = IA[:, st:st + 512]
                    if step == 0:
                        uwl = None
                        if sp_ == 0:
                            uwl = [sc_ms]
                        p.op("V", lambda v, lf=lf, uu=uu, so=so:
                             v.tensor_scalar(
                                 uu, lf, stats[:, so + 6:so + 7], 0.5,
                                 op0=ALU.mult, op1=ALU.add),
                             reads=(("LN3F", st, st + 512),
                                    ("stats", so + 6, so + 7)),
                             writes=(("UU", st, st + 512),), wait=uwl)
                    elif step == 1:
                        p.op("V", lambda v, uu=uu, ia=ia:
                             v.tensor_copy(ia, uu),
                             reads=(("UU", st, st + 512),),
                             writes=(("IA", st, st + 512),))
                    elif step == 2:
                        p.op("V", lambda v, uu=uu, ia=ia: v.tensor_tensor(
                            uu, uu, ia, op=ALU.subtract),
                            reads=sl(st), writes=(("UU", st, st + 512),))
                    elif step == 3:
                        p.op("V", lambda v, uu=uu, ia=ia:
                             v.scalar_tensor_tensor(
                                 ia, uu, 0.0, ia, op0=ALU.is_ge,
                                 op1=ALU.add),
                             reads=sl(st), writes=(("IA", st, st + 512),))
                    else:
                        xwl = None
                        if sp_ == 0:
                            prev_key = (n, 0) if jj == 1 else (n - 1, 1)
                            if prev_key in xq_evs:
                                xwl = [xq_evs[prev_key]]
                        xq_ms = p.op("V", lambda v, ia=ia, st=st, so=so:
                                     v.tensor_scalar(
                                         XQTM[:, st:st + 512], ia, -1.0,
                                         stats[:, so + 7:so + 8],
                                         op0=ALU.add, op1=ALU.mult),
                                     reads=(("IA", st, st + 512),
                                            ("stats", so + 7, so + 8)),
                                     writes=(("XQTM", st, st + 512),),
                                     wait=xwl, inc=(sp_ == 1))
            ln3f_free[0] = xq_ms
            bk = "A" if jj == 0 else "B"
            bank = ptA if jj == 0 else ptB
            wl2 = [xq_ms]
            if pt_free[bk] is not None:
                wl2.append(pt_free[bk])
            tms = None
            for c in range(8):
                tms = p.op("T", lambda t, bank=bank, c=c: t.transpose(
                    bank[:, c * 128:(c + 1) * 128],
                    XQTM[:, c * 128:(c + 1) * 128], IDENT[:]),
                    reads=("XQTM", "IDENT"), writes=("PT",),
                    wait=wl2 if c == 0 else None, inc=(c == 7))
            ev = p.op("S", lambda s, bank=bank, jj=jj: s.activation(
                XQFM[:].rearrange("p (c j2 nn) -> p c j2 nn", j2=2, nn=128)
                [:, :, jj, :],
                bank[:, 0:1024].rearrange("p (c nn) -> p c nn", nn=128),
                AF.Identity),
                reads=("PT",), writes=("XQFM",), wait=[tms], inc=True)
            xq_evs[(n, jj)] = ev
            pt_free[bk] = ev
            tp_evs.append(ev)
        WST[n] = {"tp": tp_evs, "sil": {}, "add2": {}, "tree": {}}

    def emit_conv_group(n, g):
        pb = n % 2
        HT = HTS[pb]
        htn = HTnm[pb]
        htv = HT.rearrange("p (c e) -> p c e", e=258)
        st_ = WST[n]
        sil_ms = st_["sil"]
        add2_ms = st_["add2"]
        tree_ms = st_["tree"]
        nxtv = None
        if n + 1 < NW:
            nxtv = HTS[(n + 1) % 2].rearrange("p (c e) -> p c e", e=258)
        if nxtv is not None:
            p.op("V", lambda v, g=g, nxtv=nxtv, htv=htv: v.tensor_copy(
                htv[:, g * 8:(g + 1) * 8, 257:258],
                nxtv[:, g * 8:(g + 1) * 8, 1:2]),
                reads=(HTnm[(n + 1) % 2],), writes=(htn,),
                wait=[evac_ms[(n + 1, g * 8 + 7)]])
            hl = p.op("V", lambda v, g=g, nxtv=nxtv, htv=htv:
                      v.tensor_copy(
                          nxtv[:, g * 8:(g + 1) * 8, 0:1],
                          htv[:, g * 8:(g + 1) * 8, 256:257]),
                      reads=(htn,), writes=(HTnm[(n + 1) % 2],),
                      wait=[evac_ms[(n, g * 8 + 7)]], inc=(g == 3))
            if g == 3:
                halo_ms[n + 1] = hl
        elif g == 0:
            p.op("V", lambda v, htv=htv: v.memset(htv[:, :, 257:258],
                                                   0.0),
                 writes=(htn,))
        for c in range(g * 8, g * 8 + 8):
            s4 = c % 2
            a0 = A0B[:, s4 * 256:(s4 + 1) * 256]
            l0 = ("A0B", s4 * 256, (s4 + 1) * 256)
            swl = []
            if c == 0:
                swl.append(halo_ms[n])
            if c >= 2:
                swl.append(add2_ms[c - 2])
            t0 = p.op("S", lambda s, c=c, a0=a0, HT=HT: s.activation(
                a0, hmid(HT, c, 0, 256), AF.Identity,
                scale=CW[:, c * 3:c * 3 + 1], bias=CB[:, c:c + 1]),
                reads=((htn, c * 258, c * 258 + 256), "CW", "CB"),
                writes=(l0,), wait=swl or None, inc=True)
            p.op("V", lambda v, c=c, a0=a0, HT=HT: v.scalar_tensor_tensor(
                a0, hmid(HT, c, 1, 257), CW[:, c * 3 + 1:c * 3 + 2], a0,
                op0=ALU.mult, op1=ALU.add),
                reads=((htn, c * 258 + 1, c * 258 + 257), "CW", l0),
                writes=(l0,), wait=[t0])
            a_ms = p.op("V", lambda v, c=c, a0=a0, HT=HT:
                        v.scalar_tensor_tensor(
                            a0, hmid(HT, c, 2, 258),
                            CW[:, c * 3 + 2:c * 3 + 3], a0,
                            op0=ALU.mult, op1=ALU.add),
                        reads=((htn, c * 258 + 2, c * 258 + 258), "CW", l0),
                        writes=(l0,), inc=True)
            add2_ms[c] = a_ms
            sil_ms[c] = p.op("S", lambda s, c=c, a0=a0, HT=HT: s.activation(
                hmid(HT, c, 1, 257), a0, AF.Silu),
                reads=(l0,),
                writes=((htn, c * 258 + 1, c * 258 + 257),),
                wait=[a_ms], inc=True)

        for pr in range(4 * g, 4 * g + 4):
            s2 = pr % 2
            abp = ABP[:, s2 * 512:(s2 + 1) * 512]
            lab = ("ABQ", s2 * 512, (s2 + 1) * 512)
            awl = []
            if pr >= 2:
                awl.append(tree_ms[pr - 2])
            elif m2done.get(n - 1) is not None:
                awl.append(m2done[n - 1])
            ab = p.op("S", lambda s, pr=pr, abp=abp: s.activation(
                abp, htv[:, 2 * pr:2 * pr + 2, 1:257], AF.Abs),
                reads=((htn, 2 * pr * 258, (2 * pr + 2) * 258),),
                writes=(lab,), wait=awl or None, inc=True)
            lane = pr % 2
            dst = RMA if lane == 0 else RMB
            dn = "RMA" if lane == 0 else "RMB"
            if pr < 2:
                tree_ms[pr] = p.op("V", lambda v, dst=dst, abp=abp:
                                   v.tensor_copy(dst[:], abp),
                                   reads=(lab,), writes=(dn,),
                                   wait=[ab], inc=True)
            else:
                tree_ms[pr] = p.op("V", lambda v, dst=dst, abp=abp:
                                   v.tensor_tensor(dst[:], dst[:], abp,
                                                   op=ALU.max),
                                   reads=(lab, dn), writes=(dn,),
                                   wait=[ab], inc=True)

    def emit_pipe(n):
        pb = n % 2
        so = 48 * pb + 16
        st_ = WST[n]
        tree_ms = st_["tree"]
        p.op("V", lambda v: v.tensor_tensor(RMA[:], RMA[:], RMB[:],
                                            op=ALU.max),
             reads=("RMA", "RMB"), writes=("RMA",),
             wait=[tree_ms[14], tree_ms[15]])
        rb = p.op("V", lambda v: v.tensor_tensor(
            RMF[:], RMA[:, 0:256], RMA[:, 256:512], op=ALU.max),
            reads=("RMA",), writes=(("RMB", 0, 256),), inc=True)

        wl2 = [rb]
        if pt_free["A"] is not None:
            wl2.append(pt_free["A"])
        tms = None
        for jj in range(2):
            tms = p.op("T", lambda t, jj=jj: t.transpose(
                ptA[:, jj * 128:(jj + 1) * 128],
                RMF[:, jj * 128:(jj + 1) * 128], IDENT[:]),
                reads=(("RMB", 0, 256), "IDENT"), writes=("PT",),
                wait=wl2 if jj == 0 else None, inc=(jj == 1))
        mxc = p.op("V", lambda v, so=so: v.reduce_max(
            stats[:, so:so + 2].rearrange("p (j2 one) -> p j2 one", one=1),
            ptA[:, 0:256].rearrange("p (j2 nn) -> p j2 nn", nn=128),
            axis=AX.X),
            reads=("PT",), writes=(("stats", so, so + 2),), wait=[tms],
            inc=True)
        pt_free["A"] = mxc
        clip = p.op("V", lambda v, so=so: v.tensor_scalar_max(
            stats[:, so + 2:so + 4], stats[:, so:so + 2], 1e-5),
            reads=(("stats", so, so + 2),),
            writes=(("stats", so + 2, so + 4),), inc=True)
        p.op("P", lambda g, so=so: g.tensor_tensor(
            stats[:, so + 4:so + 6], stats[:, so + 2:so + 4],
            NEGH[:, 19:21], op=ALU.pow),
            reads=(("stats", so + 2, so + 4), ("NEGH", 19, 21)),
            writes=(("stats", so + 4, so + 6),), wait=[clip])
        s127p = p.op("P", lambda g, so=so: g.tensor_tensor(
            stats[:, so + 6:so + 8], stats[:, so + 4:so + 6],
            NEGH[:, 21:23], op=ALU.mult),
            reads=(("stats", so + 4, so + 6), ("NEGH", 21, 23)),
            writes=(("stats", so + 6, so + 8),), inc=True)
        p.op("P", lambda g, so=so: g.tensor_tensor(
            stats[:, so + 8:so + 10], stats[:, so + 2:so + 4],
            NEGH[:, 23:25], op=ALU.mult),
            reads=(("stats", so + 2, so + 4), ("NEGH", 23, 25)),
            writes=(("stats", so + 8, so + 10),))
        c2a = p.op("P", lambda g, so=so: g.tensor_tensor(
            stats[:, so + 10:so + 11], stats[:, so + 8:so + 9],
            WSC[:, 3:4], op=ALU.mult),
            reads=(("stats", so + 8, so + 9), "WSC"),
            writes=(("stats", so + 10, so + 11),), inc=True)
        c2_ms = p.op("P", lambda g, so=so: g.tensor_tensor(
            stats[:, so + 11:so + 12], stats[:, so + 9:so + 10],
            WSC[:, 3:4], op=ALU.mult),
            reads=(("stats", so + 9, so + 10), "WSC"),
            writes=(("stats", so + 11, so + 12),), inc=True)
        sb2_ms = p.op("S", lambda s, so=so: s.activation(
            SB2S[:], stats[:, so + 6:so + 8], AF.Identity),
            reads=(("stats", so + 6, so + 8),), writes=("SB2S",),
            wait=[s127p], inc=True)
        wl3 = [sb2_ms]
        if pt_free["B"] is not None:
            wl3.append(pt_free["B"])
        st2 = p.op("T", lambda t: t.transpose(ptB[0:2, 0:128], SB2S[:],
                                              IDENT[:]),
                   reads=("SB2S", "IDENT"), writes=("PT",), wait=wl3,
                   inc=True)
        sb2ev = p.op("S", lambda s: s.activation(SB2[:], ptB[0:2, 0:128],
                                                 AF.Identity),
                     reads=("PT",), writes=("SB2",), wait=[st2], inc=True)
        pt_free["B"] = sb2ev
        sr_dma = p.dma("SP", SROWB[0:1, :], SB2[:], wait=[sb2ev],
                       sem="srd")
        bwl = [sr_dma]
        if ffo_done.get(n - 1) is not None:
            bwl.append(ffo_done[n - 1])
        if slot_free[2] is not None:
            bwl.append(slot_free[2])
        p.op("T", lambda t: t.matmul(
            slot(2)[:, 0:256], ONE1[:], SROWB[0:1, :], start=True,
            stop=True),
            reads=("ONE1", "SROWB"), writes=("PS",), wait=bwl)
        bc_mm = p.op("T", lambda t: t.matmul(
            slot(2)[:, 256:512], ONE1[:], SROWB[0:1, :], start=True,
            stop=True),
            reads=("ONE1", "SROWB"), writes=("PS",), inc=True)
        sbc_ms = p.op("S", lambda s: s.activation(SBCS[:], slot(2)[:, :],
                                                  AF.Identity),
                      reads=("PS",), writes=("RMA",), wait=[bc_mm],
                      inc=True)
        st_["sbc"] = sbc_ms
        st_["c2"] = c2_ms

    def emit_mm2_out(n):
        pb = n % 2
        HT = HTS[pb]
        htn = HTnm[pb]
        htv = HT.rearrange("p (c e) -> p c e", e=258)
        X1 = X1T[:, pb * 2048:(pb + 1) * 2048]
        so = 48 * pb + 16
        lb = 48 * pb + 28
        st_ = WST[n]
        sil_ms = st_["sil"]
        sbc_ms = st_["sbc"]
        c2_ms = st_["c2"]
        m2slots = [3, 4, 5, 2]
        mm_last = {}
        hq_ms = {}
        for pg in range(8):
            for step in range(5):
                for k in range(2):
                    pr = 2 * pg + k
                    st = k * 512
                    uu = UU[:, st:st + 512]
                    ia = IA[:, st:st + 512]
                    luu = ("UU", st, st + 512)
                    lia = ("IA", st, st + 512)
                    lhq = ("ABQ", st, st + 512)
                    if step == 0:
                        wl = [sil_ms[2 * pr + 1]]
                        if pr == 0:
                            wl.append(sbc_ms)
                        v1 = p.op("V", lambda v, pr=pr, uu=uu:
                                  v.tensor_tensor(
                                      uu, htv[:, 2 * pr:2 * pr + 2, 1:257],
                                      SBCS[:], op=ALU.mult),
                                  reads=((htn, 2 * pr * 258,
                                          (2 * pr + 2) * 258), "RMA"),
                                  writes=(luu,), wait=wl,
                                  inc=(pr == 15))
                        if pr == 15:
                            m2v1_last[n] = v1
                    elif step == 1:
                        p.op("V", lambda v, uu=uu, ia=ia: v.tensor_scalar(
                            ia, uu, 0.5, None, op0=ALU.add),
                            reads=(luu,), writes=(lia,))
                    elif step == 2:
                        p.op("V", lambda v, uu=uu, ia=ia:
                             v.scalar_tensor_tensor(
                                 uu, uu, 0.5, ia, op0=ALU.add,
                                 op1=ALU.subtract),
                             reads=(luu, lia), writes=(luu,))
                    elif step == 3:
                        p.op("V", lambda v, uu=uu, ia=ia:
                             v.scalar_tensor_tensor(
                                 ia, uu, 0.0, ia, op0=ALU.is_ge,
                                 op1=ALU.add),
                             reads=(luu, lia), writes=(lia,))
                    else:
                        hwl = [mm_last[pr - 2]] if pr >= 2 else None
                        hq_ms[pr] = p.op("V", lambda v, ia=ia, st=st:
                                         v.tensor_scalar(
                                             HQ[:, st:st + 512], ia, -1.0,
                                             None, op0=ALU.add),
                                         reads=(lia,), writes=(lhq,),
                                         wait=hwl, inc=True)
            for k in range(2):
                pr = 2 * pg + k
                st = k * 512
                lhq = ("ABQ", st, st + 512)
                last = None
                for ci in range(2):
                    c = 2 * pr + ci
                    for jj2 in range(2):
                        for g_i in range(2):
                            si = m2slots[jj2 * 2 + g_i]
                            wlm = None
                            if ci == 0 and jj2 == 0 and g_i == 0:
                                wlm = [hq_ms[pr]]
                                if c == 0:
                                    if ffo_done.get(n - 1) is not None:
                                        wlm.append(ffo_done[n - 1])
                                    for s_ in (3, 4, 5):
                                        if slot_free[s_] is not None:
                                            wlm.append(slot_free[s_])
                            if c == 0 and si == 2 and wlm is None:
                                wlm = [sbc_ms]
                            last = p.op("T", lambda t, si=si, c=c, jj2=jj2,
                                        g_i=g_i, st=st: t.matmul(
                                            slot(si),
                                            HQ[:, st + (c % 2) * 256 +
                                               jj2 * 128:
                                               st + (c % 2) * 256 +
                                               (jj2 + 1) * 128],
                                            W2T[:, c * 1024 + g_i * 512:
                                                c * 1024 + (g_i + 1) * 512],
                                            start=(c == 0), stop=(c == 31)),
                                        reads=(lhq, "WBIG"), writes=("PS",),
                                        wait=wlm,
                                        inc=(ci == 1 and jj2 == 1
                                             and g_i == 1))
                mm_last[pr] = last
        m2done[n] = mm_last[15]

        prev_out = None
        out_ms = []
        for jj2 in range(2):
            l4 = lb + 4 * jj2
            evs = []
            for g_i in range(2):
                si = m2slots[jj2 * 2 + g_i]
                evw = [mm_last[15], c2_ms]
                if jj2 == 0 and g_i == 0 and ffo_done.get(n - 1) is not None:
                    evw.append(ffo_done[n - 1])
                if jj2 == 1 and g_i == 0:
                    evw.append(prev_out)
                ev = p.op("S", lambda s, si=si, g_i=g_i, jj2=jj2, so=so:
                          s.activation(
                              FFO[:, g_i * 512:(g_i + 1) * 512], slot(si),
                              AF.Identity,
                              scale=stats[:, so + 10 + jj2:so + 11 + jj2]),
                          reads=("PS", ("stats", so + 10, so + 12)),
                          writes=(("FFO", g_i * 512, (g_i + 1) * 512),),
                          wait=evw, inc=True)
                evs.append(ev)
                slot_free[si] = ev
            p.op("V", lambda v: v.bn_stats(bn6[:, 0:6], FFO[:, 0:512]),
                 reads=(("FFO", 0, 512),), writes=(("bn6", 0, 6),),
                 wait=evs)
            p.op("V", lambda v: v.bn_stats(bn6[:, 6:12], FFO[:, 512:1024]),
                 reads=(("FFO", 512, 1024),), writes=(("bn6", 6, 12),))
            p.op("V", lambda v: v.bn_aggr(bn2[:], bn6[:]),
                 reads=("bn6",), writes=("bn2",))
            ve = p.op("V", lambda v, l4=l4: v.tensor_scalar(
                stats[:, l4:l4 + 1], bn2[:, 1:2], EPS, None, op0=ALU.add),
                reads=("bn2",), writes=(("stats", l4, l4 + 1),), inc=True)
            pw = p.op("P", lambda g, l4=l4: g.tensor_tensor(
                stats[:, l4 + 1:l4 + 2], stats[:, l4:l4 + 1],
                NEGH[:, 0:1], op=ALU.pow),
                reads=(("stats", l4, l4 + 1), ("NEGH", 0, 1)),
                writes=(("stats", l4 + 1, l4 + 2),), wait=[ve], inc=True)
            bc = p.op("V", lambda v, l4=l4: v.scalar_tensor_tensor(
                stats[:, l4 + 2:l4 + 3], bn2[:, 0:1], -1.0,
                stats[:, l4 + 1:l4 + 2], op0=ALU.mult, op1=ALU.mult),
                reads=("bn2", ("stats", l4 + 1, l4 + 2)),
                writes=(("stats", l4 + 2, l4 + 3),), wait=[pw], inc=True)
            lnm = p.op("S", lambda s, l4=l4: s.activation(
                FFO[:], FFO[:], AF.Identity,
                bias=stats[:, l4 + 2:l4 + 3],
                scale=stats[:, l4 + 1:l4 + 2]),
                reads=("FFO", ("stats", l4, l4 + 3)), writes=("FFO",),
                wait=[bc], inc=True)
            om = p.op("V", lambda v, jj2=jj2, X1=X1: v.tensor_tensor(
                X1[:, jj2 * 1024:(jj2 + 1) * 1024],
                X1[:, jj2 * 1024:(jj2 + 1) * 1024], FFO[:], op=ALU.add),
                reads=(("X1T", pb * 2048 + jj2 * 1024,
                        pb * 2048 + (jj2 + 1) * 1024), "FFO"),
                writes=(("X1T", pb * 2048 + jj2 * 1024,
                         pb * 2048 + (jj2 + 1) * 1024),),
                wait=[lnm], inc=True)
            out_ms.append(om)
            prev_out = om
        ffo_done[n] = out_ms[-1]
        win_out_dma[n] = p.dma(
            "SP", dy[n * 256:(n + 1) * 256, :].rearrange(
                "(jj pp) nn -> pp jj nn", pp=128),
            X1.rearrange("p (jj nn) -> p jj nn", nn=1024),
            wait=out_ms, sem="od")

    for n in range(NW + 2):
        if n >= 2:
            emit_mm2_out(n - 2)
        if n <= NW - 1:
            emit_quant(n)
            pb = n % 2
            HT = HTS[pb]
            tp_evs = WST[n]["tp"]
            lastev = None
            for hfc in range(32):
                si = hfc % 2
                wl = list(tp_evs) if hfc < 2 else []
                if slot_free[si] is not None:
                    wl.append(slot_free[si])
                mms = None
                for c in range(8):
                    mms = p.op("T", lambda t, si=si, c=c, hfc=hfc:
                               t.matmul(
                                   slot(si)[:, 0:256],
                                   W1T[:, c * 4096 + hfc * 128:
                                       c * 4096 + (hfc + 1) * 128],
                                   XQFM[:, c * 256:(c + 1) * 256],
                                   start=(c == 0), stop=(c == 7)),
                               reads=("WBIG", "XQFM"),
                               writes=("PS",),
                               wait=(wl or None) if c == 0 else None,
                               inc=(c == 7))
                ewl = [mms]
                if hfc == 0 and (n - 2) in m2v1_last:
                    ewl.append(m2v1_last[n - 2])
                lastev = p.op("S", lambda s, si=si, hfc=hfc, HT=HT:
                              s.activation(
                                  hmid(HT, hfc, 1, 257),
                                  slot(si)[:, 0:256], AF.Identity),
                              reads=("PS",),
                              writes=((HTnm[pb], hfc * 258 + 1,
                                       hfc * 258 + 257),),
                              wait=ewl, inc=True)
                slot_free[si] = lastev
                evac_ms[(n, hfc)] = lastev
                if hfc % 8 == 7 and n >= 1:
                    emit_conv_group(n - 1, hfc // 8)
            mm1_ev[n] = lastev
        elif n == NW:
            for g in range(4):
                emit_conv_group(NW - 1, g)
        if 1 <= n <= NW:
            emit_pipe(n - 1)

    p.op("SP", lambda s, v=p.e["SP"].dma_val: s.wait_ge(sems["SPd"], v))
    p.op("P", lambda g, v=p.e["P"].dma_val: g.wait_ge(sems["Pd"], v))
    for k in ("xd0", "xd1", "td", "x1d", "wd0", "wd1", "cwd", "srd", "od"):
        p.op("SP", lambda sp, k=k, v=p.dsem.get(k, 0): sp.wait_ge(
            sems[k], v))

    with nc.Block() as blk3:
        p.run(blk3)
    ctx3.close()

    return nc


def kernel(L0_attn_x, L0_attn_Wq, L0_attn_Wk, L0_attn_Wv, L0_attn_Wg,
           L0_attn_Wo, L0_attn_g_norm_w, L0_ffn_w1, L0_ffn_conv_w,
           L0_ffn_conv_b, L0_ffn_w2):
    x = np.ascontiguousarray(np.asarray(L0_attn_x, dtype=np.float32))
    B, T, D_ = x.shape
    assert (B, D_) == (NCORES, D)

    consts = make_consts(T)
    nc = build_program(T)

    common = {
        "wq": np.ascontiguousarray(L0_attn_Wq, np.float32),
        "wk": np.ascontiguousarray(L0_attn_Wk, np.float32),
        "wv": np.ascontiguousarray(L0_attn_Wv, np.float32),
        "wg": np.ascontiguousarray(L0_attn_Wg, np.float32),
        "wo": np.ascontiguousarray(L0_attn_Wo, np.float32),
        "gnw": np.ascontiguousarray(
            np.asarray(L0_attn_g_norm_w, np.float32).reshape(DV, 1)),
        "w1": np.ascontiguousarray(L0_ffn_w1, np.float32),
        "cw": np.ascontiguousarray(
            np.asarray(L0_ffn_conv_w, np.float32).reshape(DH, 3)),
        "cb": np.ascontiguousarray(
            np.asarray(L0_ffn_conv_b, np.float32).reshape(DH, 1)),
        "w2": np.ascontiguousarray(L0_ffn_w2, np.float32),
        **consts,
    }
    in_maps = []
    for i in range(NCORES):
        m = dict(common)
        m["x"] = np.ascontiguousarray(x[i])
        in_maps.append(m)

    res = run_bass_kernel_spmd(nc, in_maps, core_ids=list(range(NCORES)))
    out = np.stack([np.asarray(r["y"], np.float32) for r in res.results])
    return out


# version 07
# speedup vs baseline: 1.1214x; 1.0484x over previous
from contextlib import ExitStack

import numpy as np
import ml_dtypes
import concourse.bass as bass
import concourse.mybir as mybir
from concourse.bass_utils import run_bass_kernel_spmd

F32 = mybir.dt.float32
BF16 = mybir.dt.bfloat16
I32 = mybir.dt.int32
AF = mybir.ActivationFunctionType
ALU = mybir.AluOpType
AX = mybir.AxisListType

D = 1024
H = 16
DK = 64
DV = 128
C = 128
DH = 4096
EPS = 1e-5
NCORES = 8

ENGS = ("T", "V", "S", "P", "SP")


class MS:
    __slots__ = ("eng", "val")

    def __init__(self, eng, val):
        self.eng, self.val = eng, val


class _EngState:
    def __init__(self, name):
        self.name = name
        self.ops = []
        self.sem_val = 0
        self.dma_val = 0
        self.pend_w = []
        self.pend_r = []


def _ivl(label):
    if isinstance(label, tuple):
        return label
    return (label, None, None)


def _conflict(a, b):
    if a[0] != b[0]:
        return False
    if a[1] is None or b[1] is None:
        return True
    return a[1] < b[2] and b[1] < a[2]


class Plan:
    def __init__(self, nc, sems, carry=None):
        self.nc = nc
        self.sems = sems
        self.e = {n: _EngState(n) for n in ENGS}
        self.dsem = {}
        if carry is not None:
            for n in ENGS:
                self.e[n].sem_val = carry.e[n].sem_val
                self.e[n].dma_val = carry.e[n].dma_val
            self.dsem = dict(carry.dsem)

    def _emit(self, eng, fn):
        self.e[eng].ops.append(fn)

    def _auto_drain(self, eng, reads, writes):
        st = self.e[eng]
        rds = [_ivl(x) for x in reads]
        wrs = [_ivl(x) for x in writes]
        hit = any(_conflict(w, r) for w in st.pend_w for r in rds) \
            or any(_conflict(w, v) for w in st.pend_w for v in wrs) \
            or any(_conflict(r, v) for r in st.pend_r for v in wrs)
        if hit:
            self._emit(eng, lambda obj: obj.drain())
            st.pend_w = []
            st.pend_r = []
        if len(st.pend_w) > 96:
            st.pend_w = [(n, None, None) for n in
                         {t[0] for t in st.pend_w}]
        if len(st.pend_r) > 96:
            st.pend_r = [(n, None, None) for n in
                         {t[0] for t in st.pend_r}]
        st.pend_w += wrs
        st.pend_r += rds

    def wait(self, eng, ms):
        if ms is None:
            return
        if isinstance(ms, MS):
            ms = [ms]
        for m in ms:
            if m is None:
                continue
            sem = self.sems[m.eng]
            val = m.val
            if m.eng in ("SPd", "Pd"):
                val = self.e[m.eng[:-1]].dma_val
            self._emit(eng, lambda obj, s=sem, v=val: obj.wait_ge(s, v))

    def op(self, eng, fn, reads=(), writes=(), wait=None, inc=False,
           nodrain=False):
        self.wait(eng, wait)
        if not nodrain:
            self._auto_drain(eng, reads, writes)
        st = self.e[eng]
        if inc:
            st.sem_val += 1
            sem = self.sems[eng]

            def wrapped(obj, fn=fn, sem=sem):
                fn(obj).then_inc(sem, 1)
            self._emit(eng, wrapped)
            return MS(eng, st.sem_val)
        self._emit(eng, lambda obj, fn=fn: fn(obj))
        return None

    def dma(self, eng, out_ap, in_ap, wait=None, slow=False, sem=None):
        self.wait(eng, wait)
        key = sem if sem is not None else eng + "d"
        if key in (eng + "d",):
            self.e[eng].dma_val += 16
            val = self.e[eng].dma_val
        else:
            self.dsem[key] = self.dsem.get(key, 0) + 16
            val = self.dsem[key]
        h = self.sems[key]
        self._emit(eng, lambda obj, o=out_ap, i=in_ap, s=h, sl=slow:
                   obj.dma_start(out=o, in_=i,
                                 allow_slow_non_contiguous=sl)
                   .then_inc(s, 16))
        return MS(key, val)

    def drain(self, eng):
        self._emit(eng, lambda obj: obj.drain())
        self.e[eng].pend_w = []
        self.e[eng].pend_r = []

    def barrier_ms(self, eng):
        st = self.e[eng]
        st.sem_val += 1
        sem = self.sems[eng]
        self._emit(eng, lambda obj, s=sem: obj.sem_inc(s, 1))
        return MS(eng, st.sem_val)

    def run(self, block):
        def mk(name):
            def body(obj):
                for fn in self.e[name].ops:
                    fn(obj)
            return body
        block.tensor(mk("T"))
        block.vector(mk("V"))
        block.scalar(mk("S"))
        block.gpsimd(mk("P"))
        block.sync(mk("SP"))


def make_consts(T):
    h = np.arange(H, dtype=np.float64)
    gamma = 1.0 - np.exp2(-5.0 - h)
    j = np.arange(C, dtype=np.float64)

    inv = 1.0 / 10000.0 ** (np.arange(0, DK, 2, dtype=np.float64) / DK)
    t = np.arange(T, dtype=np.float64)
    f = np.outer(t, inv)
    costab = np.tile(np.cos(f), (1, H)).astype(np.float32)
    sintab = np.tile(np.sin(f), (1, H)).astype(np.float32)

    dkve = np.repeat(gamma[None, :] ** (C - 1 - j[:, None]), DK,
                     axis=1).astype(np.float32)
    i_ = np.arange(C, dtype=np.float64)
    dq2 = (np.repeat(gamma[None, :] ** (i_[:, None] + 1 - C), DV, axis=1)
           / 8.0).astype(np.float32)
    gct = np.ascontiguousarray(
        np.repeat(np.broadcast_to(gamma[None, :] ** C, (DK, H)), DV,
                  axis=1)).astype(np.float32)
    m = (np.arange(C)[None, :] >= np.arange(C)[:, None]).astype(np.float32)
    mask2 = np.ascontiguousarray(np.concatenate([m, m], axis=1))
    ident = np.eye(128, dtype=ml_dtypes.bfloat16)
    ones1 = np.ones((1, 128), dtype=ml_dtypes.bfloat16)
    return {"costab": costab, "sintab": sintab, "dkve": dkve, "dq2": dq2,
            "gct": gct, "mask2": mask2, "ident": ident, "ones1": ones1}


def build_program(T=4096):
    assert T % 256 == 0
    NB = T // 128
    NW = T // 256

    nc = bass.Bass("TRN2", target_bir_lowering=False)

    din = {}
    for nm, shape, dt in [
            ("x", [T, D], F32), ("wq", [D, D], F32), ("wk", [D, D], F32),
            ("wv", [D, 2 * D], F32), ("wg", [D, 2 * D], F32),
            ("wo", [2 * D, D], F32), ("gnw", [DV, 1], F32),
            ("w1", [DH, D], F32), ("cw", [DH, 3], F32), ("cb", [DH, 1], F32),
            ("w2", [D, DH], F32), ("costab", [T, 512], F32),
            ("sintab", [T, 512], F32), ("dkve", [128, 1024], F32),
            ("dq2", [128, 2048], F32), ("gct", [64, 2048], F32),
            ("mask2", [128, 256], F32), ("ident", [128, 128], BF16),
            ("ones1", [1, 128], BF16)]:
        din[nm] = nc.dram_tensor(nm, shape, dt, kind="ExternalInput").ap()

    dy = nc.dram_tensor("y", [T, D], F32, kind="ExternalOutput").ap()
    dx1 = nc.dram_tensor("x1s", [T, D], F32, kind="Internal").ap()
    dw1t = nc.dram_tensor("w1t", [D, DH], BF16, kind="Internal").ap()
    dw2t = nc.dram_tensor("w2t", [DH, D], BF16, kind="Internal").ap()

    psb = lambda nm, shape, dt=F32: nc.alloc_sbuf_tensor(
        nm, list(shape), dt).ap()
    WBIG = psb("WBIG", (128, 65536), BF16)
    WQ = WBIG[:, 0:8192]
    WK = WBIG[:, 8192:16384]
    WV = WBIG[:, 16384:32768]
    WG = WBIG[:, 32768:49152]
    WO = WBIG[:, 49152:65536]
    W1T = WBIG[:, 0:32768]
    W2T = WBIG[:, 32768:65536]

    IDENT = psb("IDENT", (128, 128), BF16)
    ONE1 = psb("ONE1", (1, 128), BF16)
    ONE1F = psb("ONE1F", (1, 128))
    NEGH = psb("NEGH", (128, 26))
    stats = psb("stats", (128, 96))
    bn6 = psb("bn6", (128, 12))
    bn2 = psb("bn2", (128, 2))
    WSC = psb("WSC", (128, 8))
    CW = psb("CW", (128, 96))
    CB = psb("CB", (128, 32))

    PS = nc.alloc_psum_tensor("PS", [128, 3072], F32).ap()
    PT = nc.alloc_psum_tensor("PT", [128, 2048], BF16).ap()
    ptA = PT[:, 0:1024]
    ptB = PT[:, 1024:2048]

    def slot(i):
        return PS[:, 512 * i:512 * (i + 1)]

    sems = {}
    for n in ["T", "V", "S", "P", "SP", "SPd", "Pd", "xd0", "xd1", "td",
              "x1d", "wd0", "wd1", "cwd", "srd", "od"]:
        sems[n] = nc.semaphore("sem_" + n).__enter__()

    ctx1 = ExitStack()
    s1 = lambda nm, shape, dt=F32: ctx1.enter_context(
        nc.sbuf_tensor(nm, list(shape), dt)).ap()

    xt = s1("xt", (128, 2 * D))
    ln1 = s1("ln1", (128, 2 * D), BF16)
    qrt = s1("qrt", (128, D), BF16)
    krt = s1("krt", (128, D), BF16)
    ropem = s1("ropem", (128, 1024), BF16)
    qfm = s1("qfm", (64, 2048), BF16)
    kfm = s1("kfm", (64, 2048), BF16)
    vtm = s1("vtm", (128, 2048), BF16)
    gsil = s1("gsil", (128, 2048), BF16)
    Ab = s1("Ab", (128, 512), BF16)
    Sst = s1("Sst", (64, 2048))
    Sbb = s1("Sbb", (64, 2048), BF16)
    otm = s1("otm", (128, 2048), BF16)
    og = s1("og", (128, 2048), BF16)
    ogfm = og
    ln2b = s1("ln2b", (128, D), BF16)
    x1t = s1("x1t", (128, D))
    ctab = s1("ctab", (128, 512))
    stab = s1("stab", (128, 512))
    DKVEB = s1("DKVEB", (128, 1024), BF16)
    DQ2B = s1("DQ2B", (128, 2048), BF16)
    GCTB = s1("GCTB", (64, 2048), BF16)
    MASKB = s1("MASKB", (128, 256), BF16)
    GNCOL = s1("GNCOL", (128, 1))

    p = Plan(nc, sems)

    p.dma("SP", IDENT, din["ident"])
    p.dma("SP", ONE1, din["ones1"])
    p.dma("SP", GNCOL, din["gnw"])
    p.dma("SP", xt[:, 0:1024], din["dkve"])
    p.dma("SP", ctab[:, 0:256], din["mask2"])
    p.dma("SP", xt[:, 1024:2048], din["dq2"][:, 0:1024])
    p.dma("SP", x1t[:, 0:1024], din["dq2"][:, 1024:2048])
    call = p.dma("SP", Sst[0:64, 0:2048], din["gct"])
    p.op("V", lambda v: v.tensor_copy(DKVEB[:], xt[:, 0:1024]),
         reads=("xt",), writes=("DKVEB",), wait=[call])
    p.op("V", lambda v: v.tensor_copy(MASKB[:], ctab[:, 0:256]),
         reads=("ctab",), writes=("MASKB",))
    p.op("V", lambda v: v.tensor_copy(DQ2B[:, 0:1024], xt[:, 1024:2048]),
         reads=("xt",), writes=("DQ2B",))
    p.op("V", lambda v: v.tensor_copy(DQ2B[:, 1024:2048], x1t[:, 0:1024]),
         reads=("x1t",), writes=("DQ2B",))
    cast4 = p.op("V", lambda v: v.tensor_copy(GCTB[:], Sst[0:64, 0:2048]),
                 reads=("Sst",), writes=("GCTB",), inc=True)
    consts_done = cast4

    wms = []
    for dst, src, w in [(WQ, din["wq"], 1024), (WK, din["wk"], 1024),
                        (WV, din["wv"], 2048), (WG, din["wg"], 2048),
                        (WO, din["wo"], 1024)]:
        wms.append(p.dma("P", dst.rearrange("p (c n) -> p c n", n=w),
                         src.rearrange("(c p) n -> p c n", p=128)))
    p.op("V", lambda v: v.memset(NEGH[:, 0:16], -0.5),
         writes=(("NEGH", 0, 16),))
    p.op("V", lambda v: v.memset(NEGH[:, 16:17], -1.0),
         writes=(("NEGH", 16, 17),))
    p.op("V", lambda v: v.memset(NEGH[:, 17:18], 127.0),
         writes=(("NEGH", 17, 18),))
    p.op("V", lambda v: v.memset(NEGH[:, 18:19], 1.0 / 127.0),
         writes=(("NEGH", 18, 19),))
    p.op("V", lambda v: v.memset(NEGH[:, 19:21], -1.0),
         writes=(("NEGH", 19, 21),))
    p.op("V", lambda v: v.memset(NEGH[:, 21:23], 127.0),
         writes=(("NEGH", 21, 23),))
    p.op("V", lambda v: v.memset(NEGH[:, 23:25], 1.0 / 127.0),
         writes=(("NEGH", 23, 25),))
    p.op("V", lambda v: v.memset(Sst[:], 0.0), writes=("Sst",),
         wait=[consts_done])
    p.op("V", lambda v: v.memset(ONE1F[:], 1.0), writes=("ONE1F",))
    wo_sc_ms = None
    for c in range(16):
        wo_sc_ms = p.op("V", lambda v, c=c: v.tensor_scalar_mul(
            WO[:, c * 1024:(c + 1) * 1024], WO[:, c * 1024:(c + 1) * 1024],
            GNCOL[:, 0:1]),
            reads=("WBIG", "GNCOL"), writes=("WBIG",),
            wait=[wms[4], consts_done] if c == 0 else None,
            inc=(c == 15))
    wready = wms[:4] + [wo_sc_ms, consts_done]

    slot_free = [None] * 6
    pt_free = {"A": None, "B": None}
    blk_end_V = {}
    x1_dma = {}
    S_done = {}
    ab_free = {}
    rope_done = {}

    projd_all = {}
    tabs = {}
    ln1ev = {}

    group_src = [(WQ, 1024, 0), (WQ, 1024, 512), (WK, 1024, 0),
                 (WK, 1024, 512), (WV, 2048, 0), (WV, 2048, 512),
                 (WV, 2048, 1024), (WV, 2048, 1536), (WG, 2048, 0),
                 (WG, 2048, 512), (WG, 2048, 1024), (WG, 2048, 1536)]

    def emit_group(b, gi):
        Wsrc, wmul, coff = group_src[gi]
        pb = b % 2
        lnb = ln1[:, pb * D:(pb + 1) * D]
        lnlb = ("ln1", pb * D, (pb + 1) * D)
        proj_ms = projd_all[b]
        si = gi % 6
        wl = [ln1ev[b]]
        if b == 0 and gi == 0:
            wl = wl + wready
        if slot_free[si] is not None:
            wl.append(slot_free[si])
        ms = None
        for c in range(8):
            ms = p.op("T", lambda t, si=si, c=c, Wsrc=Wsrc, wmul=wmul,
                      coff=coff, lnb=lnb: t.matmul(
                          slot(si), lnb[:, c * 128:(c + 1) * 128],
                          Wsrc[:, c * wmul + coff:c * wmul + coff + 512],
                          start=(c == 0), stop=(c == 7)),
                      reads=(lnlb, "WBIG"), writes=("PS",),
                      wait=wl if c == 0 else None, inc=(c == 7))
        proj_ms[gi] = ms
        if gi >= 4:
            if gi < 8:
                dstn, dst, func = "vtm", vtm[:, (gi - 4) * 512:
                                             (gi - 3) * 512], AF.Identity
                ew = [ms]
            else:
                dstn, dst, func = "gsil", gsil[:, (gi - 8) * 512:
                                               (gi - 7) * 512], AF.Silu
                ew = [ms]
                if gi == 8 and og_gate.get(b - 1) is not None:
                    ew.append(og_gate[b - 1])
            ev = p.op("S", lambda s, si=si, dst=dst, func=func:
                      s.activation(dst, slot(si), func),
                      reads=("PS",), writes=(dstn,), wait=ew, inc=True)
            slot_free[si] = ev

    def front(b):
        pb = b % 2
        so = 32 * pb
        xb = xt[:, pb * D:(pb + 1) * D]
        lnb = ln1[:, pb * D:(pb + 1) * D]
        lnlb = ("ln1", pb * D, (pb + 1) * D)
        projd_all[b] = {}

        w = [blk_end_V[b - 2]] if b >= 2 else ([consts_done] if b == 0
                                               else None)
        x_ms = p.dma("SP", xb, din["x"][b * 128:(b + 1) * 128, :], wait=w,
                     sem="xd%d" % pb)
        tw0 = [rope_done[b - 1]] if b >= 1 else None
        ct_ms = p.dma("SP", ctab[:], din["costab"][b * 128:(b + 1) * 128, :],
                      wait=tw0, sem="td")
        st_ms = p.dma("SP", stab[:], din["sintab"][b * 128:(b + 1) * 128, :],
                      sem="td")
        tabs[b] = st_ms

        p.op("V", lambda v, xb=xb: v.bn_stats(bn6[:, 0:6], xb[:, 0:512]),
             reads=("xt",), writes=("bn6",), wait=[x_ms])
        p.op("V", lambda v, xb=xb: v.bn_stats(bn6[:, 6:12], xb[:, 512:1024]),
             reads=("xt",), writes=("bn6",))
        p.op("V", lambda v: v.bn_aggr(bn2[:], bn6[:]),
             reads=("bn6",), writes=("bn2",))
        ve = p.op("V", lambda v, so=so: v.tensor_scalar(
            stats[:, so:so + 1], bn2[:, 1:2], EPS, None, op0=ALU.add),
            reads=("bn2",), writes=(("stats", so, so + 1),), inc=True)
        pw = p.op("P", lambda g, so=so: g.tensor_tensor(
            stats[:, so + 1:so + 2], stats[:, so:so + 1], NEGH[:, 0:1],
            op=ALU.pow),
            reads=(("stats", so, so + 1), ("NEGH", 0, 1)),
            writes=(("stats", so + 1, so + 2),), wait=[ve], inc=True)
        bc = p.op("V", lambda v, so=so: v.scalar_tensor_tensor(
            stats[:, so + 2:so + 3], bn2[:, 0:1], -1.0,
            stats[:, so + 1:so + 2], op0=ALU.mult, op1=ALU.mult),
            reads=("bn2", ("stats", so + 1, so + 2)),
            writes=(("stats", so + 2, so + 3),), wait=[pw], inc=True)
        lw = [bc]
        if b >= 2 and projd_all.get(b - 2):
            lw.append(projd_all[b - 2][11])
        ln1_ms = p.op("S", lambda s, xb=xb, so=so, lnb=lnb: s.activation(
            lnb, xb, AF.Identity, bias=stats[:, so + 2:so + 3],
            scale=stats[:, so + 1:so + 2]),
            reads=("xt", ("stats", so, so + 3)), writes=(lnlb,), wait=lw,
            inc=True)

        tw = [ln1_ms]
        if pt_free["A"] is not None:
            tw.append(pt_free["A"])
        tms = None
        for c in range(8):
            tms = p.op("T", lambda t, c=c, lnb=lnb: t.transpose(
                ptA[:, c * 128:(c + 1) * 128], lnb[:, c * 128:(c + 1) * 128],
                IDENT[:]),
                reads=(lnlb, "IDENT"), writes=("PT",),
                wait=tw if c == 0 else None, inc=(c == 7))
        ln1fm_ms = p.op("S", lambda s, lnb=lnb: s.activation(lnb, ptA,
                                                             AF.Identity),
                        reads=("PT",), writes=(lnlb,), wait=[tms],
                        inc=True)
        pt_free["A"] = ln1fm_ms
        ln1ev[b] = ln1fm_ms

        for gi in range(6):
            emit_group(b, gi)

    def back1(b):
        pb = b % 2
        so = 32 * pb
        ctb = ctab[:]
        stb = stab[:]
        proj_ms = projd_all[b]

        def emit_rope(psrc, dst, with_dkv, wait_ms, nm_dst):
            s1_ = psrc.rearrange("p (h two d) -> p h two d", two=2, d=32)
            x1s_, x2s_ = s1_[:, :, 0, :], s1_[:, :, 1, :]
            o1 = dst.rearrange("p (h two d) -> p h two d", two=2, d=32)
            o1a, o2a = o1[:, :, 0, :], o1[:, :, 1, :]
            cv = ctb.rearrange("p (h d) -> p h d", d=32)
            sv = stb.rearrange("p (h d) -> p h d", d=32)
            m0 = ropem[:, 0:512].rearrange("p (h d) -> p h d", d=32)
            m1 = ropem[:, 512:1024].rearrange("p (h d) -> p h d", d=32)
            dkv = DKVEB.rearrange("p (h two d) -> p h two d", two=2, d=32)
            p.op("V", lambda v: v.tensor_mul(m0, x1s_, cv),
                 reads=("PS", "ctab"), writes=("ropem",), wait=wait_ms)
            p.op("V", lambda v: v.tensor_mul(m1, x2s_, sv),
                 reads=("PS", "stab"), writes=("ropem",))
            if with_dkv:
                p.op("V", lambda v: v.tensor_sub(m0, m0, m1),
                     reads=("ropem",), writes=("ropem",))
                p.op("V", lambda v: v.tensor_mul(o1a, m0, dkv[:, :, 0, :]),
                     reads=("ropem", "DKVEB"), writes=(nm_dst,))
            else:
                p.op("V", lambda v: v.tensor_sub(o1a, m0, m1),
                     reads=("ropem",), writes=(nm_dst,))
            p.op("V", lambda v: v.tensor_mul(m0, x1s_, sv),
                 reads=("PS", "stab"), writes=("ropem",))
            p.op("V", lambda v: v.tensor_mul(m1, x2s_, cv),
                 reads=("PS", "ctab"), writes=("ropem",))
            if with_dkv:
                p.op("V", lambda v: v.tensor_add(m0, m0, m1),
                     reads=("ropem",), writes=("ropem",))
                return p.op("V", lambda v: v.tensor_mul(
                    o2a, m0, dkv[:, :, 1, :]),
                    reads=("ropem", "DKVEB"), writes=(nm_dst,), inc=True)
            return p.op("V", lambda v: v.tensor_add(o2a, m0, m1),
                        reads=("ropem",), writes=(nm_dst,), inc=True)

        rope_w_q = [proj_ms[0], proj_ms[1], tabs[b]]
        if b > 0:
            rope_w_q.append(S_done[b - 1])
        qr_ms = emit_rope(PS[:, 0:1024], qrt, False, rope_w_q, "qrt")
        kr_ms = emit_rope(PS[:, 1024:2048], krt, True,
                          [proj_ms[2], proj_ms[3]], "krt")
        rope_done[b] = kr_ms
        slot_free[0] = slot_free[1] = qr_ms
        slot_free[2] = slot_free[3] = kr_ms
        for gi in range(6, 12):
            emit_group(b, gi)
        vg_ev = slot_free[5]

        if b > 0:
            sg_v = p.op("V", lambda v: v.tensor_mul(Sst[:], Sst[:], GCTB[:]),
                        reads=("Sst", "GCTB"), writes=("Sst",),
                        wait=[S_done[b - 1]], inc=True)
            sb_ms = p.op("S", lambda s: s.activation(Sbb[:], Sst[:],
                                                     AF.Identity),
                         reads=("Sst",), writes=("Sbb",), wait=[sg_v],
                         inc=True)
        else:
            sg_v = sb_ms = None

        ev_fm = {}
        for wv_i, (src, dst, nm_s, nm_d, h0, rms_) in enumerate([
                (qrt, qfm, "qrt", "qfm", 0, None),
                (qrt, qfm, "qrt", "qfm", 8, None),
                (krt, kfm, "krt", "kfm", 0, None),
                (krt, kfm, "krt", "kfm", 8, None)]):
            bk = "A" if wv_i % 2 == 0 else "B"
            bank = ptA if bk == "A" else ptB
            wl = [qr_ms if src is qrt else kr_ms]
            if pt_free[bk] is not None:
                wl.append(pt_free[bk])
            tms = None
            for hh in range(8):
                hx = h0 + hh
                tms = p.op("T", lambda t, bank=bank, hh=hh, hx=hx, src=src:
                           t.transpose(bank[0:64, hh * 128:(hh + 1) * 128],
                                       src[:, hx * 64:(hx + 1) * 64],
                                       IDENT[:]),
                           reads=(nm_s, "IDENT"), writes=("PT",),
                           wait=wl if hh == 0 else None, inc=(hh == 7))
            ev_fm[wv_i] = p.op(
                "S", lambda s, bank=bank, dst=dst, h0=h0: s.activation(
                    dst[0:64, h0 * 128:(h0 + 8) * 128], bank[0:64, 0:1024],
                    AF.Identity),
                reads=("PT",), writes=(nm_d,), wait=[tms], inc=True)
            pt_free[bk] = ev_fm[wv_i]

        mask_ms = {}
        o_ev = {}
        for wv_i in range(4):
            si = wv_i % 2
            reg = 0
            wl = [ev_fm[1], ev_fm[3]]
            if slot_free[si] is not None:
                wl.append(slot_free[si])
            ams = None
            for hh in range(4):
                hx = wv_i * 4 + hh
                ams = p.op("T", lambda t, si=si, hh=hh, hx=hx: t.matmul(
                    slot(si)[:, hh * 128:(hh + 1) * 128],
                    kfm[0:64, hx * 128:(hx + 1) * 128],
                    qfm[0:64, hx * 128:(hx + 1) * 128],
                    start=True, stop=True),
                    reads=("kfm", "qfm"), writes=("PS",),
                    wait=wl if hh == 0 else None, inc=(hh == 3))
            mwl = [ams]
            if ab_free.get(0) is not None:
                mwl.append(ab_free[0])
            msk = None
            for pr in range(2):
                msk = p.op("V", lambda v, si=si, reg=reg, pr=pr:
                           v.tensor_mul(
                               Ab[:, reg * 512 + pr * 256:
                                  reg * 512 + (pr + 1) * 256],
                               slot(si)[:, pr * 256:(pr + 1) * 256],
                               MASKB[:]),
                           reads=("PS", "MASKB"), writes=("Ab",),
                           wait=mwl if pr == 0 else None, inc=(pr == 1))
            mask_ms[wv_i] = msk
            slot_free[si] = msk

            osi = 2 + wv_i
            owl = [msk, vg_ev]
            if sb_ms is not None:
                owl.append(sb_ms)
            if slot_free[osi] is not None:
                owl.append(slot_free[osi])
            oms = None
            for hh in range(4):
                hx = wv_i * 4 + hh
                oms = p.op("T", lambda t, osi=osi, hh=hh, hx=hx, reg=reg,
                           sf=(b == 0): t.matmul(
                               slot(osi)[:, hh * 128:(hh + 1) * 128],
                               Ab[:, reg * 512 + hh * 128:
                                   reg * 512 + (hh + 1) * 128],
                               vtm[:, hx * 128:(hx + 1) * 128],
                               start=True, stop=sf),
                           reads=("Ab", "vtm"), writes=("PS",),
                           wait=owl if hh == 0 else None,
                           inc=(b == 0 and hh == 3))
                if b > 0:
                    oms = p.op("T", lambda t, osi=osi, hh=hh, hx=hx:
                               t.matmul(
                                   slot(osi)[:, hh * 128:(hh + 1) * 128],
                                   qfm[0:64, hx * 128:(hx + 1) * 128],
                                   Sbb[0:64, hx * 128:(hx + 1) * 128],
                                   start=False, stop=True),
                               reads=("qfm", "Sbb"), writes=("PS",),
                               inc=(hh == 3))
            ab_free[0] = oms
            o_ev[wv_i] = p.op("V", lambda v, osi=osi, wv_i=wv_i:
                              v.tensor_mul(
                                  otm[:, wv_i * 512:(wv_i + 1) * 512],
                                  slot(osi),
                                  DQ2B[:, wv_i * 512:(wv_i + 1) * 512]),
                              reads=("PS", "DQ2B"), writes=("otm",),
                              wait=[oms], inc=True)
            slot_free[osi] = o_ev[wv_i]

        kv_upd = []
        for wv_i in range(4):
            si = wv_i % 2
            wl = [kr_ms, vg_ev, slot_free[si]]
            kms = None
            for hh in range(4):
                hx = wv_i * 4 + hh
                kms = p.op("T", lambda t, si=si, hh=hh, hx=hx: t.matmul(
                    slot(si)[0:64, hh * 128:(hh + 1) * 128],
                    krt[:, hx * 64:(hx + 1) * 64],
                    vtm[:, hx * 128:(hx + 1) * 128],
                    start=True, stop=True),
                    reads=("krt", "vtm"), writes=("PS",),
                    wait=wl if hh == 0 else None, inc=(hh == 3))
            if b == 0:
                msv = p.op("V", lambda v, si=si, wv_i=wv_i: v.tensor_copy(
                    Sst[0:64, wv_i * 512:(wv_i + 1) * 512],
                    slot(si)[0:64, :]),
                    reads=("PS",), writes=("Sst",), wait=[kms], inc=True)
            else:
                msv = p.op("V", lambda v, si=si, wv_i=wv_i: v.tensor_add(
                    Sst[0:64, wv_i * 512:(wv_i + 1) * 512],
                    Sst[0:64, wv_i * 512:(wv_i + 1) * 512],
                    slot(si)[0:64, :]),
                    reads=("PS", "Sst"), writes=("Sst",),
                    wait=[kms, sg_v, sb_ms], inc=True)
            kv_upd.append(msv)
            slot_free[si] = msv
        S_done[b] = kv_upd[-1]
        back_state[b] = (o_ev, )

    def back2(b):
        pb = b % 2
        so = 32 * pb
        xb = xt[:, pb * D:(pb + 1) * D]
        x1b = x1t[:]
        (o_ev,) = back_state[b]

        p.op("V", lambda v: v.tensor_mul(og[:], otm[:], otm[:]),
             reads=("otm",), writes=("og",), wait=[o_ev[3]])
        p.op("V", lambda v, so=so: v.reduce_sum(
            stats[:, so + 8:so + 24].rearrange("p (h one) -> p h one",
                                               one=1),
            og[:].rearrange("p (h d) -> p h d", d=128), axis=AX.X),
            reads=("og",), writes=(("stats", so + 8, so + 24),))
        ve2 = p.op("V", lambda v, so=so: v.tensor_scalar(
            stats[:, so + 8:so + 24], stats[:, so + 8:so + 24], 1.0 / DV,
            EPS, op0=ALU.mult, op1=ALU.add),
            reads=(("stats", so + 8, so + 24),),
            writes=(("stats", so + 8, so + 24),), inc=True)
        pw2 = p.op("P", lambda g, so=so: g.tensor_tensor(
            stats[:, so + 8:so + 24], stats[:, so + 8:so + 24],
            NEGH[:, 0:16], op=ALU.pow),
            reads=(("stats", so + 8, so + 24), ("NEGH", 0, 16)),
            writes=(("stats", so + 8, so + 24),), wait=[ve2], inc=True)
        ogm = None
        for hx in range(16):
            ogm = p.op("V", lambda v, hx=hx, so=so: v.tensor_scalar_mul(
                og[:, hx * 128:(hx + 1) * 128],
                otm[:, hx * 128:(hx + 1) * 128],
                stats[:, so + 8 + hx:so + 9 + hx]),
                reads=("otm", ("stats", so + 8, so + 24)), writes=("og",),
                wait=[pw2] if hx == 0 else None, inc=(hx == 15))
        og_ms = p.op("V", lambda v: v.tensor_mul(og[:], og[:], gsil[:]),
                     reads=("og", "gsil"), writes=("og",), inc=True)
        og_gate[b] = og_ms

        ogfm_ev = {}
        for wv_i in range(2):
            bk = "A" if wv_i == 0 else "B"
            bank = ptA if wv_i == 0 else ptB
            wl = [og_ms]
            if pt_free[bk] is not None:
                wl.append(pt_free[bk])
            tms = None
            for hh in range(8):
                c = wv_i * 8 + hh
                tms = p.op("T", lambda t, bank=bank, hh=hh, c=c: t.transpose(
                    bank[:, hh * 128:(hh + 1) * 128],
                    og[:, c * 128:(c + 1) * 128], IDENT[:]),
                    reads=("og", "IDENT"), writes=("PT",),
                    wait=wl if hh == 0 else None, inc=(hh == 7))
            ogfm_ev[wv_i] = p.op(
                "S", lambda s, bank=bank, wv_i=wv_i: s.activation(
                    og[:, wv_i * 1024:(wv_i + 1) * 1024], bank[:, 0:1024],
                    AF.Identity),
                reads=("PT",), writes=("og",), wait=[tms], inc=True)
            pt_free[bk] = ogfm_ev[wv_i]

        wo_t = {}
        for g_i in range(2):
            si = 4 + g_i
            wl = [ogfm_ev[1]]
            if slot_free[si] is not None:
                wl.append(slot_free[si])
            wms2 = None
            for c in range(16):
                wms2 = p.op("T", lambda t, si=si, c=c, g_i=g_i: t.matmul(
                    slot(si), og[:, c * 128:(c + 1) * 128],
                    WO[:, c * 1024 + g_i * 512:c * 1024 + (g_i + 1) * 512],
                    start=(c == 0), stop=(c == 15)),
                    reads=("og", "WBIG"), writes=("PS",),
                    wait=wl if c == 0 else None, inc=(c == 15))
            wo_t[g_i] = wms2

        p.op("V", lambda v: v.bn_stats(bn6[:, 0:6], PS[:, 2048:2560]),
             reads=("PS",), writes=("bn6",), wait=[wo_t[0], wo_t[1]])
        p.op("V", lambda v: v.bn_stats(bn6[:, 6:12], PS[:, 2560:3072]),
             reads=("PS",), writes=("bn6",))
        p.op("V", lambda v: v.bn_aggr(bn2[:], bn6[:]),
             reads=("bn6",), writes=("bn2",))
        ve3 = p.op("V", lambda v, so=so: v.tensor_scalar(
            stats[:, so + 4:so + 5], bn2[:, 1:2], EPS, None, op0=ALU.add),
            reads=("bn2",), writes=(("stats", so + 4, so + 5),), inc=True)
        pw3 = p.op("P", lambda g, so=so: g.tensor_tensor(
            stats[:, so + 5:so + 6], stats[:, so + 4:so + 5], NEGH[:, 0:1],
            op=ALU.pow),
            reads=(("stats", so + 4, so + 5), ("NEGH", 0, 1)),
            writes=(("stats", so + 5, so + 6),), wait=[ve3], inc=True)
        bc3 = p.op("V", lambda v, so=so: v.scalar_tensor_tensor(
            stats[:, so + 6:so + 7], bn2[:, 0:1], -1.0,
            stats[:, so + 5:so + 6], op0=ALU.mult, op1=ALU.mult),
            reads=("bn2", ("stats", so + 5, so + 6)),
            writes=(("stats", so + 6, so + 7),), wait=[pw3], inc=True)
        ln2_ms = p.op("S", lambda s, so=so: s.activation(
            ln2b[:], PS[:, 2048:3072], AF.Identity,
            bias=stats[:, so + 6:so + 7], scale=stats[:, so + 5:so + 6]),
            reads=("PS", ("stats", so + 4, so + 7)), writes=("ln2b",),
            wait=[bc3], inc=True)
        slot_free[4] = slot_free[5] = ln2_ms
        xw = [ln2_ms]
        if b >= 1:
            xw.append(x1_dma[b - 1])
        x1_ms = p.op("V", lambda v, x1b=x1b, xb=xb: v.tensor_add(
            x1b, xb, ln2b[:]),
            reads=("xt", "ln2b"), writes=("x1t",), wait=xw, inc=True)
        blk_end_V[b] = x1_ms
        x1_dma[b] = p.dma("SP", dx1[b * 128:(b + 1) * 128, :], x1b,
                          wait=[x1_ms], sem="x1d")

    back_state = {}
    og_gate = {}
    front(0)
    for b in range(NB):
        back1(b)
        if b + 1 < NB:
            front(b + 1)
        back2(b)

    p.op("SP", lambda s, v=p.e["SP"].dma_val: s.wait_ge(sems["SPd"], v))
    p.op("P", lambda g, v=p.e["P"].dma_val: g.wait_ge(sems["Pd"], v))
    for k in ("xd0", "xd1", "td", "x1d"):
        p.op("SP", lambda sp, k=k, v=p.dsem.get(k, 0): sp.wait_ge(
            sems[k], v))

    with nc.Block() as blk1:
        p.run(blk1)
    ctx1.close()

    ctxp = ExitStack()
    s2_ = lambda nm, shape, dt=F32: ctxp.enter_context(
        nc.sbuf_tensor(nm, list(shape), dt)).ap()
    WSTG = s2_("WSTG", (128, 2 * 4096))
    WTERN = s2_("WTERN", (128, 4096), BF16)
    WTT = s2_("WTT", (128, 4096), BF16)
    WSUM = s2_("WSUM", (128, 64))
    UUW = s2_("UUW", (128, 1024))
    G1W = s2_("G1W", (128, 1024))
    G2W = s2_("G2W", (128, 1024))

    p = Plan(nc, sems, carry=p)
    slot_free = [None] * 6
    pt_free = {"A": None, "B": None}

    sum_last2 = [None, None]
    col = 0
    for dsrc, nchunk, width in [(din["w1"], 32, 1024), (din["w2"], 8, 4096)]:
        for r in range(nchunk):
            pbuf = WSTG[:, (r % 2) * 4096:(r % 2) * 4096 + width]
            wl = [sum_last2[r % 2]] if sum_last2[r % 2] is not None else None
            dm = p.dma("P", pbuf, dsrc[r * 128:(r + 1) * 128, :], wait=wl,
                       sem="wd%d" % (r % 2))
            ms = p.op("V", lambda v, pbuf=pbuf, col=col: v.reduce_sum(
                WSUM[:, col:col + 1], pbuf, axis=AX.X,
                apply_absolute_value=True),
                reads=("WSTG",), writes=("WSUM",), wait=[dm], inc=True)
            sum_last2[r % 2] = ms
            col += 1
    p.op("V", lambda v: v.reduce_sum(WSUM[:, 40:41], WSUM[:, 0:32],
                                     axis=AX.X),
         reads=("WSUM",), writes=("WSUM",),
         wait=[sum_last2[0], sum_last2[1]])
    t12 = p.op("V", lambda v: v.reduce_sum(WSUM[:, 41:42], WSUM[:, 32:40],
                                           axis=AX.X),
               reads=("WSUM",), writes=("WSUM",), inc=True)
    wsb = p.op("V", lambda v: v.tensor_copy(WTERN[:, 0:2], WSUM[:, 40:42]),
               reads=("WSUM",), writes=("WTERN",), wait=[t12], inc=True)
    tpm = p.op("T", lambda t: t.transpose(ptA[0:2, 0:128],
                                          WTERN[:, 0:2], IDENT[:]),
               reads=("WTERN", "IDENT"), writes=("PT",), wait=[wsb],
               inc=True)
    p.op("V", lambda v: v.reduce_sum(stats[0:2, 0:1], ptA[0:2, 0:128],
                                     axis=AX.X),
         reads=("PT",), writes=("stats",), wait=[tpm])
    p.op("V", lambda v: v.tensor_scalar(stats[0:2, 1:2], stats[0:2, 0:1],
                                        1.0 / (DH * D), 1e-5,
                                        op0=ALU.mult, op1=ALU.max),
         reads=("stats",), writes=("stats",))
    rc = p.op("V", lambda v: v.reciprocal(stats[0:2, 2:3], stats[0:2, 1:2]),
              reads=("stats",), writes=("stats",), inc=True)
    rcb = p.op("V", lambda v: v.tensor_copy(WTERN[0:2, 4:6],
                                            stats[0:2, 1:3]),
               reads=("stats",), writes=("WTERN",), wait=[rc], inc=True)
    tq1 = p.op("T", lambda t: t.transpose(ptA[0:1, 0:2], WTERN[0:2, 4:5],
                                          IDENT[0:2, 0:2]),
               reads=("WTERN", "IDENT"), writes=("PT",), wait=[rcb, tpm])
    tq2 = p.op("T", lambda t: t.transpose(ptA[0:1, 2:4], WTERN[0:2, 5:6],
                                          IDENT[0:2, 0:2]),
               reads=("WTERN", "IDENT"), writes=("PT",), inc=True)
    srow = p.op("V", lambda v: v.tensor_copy(WSUM[0:1, 48:52],
                                             ptA[0:1, 0:4]),
                reads=("PT",), writes=("WSUM",), wait=[tq2], inc=True)

    def stat_bcast(src_11, dst_col, wait_ms):
        mm = p.op("T", lambda t, src_11=src_11: t.matmul(
            slot(5)[:, 0:1], ONE1F[:], src_11, start=True, stop=True),
            reads=("ONE1F", "WSUM"), writes=("PS",), wait=wait_ms, inc=True)
        return p.op("V", lambda v, dst_col=dst_col: v.tensor_copy(
            dst_col, slot(5)[:, 0:1]),
            reads=("PS",), writes=("WSC",), wait=[mm], inc=True)

    b0 = stat_bcast(WSUM[0:1, 50:51], WSC[:, 0:1], [srow])
    b1 = stat_bcast(WSUM[0:1, 48:49], WSC[:, 1:2], [b0])
    b2 = stat_bcast(WSUM[0:1, 51:52], WSC[:, 2:3], [b1])
    b3 = stat_bcast(WSUM[0:1, 49:50], WSC[:, 3:4], [b2])

    cw_dm = p.dma("P", CW[:].rearrange("p (c k) -> p c k", k=3),
                  din["cw"].rearrange("(c p) k -> p c k", p=128), sem="cwd")
    cb_dm = p.dma("P", CB[:].rearrange("p (c one) -> p c one", one=1),
                  din["cb"].rearrange("(c p) one -> p c one", p=128),
                  slow=True, sem="cwd")
    p.op("V", lambda v: v.tensor_scalar_mul(CW[:], CW[:], WSC[:, 1:2]),
         reads=("CW", "WSC"), writes=("CW",), wait=[cb_dm, b1])

    pb_free = {}
    tern_T_last = tq2
    last_store = None
    for wi, (dsrc, ddst, nchunk, width, swc) in enumerate([
            (din["w1"], dw1t, 32, 1024, WSC[:, 0:1]),
            (din["w2"], dw2t, 8, 4096, WSC[:, 2:3])]):
        for r in range(nchunk):
            pbuf = WSTG[:, (r % 2) * 4096:(r % 2) * 4096 + width]
            wl = [b3]
            prev = pb_free.get((wi, r - 2))
            if prev is None and wi == 1 and r < 2:
                prev = pb_free.get((0, 30 + r))
            if prev is not None:
                wl.append(prev)
            dm = p.dma("P", pbuf, dsrc[r * 128:(r + 1) * 128, :], wait=wl,
                       sem="wd%d" % (r % 2))
            qt_ms = None
            for st0 in range(0, width, 1024):
                wwl = [dm]
                if tern_T_last is not None and st0 == 0:
                    wwl.append(tern_T_last)
                p.op("V", lambda v, pbuf=pbuf, st0=st0, swc=swc:
                     v.tensor_scalar_mul(UUW[:], pbuf[:, st0:st0 + 1024],
                                         swc),
                     reads=("WSTG", "WSC"), writes=("UUW",), wait=wwl)
                p.op("V", lambda v: v.tensor_single_scalar(
                    G1W[:], UUW[:], 0.5, op=ALU.is_ge),
                    reads=("UUW",), writes=("G1W",))
                p.op("V", lambda v: v.tensor_single_scalar(
                    G2W[:], UUW[:], -0.5, op=ALU.is_le),
                    reads=("UUW",), writes=("G2W",))
                qt_ms = p.op("V", lambda v, st0=st0: v.tensor_sub(
                    WTERN[:, st0:st0 + 1024], G1W[:], G2W[:]),
                    reads=("G1W", "G2W"), writes=("WTERN",),
                    inc=(st0 + 1024 >= width))
            pb_free[(wi, r)] = qt_ms
            ntc = width // 128
            ev_list = []
            for w0 in range(0, ntc, 8):
                bk = "A" if (w0 // 8) % 2 == 0 else "B"
                bank = ptA if bk == "A" else ptB
                nw = min(8, ntc - w0)
                wl2 = [qt_ms]
                if pt_free[bk] is not None:
                    wl2.append(pt_free[bk])
                tms = None
                for k in range(nw):
                    c = w0 + k
                    tms = p.op("T", lambda t, bank=bank, k=k, c=c:
                               t.transpose(bank[:, k * 128:(k + 1) * 128],
                                           WTERN[:, c * 128:(c + 1) * 128],
                                           IDENT[:]),
                               reads=("WTERN", "IDENT"), writes=("PT",),
                               wait=wl2 if k == 0 else None,
                               inc=(k == nw - 1))
                ewl = [tms]
                if last_store is not None and w0 == 0:
                    ewl.append(last_store)
                ev = p.op("S", lambda s, bank=bank, w0=w0, nw=nw:
                          s.activation(WTT[:, w0 * 128:(w0 + nw) * 128],
                                       bank[:, 0:nw * 128], AF.Identity),
                          reads=("PT",), writes=("WTT",), wait=ewl, inc=True)
                ev_list.append(ev)
                pt_free[bk] = ev
            tern_T_last = tms
            for ci in range(ntc):
                last_store = p.dma(
                    "SP",
                    ddst[ci * 128:(ci + 1) * 128, r * 128:(r + 1) * 128],
                    WTT[:, ci * 128:(ci + 1) * 128],
                    wait=[ev_list[ci // 8]] if ci % 8 == 0 else None)

    p.dma("SP", W1T.rearrange("p (c n) -> p c n", n=4096),
          dw1t.rearrange("(c p) n -> p c n", p=128), wait=[last_store])
    p.dma("SP", W2T.rearrange("p (c n) -> p c n", n=1024),
          dw2t.rearrange("(c p) n -> p c n", p=128))
    p.op("SP", lambda s, v=p.e["SP"].dma_val: s.wait_ge(sems["SPd"], v))
    p.op("P", lambda g, v=p.e["P"].dma_val: g.wait_ge(sems["Pd"], v))

    with nc.Block() as blk2:
        p.run(blk2)
    ctxp.close()

    ctx3 = ExitStack()
    s3_ = lambda nm, shape, dt=F32: ctx3.enter_context(
        nc.sbuf_tensor(nm, list(shape), dt)).ap()
    X1T = s3_("X1T", (128, 2 * 2048))
    LN3F = s3_("LN3F", (128, 1024))
    XQTM = s3_("XQTM", (128, 1024), BF16)
    XQFM = s3_("XQFM", (128, 2048), BF16)
    HT0 = s3_("HT0", (128, 32 * 258), BF16)
    HT1 = s3_("HT1", (128, 32 * 258), BF16)
    A0B = s3_("A0B", (128, 2 * 256))
    ABQ = s3_("ABQ", (128, 2 * 512), BF16)
    ABP = ABQ
    HQ = ABQ
    RMA = s3_("RMA", (128, 512), BF16)
    RMB = s3_("RMB", (128, 512), BF16)
    RMF = RMB[:, 0:256]
    SB2S = s3_("SB2S", (128, 2), BF16)
    UU = s3_("UU", (128, 2 * 512))
    IA = s3_("IA", (128, 2 * 512), I32)
    RR = IA
    SBCS = RMA
    SB2 = s3_("SB2", (2, 128), BF16)
    SROWB = s3_("SROWB", (1, 256), BF16)
    FFO = s3_("FFO", (128, 1024))
    LN4B = FFO

    p = Plan(nc, sems, carry=p)
    slot_free = [None] * 6
    pt_free = {"A": None, "B": None}

    HTS = [HT0, HT1]
    HTnm = ["HT0", "HT1"]

    def hmid(ht, c, a, bcol):
        return ht[:, c * 258 + a:c * 258 + bcol]

    mm1_ev = {}
    win_out_dma = {}
    ffo_done = {}
    halo_ms = {}
    xq_evs = {}
    ln3f_free = [None]
    m2done = {}
    evac_ms = {}

    mz_ms = None
    for c in range(32):
        mz_ms = p.op("V", lambda v, c=c: v.memset(hmid(HT0, c, 0, 1), 0.0),
                     writes=("HT0",), inc=(c == 31))
    halo_ms[0] = mz_ms

    WST = {}
    m2v1_last = {}

    def emit_quant(n):
        pb = n % 2
        pb = n % 2
        X1 = X1T[:, pb * 2048:(pb + 1) * 2048]
        HT = HTS[pb]
        wl = []
        if n >= 2:
            wl.append(win_out_dma.get(n - 2))
        x_ms = p.dma("SP", X1.rearrange("p (jj nn) -> p jj nn", nn=1024),
                     dx1[n * 256:(n + 1) * 256, :].rearrange(
                         "(jj pp) nn -> pp jj nn", pp=128),
                     wait=[m for m in wl if m is not None] or None,
                     sem="xd%d" % pb)
        tp_evs = []
        for jj in range(2):
            so = 48 * pb + 8 * jj
            x1lb = ("X1T", pb * 2048 + jj * 1024, pb * 2048 + (jj + 1) * 1024)
            xb = X1[:, jj * 1024:(jj + 1) * 1024]
            p.op("V", lambda v, xb=xb: v.bn_stats(bn6[:, 0:6],
                                                  xb[:, 0:512]),
                 reads=(x1lb,), writes=(("bn6", 0, 6),), wait=[x_ms])
            p.op("V", lambda v, xb=xb: v.bn_stats(bn6[:, 6:12],
                                                  xb[:, 512:1024]),
                 reads=(x1lb,), writes=(("bn6", 6, 12),))
            p.op("V", lambda v: v.bn_aggr(bn2[:], bn6[:]),
                 reads=("bn6",), writes=("bn2",))
            ve = p.op("V", lambda v, so=so: v.tensor_scalar(
                stats[:, so:so + 1], bn2[:, 1:2], EPS, None, op0=ALU.add),
                reads=("bn2",), writes=(("stats", so, so + 1),), inc=True)
            pw = p.op("P", lambda g, so=so: g.tensor_tensor(
                stats[:, so + 1:so + 2], stats[:, so:so + 1], NEGH[:, 0:1],
                op=ALU.pow),
                reads=(("stats", so, so + 1), ("NEGH", 0, 1)),
                writes=(("stats", so + 1, so + 2),), wait=[ve], inc=True)
            bc = p.op("V", lambda v, so=so: v.scalar_tensor_tensor(
                stats[:, so + 2:so + 3], bn2[:, 0:1], -1.0,
                stats[:, so + 1:so + 2], op0=ALU.mult, op1=ALU.mult),
                reads=("bn2", ("stats", so + 1, so + 2)),
                writes=(("stats", so + 2, so + 3),), wait=[pw], inc=True)
            lnm = p.op("S", lambda s, xb=xb, so=so: s.activation(
                LN3F[:], xb, AF.Identity, bias=stats[:, so + 2:so + 3],
                scale=stats[:, so + 1:so + 2]),
                reads=(x1lb, ("stats", so, so + 3)), writes=("LN3F",),
                wait=[bc, ln3f_free[0]] if ln3f_free[0] else [bc],
                inc=True)
            mx = p.op("V", lambda v, so=so: v.reduce_max(
                stats[:, so + 3:so + 4], LN3F[:], axis=AX.X,
                apply_absolute_value=True),
                reads=("LN3F",), writes=(("stats", so + 3, so + 4),),
                wait=[lnm], inc=True)
            mxc = p.op("V", lambda v, so=so: v.tensor_scalar_max(
                stats[:, so + 4:so + 5], stats[:, so + 3:so + 4], 1e-5),
                reads=(("stats", so + 3, so + 4),),
                writes=(("stats", so + 4, so + 5),), inc=True)
            p.op("P", lambda g, so=so: g.tensor_tensor(
                stats[:, so + 5:so + 6], stats[:, so + 4:so + 5],
                NEGH[:, 16:17], op=ALU.pow),
                reads=(("stats", so + 4, so + 5), ("NEGH", 16, 17)),
                writes=(("stats", so + 5, so + 6),), wait=[mxc])
            p.op("P", lambda g, so=so: g.tensor_tensor(
                stats[:, so + 6:so + 7], stats[:, so + 5:so + 6],
                NEGH[:, 17:18], op=ALU.mult),
                reads=(("stats", so + 5, so + 6), ("NEGH", 17, 18)),
                writes=(("stats", so + 6, so + 7),))
            sc_ms = p.op("P", lambda g, so=so: g.tensor_tensor(
                stats[:, so + 7:so + 8], stats[:, so + 4:so + 5],
                NEGH[:, 18:19], op=ALU.mult),
                reads=(("stats", so + 4, so + 5), ("NEGH", 18, 19)),
                writes=(("stats", so + 7, so + 8),), inc=True)
            xq_ms = None
            sl = lambda st: (("UU", st, st + 512), ("IA", st, st + 512))
            for step in range(5):
                for sp_ in range(2):
                    st = sp_ * 512
                    lf = LN3F[:, st:st + 512]
                    uu = UU[:, st:st + 512]
                    ia = IA[:, st:st + 512]
                    if step == 0:
                        uwl = None
                        if sp_ == 0:
                            uwl = [sc_ms]
                        p.op("V", lambda v, lf=lf, uu=uu, so=so:
                             v.tensor_scalar(
                                 uu, lf, stats[:, so + 6:so + 7], 0.5,
                                 op0=ALU.mult, op1=ALU.add),
                             reads=(("LN3F", st, st + 512),
                                    ("stats", so + 6, so + 7)),
                             writes=(("UU", st, st + 512),), wait=uwl)
                    elif step == 1:
                        p.op("V", lambda v, uu=uu, ia=ia:
                             v.tensor_copy(ia, uu),
                             reads=(("UU", st, st + 512),),
                             writes=(("IA", st, st + 512),))
                    elif step == 2:
                        p.op("V", lambda v, uu=uu, ia=ia: v.tensor_tensor(
                            uu, uu, ia, op=ALU.subtract),
                            reads=sl(st), writes=(("UU", st, st + 512),))
                    elif step == 3:
                        p.op("V", lambda v, uu=uu, ia=ia:
                             v.scalar_tensor_tensor(
                                 ia, uu, 0.0, ia, op0=ALU.is_ge,
                                 op1=ALU.add),
                             reads=sl(st), writes=(("IA", st, st + 512),))
                    else:
                        xwl = None
                        if sp_ == 0:
                            prev_key = (n, 0) if jj == 1 else (n - 1, 1)
                            if prev_key in xq_evs:
                                xwl = [xq_evs[prev_key]]
                        xq_ms = p.op("V", lambda v, ia=ia, st=st, so=so:
                                     v.tensor_scalar(
                                         XQTM[:, st:st + 512], ia, -1.0,
                                         stats[:, so + 7:so + 8],
                                         op0=ALU.add, op1=ALU.mult),
                                     reads=(("IA", st, st + 512),
                                            ("stats", so + 7, so + 8)),
                                     writes=(("XQTM", st, st + 512),),
                                     wait=xwl, inc=(sp_ == 1))
            ln3f_free[0] = xq_ms
            bk = "A" if jj == 0 else "B"
            bank = ptA if jj == 0 else ptB
            wl2 = [xq_ms]
            if pt_free[bk] is not None:
                wl2.append(pt_free[bk])
            tms = None
            for c in range(8):
                tms = p.op("T", lambda t, bank=bank, c=c: t.transpose(
                    bank[:, c * 128:(c + 1) * 128],
                    XQTM[:, c * 128:(c + 1) * 128], IDENT[:]),
                    reads=("XQTM", "IDENT"), writes=("PT",),
                    wait=wl2 if c == 0 else None, inc=(c == 7))
            ev = p.op("S", lambda s, bank=bank, jj=jj: s.activation(
                XQFM[:].rearrange("p (c j2 nn) -> p c j2 nn", j2=2, nn=128)
                [:, :, jj, :],
                bank[:, 0:1024].rearrange("p (c nn) -> p c nn", nn=128),
                AF.Identity),
                reads=("PT",), writes=("XQFM",), wait=[tms], inc=True)
            xq_evs[(n, jj)] = ev
            pt_free[bk] = ev
            tp_evs.append(ev)
        WST[n] = {"tp": tp_evs, "sil": {}, "add2": {}, "tree": {}}

    def emit_conv_group(n, g):
        pb = n % 2
        HT = HTS[pb]
        htn = HTnm[pb]
        htv = HT.rearrange("p (c e) -> p c e", e=258)
        st_ = WST[n]
        sil_ms = st_["sil"]
        add2_ms = st_["add2"]
        tree_ms = st_["tree"]
        nxtv = None
        if n + 1 < NW:
            nxtv = HTS[(n + 1) % 2].rearrange("p (c e) -> p c e", e=258)
        if nxtv is not None:
            p.op("V", lambda v, g=g, nxtv=nxtv, htv=htv: v.tensor_copy(
                htv[:, g * 8:(g + 1) * 8, 257:258],
                nxtv[:, g * 8:(g + 1) * 8, 1:2]),
                reads=(HTnm[(n + 1) % 2],), writes=(htn,),
                wait=[evac_ms[(n + 1, g * 8 + 7)]])
            hl = p.op("V", lambda v, g=g, nxtv=nxtv, htv=htv:
                      v.tensor_copy(
                          nxtv[:, g * 8:(g + 1) * 8, 0:1],
                          htv[:, g * 8:(g + 1) * 8, 256:257]),
                      reads=(htn,), writes=(HTnm[(n + 1) % 2],),
                      wait=[evac_ms[(n, g * 8 + 7)]], inc=(g == 3))
            if g == 3:
                halo_ms[n + 1] = hl
        elif g == 0:
            p.op("V", lambda v, htv=htv: v.memset(htv[:, :, 257:258],
                                                   0.0),
                 writes=(htn,))
        for c in range(g * 8, g * 8 + 8):
            s4 = c % 2
            a0 = A0B[:, s4 * 256:(s4 + 1) * 256]
            l0 = ("A0B", s4 * 256, (s4 + 1) * 256)
            swl = []
            if c == 0:
                swl.append(halo_ms[n])
            if c >= 2:
                swl.append(add2_ms[c - 2])
            t0 = p.op("S", lambda s, c=c, a0=a0, HT=HT: s.activation(
                a0, hmid(HT, c, 0, 256), AF.Identity,
                scale=CW[:, c * 3:c * 3 + 1], bias=CB[:, c:c + 1]),
                reads=((htn, c * 258, c * 258 + 256), "CW", "CB"),
                writes=(l0,), wait=swl or None, inc=True)
            p.op("V", lambda v, c=c, a0=a0, HT=HT: v.scalar_tensor_tensor(
                a0, hmid(HT, c, 1, 257), CW[:, c * 3 + 1:c * 3 + 2], a0,
                op0=ALU.mult, op1=ALU.add),
                reads=((htn, c * 258 + 1, c * 258 + 257), "CW", l0),
                writes=(l0,), wait=[t0])
            a_ms = p.op("V", lambda v, c=c, a0=a0, HT=HT:
                        v.scalar_tensor_tensor(
                            a0, hmid(HT, c, 2, 258),
                            CW[:, c * 3 + 2:c * 3 + 3], a0,
                            op0=ALU.mult, op1=ALU.add),
                        reads=((htn, c * 258 + 2, c * 258 + 258), "CW", l0),
                        writes=(l0,), inc=True)
            add2_ms[c] = a_ms
            sil_ms[c] = p.op("S", lambda s, c=c, a0=a0, HT=HT: s.activation(
                hmid(HT, c, 1, 257), a0, AF.Silu),
                reads=(l0,),
                writes=((htn, c * 258 + 1, c * 258 + 257),),
                wait=[a_ms], inc=True)

        for pr in range(4 * g, 4 * g + 4):
            s2 = pr % 2
            abp = ABP[:, s2 * 512:(s2 + 1) * 512]
            lab = ("ABQ", s2 * 512, (s2 + 1) * 512)
            awl = []
            if pr >= 2:
                awl.append(tree_ms[pr - 2])
            elif m2done.get(n - 1) is not None:
                awl.append(m2done[n - 1])
            ab = p.op("S", lambda s, pr=pr, abp=abp: s.activation(
                abp, htv[:, 2 * pr:2 * pr + 2, 1:257], AF.Abs),
                reads=((htn, 2 * pr * 258, (2 * pr + 2) * 258),),
                writes=(lab,), wait=awl or None, inc=True)
            lane = pr % 2
            dst = RMA if lane == 0 else RMB
            dn = "RMA" if lane == 0 else "RMB"
            if pr < 2:
                tree_ms[pr] = p.op("V", lambda v, dst=dst, abp=abp:
                                   v.tensor_copy(dst[:], abp),
                                   reads=(lab,), writes=(dn,),
                                   wait=[ab], inc=True)
            else:
                tree_ms[pr] = p.op("V", lambda v, dst=dst, abp=abp:
                                   v.tensor_tensor(dst[:], dst[:], abp,
                                                   op=ALU.max),
                                   reads=(lab, dn), writes=(dn,),
                                   wait=[ab], inc=True)

    def emit_pipe(n):
        pb = n % 2
        so = 48 * pb + 16
        st_ = WST[n]
        tree_ms = st_["tree"]
        p.op("V", lambda v: v.tensor_tensor(RMA[:], RMA[:], RMB[:],
                                            op=ALU.max),
             reads=("RMA", "RMB"), writes=("RMA",),
             wait=[tree_ms[14], tree_ms[15]])
        rb = p.op("V", lambda v: v.tensor_tensor(
            RMF[:], RMA[:, 0:256], RMA[:, 256:512], op=ALU.max),
            reads=("RMA",), writes=(("RMB", 0, 256),), inc=True)

        wl2 = [rb]
        if pt_free["A"] is not None:
            wl2.append(pt_free["A"])
        tms = None
        for jj in range(2):
            tms = p.op("T", lambda t, jj=jj: t.transpose(
                ptA[:, jj * 128:(jj + 1) * 128],
                RMF[:, jj * 128:(jj + 1) * 128], IDENT[:]),
                reads=(("RMB", 0, 256), "IDENT"), writes=("PT",),
                wait=wl2 if jj == 0 else None, inc=(jj == 1))
        mxc = p.op("V", lambda v, so=so: v.reduce_max(
            stats[:, so:so + 2].rearrange("p (j2 one) -> p j2 one", one=1),
            ptA[:, 0:256].rearrange("p (j2 nn) -> p j2 nn", nn=128),
            axis=AX.X),
            reads=("PT",), writes=(("stats", so, so + 2),), wait=[tms],
            inc=True)
        pt_free["A"] = mxc
        clip = p.op("V", lambda v, so=so: v.tensor_scalar_max(
            stats[:, so + 2:so + 4], stats[:, so:so + 2], 1e-5),
            reads=(("stats", so, so + 2),),
            writes=(("stats", so + 2, so + 4),), inc=True)
        p.op("P", lambda g, so=so: g.tensor_tensor(
            stats[:, so + 4:so + 6], stats[:, so + 2:so + 4],
            NEGH[:, 19:21], op=ALU.pow),
            reads=(("stats", so + 2, so + 4), ("NEGH", 19, 21)),
            writes=(("stats", so + 4, so + 6),), wait=[clip])
        s127p = p.op("P", lambda g, so=so: g.tensor_tensor(
            stats[:, so + 6:so + 8], stats[:, so + 4:so + 6],
            NEGH[:, 21:23], op=ALU.mult),
            reads=(("stats", so + 4, so + 6), ("NEGH", 21, 23)),
            writes=(("stats", so + 6, so + 8),), inc=True)
        p.op("P", lambda g, so=so: g.tensor_tensor(
            stats[:, so + 8:so + 10], stats[:, so + 2:so + 4],
            NEGH[:, 23:25], op=ALU.mult),
            reads=(("stats", so + 2, so + 4), ("NEGH", 23, 25)),
            writes=(("stats", so + 8, so + 10),))
        c2a = p.op("P", lambda g, so=so: g.tensor_tensor(
            stats[:, so + 10:so + 11], stats[:, so + 8:so + 9],
            WSC[:, 3:4], op=ALU.mult),
            reads=(("stats", so + 8, so + 9), "WSC"),
            writes=(("stats", so + 10, so + 11),), inc=True)
        c2_ms = p.op("P", lambda g, so=so: g.tensor_tensor(
            stats[:, so + 11:so + 12], stats[:, so + 9:so + 10],
            WSC[:, 3:4], op=ALU.mult),
            reads=(("stats", so + 9, so + 10), "WSC"),
            writes=(("stats", so + 11, so + 12),), inc=True)
        sb2_ms = p.op("S", lambda s, so=so: s.activation(
            SB2S[:], stats[:, so + 6:so + 8], AF.Identity),
            reads=(("stats", so + 6, so + 8),), writes=("SB2S",),
            wait=[s127p], inc=True)
        wl3 = [sb2_ms]
        if pt_free["B"] is not None:
            wl3.append(pt_free["B"])
        st2 = p.op("T", lambda t: t.transpose(ptB[0:2, 0:128], SB2S[:],
                                              IDENT[:]),
                   reads=("SB2S", "IDENT"), writes=("PT",), wait=wl3,
                   inc=True)
        sb2ev = p.op("S", lambda s: s.activation(SB2[:], ptB[0:2, 0:128],
                                                 AF.Identity),
                     reads=("PT",), writes=("SB2",), wait=[st2], inc=True)
        pt_free["B"] = sb2ev
        sr_dma = p.dma("SP", SROWB[0:1, :], SB2[:], wait=[sb2ev],
                       sem="srd")
        bwl = [sr_dma]
        if ffo_done.get(n - 1) is not None:
            bwl.append(ffo_done[n - 1])
        if slot_free[2] is not None:
            bwl.append(slot_free[2])
        p.op("T", lambda t: t.matmul(
            slot(2)[:, 0:256], ONE1[:], SROWB[0:1, :], start=True,
            stop=True),
            reads=("ONE1", "SROWB"), writes=("PS",), wait=bwl)
        bc_mm = p.op("T", lambda t: t.matmul(
            slot(2)[:, 256:512], ONE1[:], SROWB[0:1, :], start=True,
            stop=True),
            reads=("ONE1", "SROWB"), writes=("PS",), inc=True)
        sbc_ms = p.op("S", lambda s: s.activation(SBCS[:], slot(2)[:, :],
                                                  AF.Identity),
                      reads=("PS",), writes=("RMA",), wait=[bc_mm],
                      inc=True)
        st_["sbc"] = sbc_ms
        st_["c2"] = c2_ms

    def emit_mm2_out(n):
        pb = n % 2
        HT = HTS[pb]
        htn = HTnm[pb]
        htv = HT.rearrange("p (c e) -> p c e", e=258)
        X1 = X1T[:, pb * 2048:(pb + 1) * 2048]
        so = 48 * pb + 16
        lb = 48 * pb + 28
        st_ = WST[n]
        sil_ms = st_["sil"]
        sbc_ms = st_["sbc"]
        c2_ms = st_["c2"]
        m2slots = [3, 4, 5, 2]
        mm_last = {}
        hq_ms = {}
        for pg in range(8):
            for step in range(5):
                for k in range(2):
                    pr = 2 * pg + k
                    st = k * 512
                    uu = UU[:, st:st + 512]
                    ia = IA[:, st:st + 512]
                    luu = ("UU", st, st + 512)
                    lia = ("IA", st, st + 512)
                    lhq = ("ABQ", st, st + 512)
                    if step == 0:
                        wl = [sil_ms[2 * pr + 1]]
                        if pr == 0:
                            wl.append(sbc_ms)
                        v1 = p.op("V", lambda v, pr=pr, uu=uu:
                                  v.tensor_tensor(
                                      uu, htv[:, 2 * pr:2 * pr + 2, 1:257],
                                      SBCS[:], op=ALU.mult),
                                  reads=((htn, 2 * pr * 258,
                                          (2 * pr + 2) * 258), "RMA"),
                                  writes=(luu,), wait=wl,
                                  inc=(pr == 15))
                        if pr == 15:
                            m2v1_last[n] = v1
                    elif step == 1:
                        p.op("V", lambda v, uu=uu, ia=ia: v.tensor_scalar(
                            ia, uu, 0.5, None, op0=ALU.add),
                            reads=(luu,), writes=(lia,))
                    elif step == 2:
                        p.op("V", lambda v, uu=uu, ia=ia:
                             v.scalar_tensor_tensor(
                                 uu, uu, 0.5, ia, op0=ALU.add,
                                 op1=ALU.subtract),
                             reads=(luu, lia), writes=(luu,))
                    elif step == 3:
                        p.op("V", lambda v, uu=uu, ia=ia:
                             v.scalar_tensor_tensor(
                                 ia, uu, 0.0, ia, op0=ALU.is_ge,
                                 op1=ALU.add),
                             reads=(luu, lia), writes=(lia,))
                    else:
                        hwl = [mm_last[pr - 2]] if pr >= 2 else None
                        hq_ms[pr] = p.op("V", lambda v, ia=ia, st=st:
                                         v.tensor_scalar(
                                             HQ[:, st:st + 512], ia, -1.0,
                                             None, op0=ALU.add),
                                         reads=(lia,), writes=(lhq,),
                                         wait=hwl, inc=True)
            for k in range(2):
                pr = 2 * pg + k
                st = k * 512
                lhq = ("ABQ", st, st + 512)
                last = None
                for ci in range(2):
                    c = 2 * pr + ci
                    for jj2 in range(2):
                        for g_i in range(2):
                            si = m2slots[jj2 * 2 + g_i]
                            wlm = None
                            if ci == 0 and jj2 == 0 and g_i == 0:
                                wlm = [hq_ms[pr]]
                                if c == 0:
                                    if ffo_done.get(n - 1) is not None:
                                        wlm.append(ffo_done[n - 1])
                                    for s_ in (3, 4, 5):
                                        if slot_free[s_] is not None:
                                            wlm.append(slot_free[s_])
                            if c == 0 and si == 2 and wlm is None:
                                wlm = [sbc_ms]
                            last = p.op("T", lambda t, si=si, c=c, jj2=jj2,
                                        g_i=g_i, st=st: t.matmul(
                                            slot(si),
                                            HQ[:, st + (c % 2) * 256 +
                                               jj2 * 128:
                                               st + (c % 2) * 256 +
                                               (jj2 + 1) * 128],
                                            W2T[:, c * 1024 + g_i * 512:
                                                c * 1024 + (g_i + 1) * 512],
                                            start=(c == 0), stop=(c == 31)),
                                        reads=(lhq, "WBIG"), writes=("PS",),
                                        wait=wlm,
                                        inc=(ci == 1 and jj2 == 1
                                             and g_i == 1))
                mm_last[pr] = last
        m2done[n] = mm_last[15]

        prev_out = None
        out_ms = []
        for jj2 in range(2):
            l4 = lb + 4 * jj2
            evs = []
            for g_i in range(2):
                si = m2slots[jj2 * 2 + g_i]
                evw = [mm_last[15], c2_ms]
                if jj2 == 0 and g_i == 0 and ffo_done.get(n - 1) is not None:
                    evw.append(ffo_done[n - 1])
                if jj2 == 1 and g_i == 0:
                    evw.append(prev_out)
                ev = p.op("S", lambda s, si=si, g_i=g_i, jj2=jj2, so=so:
                          s.activation(
                              FFO[:, g_i * 512:(g_i + 1) * 512], slot(si),
                              AF.Identity,
                              scale=stats[:, so + 10 + jj2:so + 11 + jj2]),
                          reads=("PS", ("stats", so + 10, so + 12)),
                          writes=(("FFO", g_i * 512, (g_i + 1) * 512),),
                          wait=evw, inc=True)
                evs.append(ev)
                slot_free[si] = ev
            p.op("V", lambda v: v.bn_stats(bn6[:, 0:6], FFO[:, 0:512]),
                 reads=(("FFO", 0, 512),), writes=(("bn6", 0, 6),),
                 wait=evs)
            p.op("V", lambda v: v.bn_stats(bn6[:, 6:12], FFO[:, 512:1024]),
                 reads=(("FFO", 512, 1024),), writes=(("bn6", 6, 12),))
            p.op("V", lambda v: v.bn_aggr(bn2[:], bn6[:]),
                 reads=("bn6",), writes=("bn2",))
            ve = p.op("V", lambda v, l4=l4: v.tensor_scalar(
                stats[:, l4:l4 + 1], bn2[:, 1:2], EPS, None, op0=ALU.add),
                reads=("bn2",), writes=(("stats", l4, l4 + 1),), inc=True)
            pw = p.op("P", lambda g, l4=l4: g.tensor_tensor(
                stats[:, l4 + 1:l4 + 2], stats[:, l4:l4 + 1],
                NEGH[:, 0:1], op=ALU.pow),
                reads=(("stats", l4, l4 + 1), ("NEGH", 0, 1)),
                writes=(("stats", l4 + 1, l4 + 2),), wait=[ve], inc=True)
            bc = p.op("V", lambda v, l4=l4: v.scalar_tensor_tensor(
                stats[:, l4 + 2:l4 + 3], bn2[:, 0:1], -1.0,
                stats[:, l4 + 1:l4 + 2], op0=ALU.mult, op1=ALU.mult),
                reads=("bn2", ("stats", l4 + 1, l4 + 2)),
                writes=(("stats", l4 + 2, l4 + 3),), wait=[pw], inc=True)
            lnm = p.op("S", lambda s, l4=l4: s.activation(
                FFO[:], FFO[:], AF.Identity,
                bias=stats[:, l4 + 2:l4 + 3],
                scale=stats[:, l4 + 1:l4 + 2]),
                reads=("FFO", ("stats", l4, l4 + 3)), writes=("FFO",),
                wait=[bc], inc=True)
            om = p.op("V", lambda v, jj2=jj2, X1=X1: v.tensor_tensor(
                X1[:, jj2 * 1024:(jj2 + 1) * 1024],
                X1[:, jj2 * 1024:(jj2 + 1) * 1024], FFO[:], op=ALU.add),
                reads=(("X1T", pb * 2048 + jj2 * 1024,
                        pb * 2048 + (jj2 + 1) * 1024), "FFO"),
                writes=(("X1T", pb * 2048 + jj2 * 1024,
                         pb * 2048 + (jj2 + 1) * 1024),),
                wait=[lnm], inc=True)
            out_ms.append(om)
            prev_out = om
        ffo_done[n] = out_ms[-1]
        win_out_dma[n] = p.dma(
            "SP", dy[n * 256:(n + 1) * 256, :].rearrange(
                "(jj pp) nn -> pp jj nn", pp=128),
            X1.rearrange("p (jj nn) -> p jj nn", nn=1024),
            wait=out_ms, sem="od")

    for n in range(NW + 2):
        if n >= 2:
            emit_mm2_out(n - 2)
        if n <= NW - 1:
            emit_quant(n)
            pb = n % 2
            HT = HTS[pb]
            tp_evs = WST[n]["tp"]
            lastev = None
            for hfc in range(32):
                si = hfc % 2
                wl = list(tp_evs) if hfc < 2 else []
                if slot_free[si] is not None:
                    wl.append(slot_free[si])
                mms = None
                for c in range(8):
                    mms = p.op("T", lambda t, si=si, c=c, hfc=hfc:
                               t.matmul(
                                   slot(si)[:, 0:256],
                                   W1T[:, c * 4096 + hfc * 128:
                                       c * 4096 + (hfc + 1) * 128],
                                   XQFM[:, c * 256:(c + 1) * 256],
                                   start=(c == 0), stop=(c == 7)),
                               reads=("WBIG", "XQFM"),
                               writes=("PS",),
                               wait=(wl or None) if c == 0 else None,
                               inc=(c == 7))
                ewl = [mms]
                if hfc == 0 and (n - 2) in m2v1_last:
                    ewl.append(m2v1_last[n - 2])
                lastev = p.op("S", lambda s, si=si, hfc=hfc, HT=HT:
                              s.activation(
                                  hmid(HT, hfc, 1, 257),
                                  slot(si)[:, 0:256], AF.Identity),
                              reads=("PS",),
                              writes=((HTnm[pb], hfc * 258 + 1,
                                       hfc * 258 + 257),),
                              wait=ewl, inc=True)
                slot_free[si] = lastev
                evac_ms[(n, hfc)] = lastev
                if hfc % 8 == 7 and n >= 1:
                    emit_conv_group(n - 1, hfc // 8)
            mm1_ev[n] = lastev
        elif n == NW:
            for g in range(4):
                emit_conv_group(NW - 1, g)
        if 1 <= n <= NW:
            emit_pipe(n - 1)

    p.op("SP", lambda s, v=p.e["SP"].dma_val: s.wait_ge(sems["SPd"], v))
    p.op("P", lambda g, v=p.e["P"].dma_val: g.wait_ge(sems["Pd"], v))
    for k in ("xd0", "xd1", "td", "x1d", "wd0", "wd1", "cwd", "srd", "od"):
        p.op("SP", lambda sp, k=k, v=p.dsem.get(k, 0): sp.wait_ge(
            sems[k], v))

    with nc.Block() as blk3:
        p.run(blk3)
    ctx3.close()

    return nc


def kernel(L0_attn_x, L0_attn_Wq, L0_attn_Wk, L0_attn_Wv, L0_attn_Wg,
           L0_attn_Wo, L0_attn_g_norm_w, L0_ffn_w1, L0_ffn_conv_w,
           L0_ffn_conv_b, L0_ffn_w2):
    x = np.ascontiguousarray(np.asarray(L0_attn_x, dtype=np.float32))
    B, T, D_ = x.shape
    assert (B, D_) == (NCORES, D)

    consts = make_consts(T)
    nc = build_program(T)

    common = {
        "wq": np.ascontiguousarray(L0_attn_Wq, np.float32),
        "wk": np.ascontiguousarray(L0_attn_Wk, np.float32),
        "wv": np.ascontiguousarray(L0_attn_Wv, np.float32),
        "wg": np.ascontiguousarray(L0_attn_Wg, np.float32),
        "wo": np.ascontiguousarray(L0_attn_Wo, np.float32),
        "gnw": np.ascontiguousarray(
            np.asarray(L0_attn_g_norm_w, np.float32).reshape(DV, 1)),
        "w1": np.ascontiguousarray(L0_ffn_w1, np.float32),
        "cw": np.ascontiguousarray(
            np.asarray(L0_ffn_conv_w, np.float32).reshape(DH, 3)),
        "cb": np.ascontiguousarray(
            np.asarray(L0_ffn_conv_b, np.float32).reshape(DH, 1)),
        "w2": np.ascontiguousarray(L0_ffn_w2, np.float32),
        **consts,
    }
    in_maps = []
    for i in range(NCORES):
        m = dict(common)
        m["x"] = np.ascontiguousarray(x[i])
        in_maps.append(m)

    res = run_bass_kernel_spmd(nc, in_maps, core_ids=list(range(NCORES)))
    out = np.stack([np.asarray(r["y"], np.float32) for r in res.results])
    return out


# version 08
# speedup vs baseline: 1.1682x; 1.0417x over previous
from contextlib import ExitStack

import numpy as np
import ml_dtypes
import concourse.bass as bass
import concourse.mybir as mybir
from concourse.bass_utils import run_bass_kernel_spmd

F32 = mybir.dt.float32
BF16 = mybir.dt.bfloat16
I32 = mybir.dt.int32
AF = mybir.ActivationFunctionType
ALU = mybir.AluOpType
AX = mybir.AxisListType

D = 1024
H = 16
DK = 64
DV = 128
C = 128
DH = 4096
EPS = 1e-5
NCORES = 8

ENGS = ("T", "V", "S", "P", "SP")


class MS:
    __slots__ = ("eng", "val")

    def __init__(self, eng, val):
        self.eng, self.val = eng, val


class _EngState:
    def __init__(self, name):
        self.name = name
        self.ops = []
        self.sem_val = 0
        self.dma_val = 0
        self.pend_w = []
        self.pend_r = []


def _ivl(label):
    if isinstance(label, tuple):
        return label
    return (label, None, None)


def _conflict(a, b):
    if a[0] != b[0]:
        return False
    if a[1] is None or b[1] is None:
        return True
    return a[1] < b[2] and b[1] < a[2]


class Plan:
    def __init__(self, nc, sems, carry=None):
        self.nc = nc
        self.sems = sems
        self.e = {n: _EngState(n) for n in ENGS}
        self.dsem = {}
        if carry is not None:
            for n in ENGS:
                self.e[n].sem_val = carry.e[n].sem_val
                self.e[n].dma_val = carry.e[n].dma_val
            self.dsem = dict(carry.dsem)

    def _emit(self, eng, fn):
        self.e[eng].ops.append(fn)

    def _auto_drain(self, eng, reads, writes):
        st = self.e[eng]
        rds = [_ivl(x) for x in reads]
        wrs = [_ivl(x) for x in writes]
        hit = any(_conflict(w, r) for w in st.pend_w for r in rds) \
            or any(_conflict(w, v) for w in st.pend_w for v in wrs) \
            or any(_conflict(r, v) for r in st.pend_r for v in wrs)
        if hit:
            self._emit(eng, lambda obj: obj.drain())
            st.pend_w = []
            st.pend_r = []
        if len(st.pend_w) > 96:
            st.pend_w = [(n, None, None) for n in
                         {t[0] for t in st.pend_w}]
        if len(st.pend_r) > 96:
            st.pend_r = [(n, None, None) for n in
                         {t[0] for t in st.pend_r}]
        st.pend_w += wrs
        st.pend_r += rds

    def wait(self, eng, ms):
        if ms is None:
            return
        if isinstance(ms, MS):
            ms = [ms]
        for m in ms:
            if m is None:
                continue
            sem = self.sems[m.eng]
            val = m.val
            if m.eng in ("SPd", "Pd"):
                val = self.e[m.eng[:-1]].dma_val
            self._emit(eng, lambda obj, s=sem, v=val: obj.wait_ge(s, v))

    def op(self, eng, fn, reads=(), writes=(), wait=None, inc=False,
           nodrain=False):
        self.wait(eng, wait)
        if not nodrain:
            self._auto_drain(eng, reads, writes)
        st = self.e[eng]
        if inc:
            st.sem_val += 1
            sem = self.sems[eng]

            def wrapped(obj, fn=fn, sem=sem):
                fn(obj).then_inc(sem, 1)
            self._emit(eng, wrapped)
            return MS(eng, st.sem_val)
        self._emit(eng, lambda obj, fn=fn: fn(obj))
        return None

    def dma(self, eng, out_ap, in_ap, wait=None, slow=False, sem=None):
        self.wait(eng, wait)
        key = sem if sem is not None else eng + "d"
        if key in (eng + "d",):
            self.e[eng].dma_val += 16
            val = self.e[eng].dma_val
        else:
            self.dsem[key] = self.dsem.get(key, 0) + 16
            val = self.dsem[key]
        h = self.sems[key]
        self._emit(eng, lambda obj, o=out_ap, i=in_ap, s=h, sl=slow:
                   obj.dma_start(out=o, in_=i,
                                 allow_slow_non_contiguous=sl)
                   .then_inc(s, 16))
        return MS(key, val)

    def drain(self, eng):
        self._emit(eng, lambda obj: obj.drain())
        self.e[eng].pend_w = []
        self.e[eng].pend_r = []

    def barrier_ms(self, eng):
        st = self.e[eng]
        st.sem_val += 1
        sem = self.sems[eng]
        self._emit(eng, lambda obj, s=sem: obj.sem_inc(s, 1))
        return MS(eng, st.sem_val)

    def run(self, block):
        def mk(name):
            def body(obj):
                for fn in self.e[name].ops:
                    fn(obj)
            return body
        block.tensor(mk("T"))
        block.vector(mk("V"))
        block.scalar(mk("S"))
        block.gpsimd(mk("P"))
        block.sync(mk("SP"))


def make_consts(T):
    h = np.arange(H, dtype=np.float64)
    gamma = 1.0 - np.exp2(-5.0 - h)
    j = np.arange(C, dtype=np.float64)

    inv = 1.0 / 10000.0 ** (np.arange(0, DK, 2, dtype=np.float64) / DK)
    t = np.arange(T, dtype=np.float64)
    f = np.outer(t, inv)
    costab = np.tile(np.cos(f), (1, H)).astype(np.float32)
    sintab = np.tile(np.sin(f), (1, H)).astype(np.float32)

    dkve = np.repeat(gamma[None, :] ** (C - 1 - j[:, None]), DK,
                     axis=1).astype(np.float32)
    i_ = np.arange(C, dtype=np.float64)
    dq2 = (np.repeat(gamma[None, :] ** (i_[:, None] + 1 - C), DV, axis=1)
           / 8.0).astype(np.float32)
    gct = np.ascontiguousarray(
        np.repeat(np.broadcast_to(gamma[None, :] ** C, (DK, H)), DV,
                  axis=1)).astype(np.float32)
    m = (np.arange(C)[None, :] >= np.arange(C)[:, None]).astype(np.float32)
    mask2 = np.ascontiguousarray(np.concatenate([m, m], axis=1))
    ident = np.eye(128, dtype=ml_dtypes.bfloat16)
    ones1 = np.ones((1, 128), dtype=ml_dtypes.bfloat16)
    return {"costab": costab, "sintab": sintab, "dkve": dkve, "dq2": dq2,
            "gct": gct, "mask2": mask2, "ident": ident, "ones1": ones1}


def build_program(T=4096):
    assert T % 256 == 0
    NB = T // 128
    NW = T // 256

    nc = bass.Bass("TRN2", target_bir_lowering=False)

    din = {}
    for nm, shape, dt in [
            ("x", [T, D], F32), ("wq", [D, D], F32), ("wk", [D, D], F32),
            ("wv", [D, 2 * D], F32), ("wg", [D, 2 * D], F32),
            ("wo", [2 * D, D], F32), ("gnw", [DV, 1], F32),
            ("w1", [DH, D], F32), ("cw", [DH, 3], F32), ("cb", [DH, 1], F32),
            ("w2", [D, DH], F32), ("costab", [T, 512], F32),
            ("sintab", [T, 512], F32), ("dkve", [128, 1024], F32),
            ("dq2", [128, 2048], F32), ("gct", [64, 2048], F32),
            ("mask2", [128, 256], F32), ("ident", [128, 128], BF16),
            ("ones1", [1, 128], BF16)]:
        din[nm] = nc.dram_tensor(nm, shape, dt, kind="ExternalInput").ap()

    dy = nc.dram_tensor("y", [T, D], F32, kind="ExternalOutput").ap()
    dx1 = nc.dram_tensor("x1s", [T, D], F32, kind="Internal").ap()
    dw1t = nc.dram_tensor("w1t", [D, DH], BF16, kind="Internal").ap()
    dw2t = nc.dram_tensor("w2t", [DH, D], BF16, kind="Internal").ap()

    psb = lambda nm, shape, dt=F32: nc.alloc_sbuf_tensor(
        nm, list(shape), dt).ap()
    WBIG = psb("WBIG", (128, 65536), BF16)
    WQ = WBIG[:, 0:8192]
    WK = WBIG[:, 8192:16384]
    WV = WBIG[:, 16384:32768]
    WG = WBIG[:, 32768:49152]
    WO = WBIG[:, 49152:65536]
    W1T = WBIG[:, 0:32768]
    W2T = WBIG[:, 32768:65536]

    IDENT = psb("IDENT", (128, 128), BF16)
    ONE1 = psb("ONE1", (1, 128), BF16)
    ONE1F = psb("ONE1F", (1, 128))
    NEGH = psb("NEGH", (128, 26))
    stats = psb("stats", (128, 96))
    bn6 = psb("bn6", (128, 12))
    bn2 = psb("bn2", (128, 2))
    WSC = psb("WSC", (128, 8))
    CW = psb("CW", (128, 96))
    CB = psb("CB", (128, 32))

    PS = nc.alloc_psum_tensor("PS", [128, 3072], F32).ap()
    PT = nc.alloc_psum_tensor("PT", [128, 2048], BF16).ap()
    ptA = PT[:, 0:1024]
    ptB = PT[:, 1024:2048]

    def slot(i):
        return PS[:, 512 * i:512 * (i + 1)]

    sems = {}
    for n in ["T", "V", "S", "P", "SP", "SPd", "Pd", "xd0", "xd1", "td",
              "x1d", "wd0", "wd1", "cwd", "srd", "od"]:
        sems[n] = nc.semaphore("sem_" + n).__enter__()

    ctx1 = ExitStack()
    s1 = lambda nm, shape, dt=F32: ctx1.enter_context(
        nc.sbuf_tensor(nm, list(shape), dt)).ap()

    xt = s1("xt", (128, 2 * D))
    ln1 = s1("ln1", (128, 2 * D), BF16)
    QKS = s1("QKS", (128, D))
    qrt = s1("qrt", (128, D), BF16)
    krt = s1("krt", (128, D), BF16)
    ropem = s1("ropem", (128, 1024), BF16)
    qfm = s1("qfm", (64, 2048), BF16)
    kfm = s1("kfm", (64, 2048), BF16)
    vtm = s1("vtm", (128, 2048), BF16)
    gsil = s1("gsil", (128, 2048), BF16)
    Ab = s1("Ab", (128, 512), BF16)
    Sst = s1("Sst", (64, 2048))
    Sbb = s1("Sbb", (64, 2048), BF16)
    otm = s1("otm", (128, 2048), BF16)
    og = s1("og", (128, 2048), BF16)
    ogfm = og
    ln2b = s1("ln2b", (128, D), BF16)
    ctab = s1("ctab", (128, 512))
    stab = s1("stab", (128, 512))
    DKVEB = s1("DKVEB", (128, 1024), BF16)
    DQ2B = s1("DQ2B", (128, 2048), BF16)
    GCTB = s1("GCTB", (64, 2048), BF16)
    MASKB = s1("MASKB", (128, 256), BF16)
    GNCOL = s1("GNCOL", (128, 1))

    p = Plan(nc, sems)

    p.dma("SP", IDENT, din["ident"])
    p.dma("SP", ONE1, din["ones1"])
    p.dma("SP", GNCOL, din["gnw"])
    p.dma("SP", xt[:, 0:1024], din["dkve"])
    p.dma("SP", ctab[:, 0:256], din["mask2"])
    p.dma("SP", xt[:, 1024:2048], din["dq2"][:, 0:1024])
    call = p.dma("SP", Sst[0:64, 0:2048], din["gct"])
    dk_ms = p.op("V", lambda v: v.tensor_copy(DKVEB[:], xt[:, 0:1024]),
                 reads=("xt",), writes=("DKVEB",), wait=[call], inc=True)
    p.op("V", lambda v: v.tensor_copy(MASKB[:], ctab[:, 0:256]),
         reads=("ctab",), writes=("MASKB",))
    p.op("V", lambda v: v.tensor_copy(DQ2B[:, 0:1024], xt[:, 1024:2048]),
         reads=("xt",), writes=("DQ2B",))
    d2h = p.dma("SP", xt[:, 0:1024], din["dq2"][:, 1024:2048],
                wait=[dk_ms])
    p.op("V", lambda v: v.tensor_copy(DQ2B[:, 1024:2048], xt[:, 0:1024]),
         reads=("xt",), writes=("DQ2B",), wait=[d2h])
    cast4 = p.op("V", lambda v: v.tensor_copy(GCTB[:], Sst[0:64, 0:2048]),
                 reads=("Sst",), writes=("GCTB",), inc=True)
    consts_done = cast4

    wms = []
    for dst, src, w in [(WQ, din["wq"], 1024), (WK, din["wk"], 1024),
                        (WV, din["wv"], 2048), (WG, din["wg"], 2048),
                        (WO, din["wo"], 1024)]:
        wms.append(p.dma("P", dst.rearrange("p (c n) -> p c n", n=w),
                         src.rearrange("(c p) n -> p c n", p=128)))
    p.op("V", lambda v: v.memset(NEGH[:, 0:16], -0.5),
         writes=(("NEGH", 0, 16),))
    p.op("V", lambda v: v.memset(NEGH[:, 16:17], -1.0),
         writes=(("NEGH", 16, 17),))
    p.op("V", lambda v: v.memset(NEGH[:, 17:18], 127.0),
         writes=(("NEGH", 17, 18),))
    p.op("V", lambda v: v.memset(NEGH[:, 18:19], 1.0 / 127.0),
         writes=(("NEGH", 18, 19),))
    p.op("V", lambda v: v.memset(NEGH[:, 19:21], -1.0),
         writes=(("NEGH", 19, 21),))
    p.op("V", lambda v: v.memset(NEGH[:, 21:23], 127.0),
         writes=(("NEGH", 21, 23),))
    p.op("V", lambda v: v.memset(NEGH[:, 23:25], 1.0 / 127.0),
         writes=(("NEGH", 23, 25),))
    p.op("V", lambda v: v.memset(Sst[:], 0.0), writes=("Sst",),
         wait=[consts_done])
    p.op("V", lambda v: v.memset(ONE1F[:], 1.0), writes=("ONE1F",))
    wo_sc_ms = None
    for c in range(16):
        wo_sc_ms = p.op("V", lambda v, c=c: v.tensor_scalar_mul(
            WO[:, c * 1024:(c + 1) * 1024], WO[:, c * 1024:(c + 1) * 1024],
            GNCOL[:, 0:1]),
            reads=("WBIG", "GNCOL"), writes=("WBIG",),
            wait=[wms[4], consts_done] if c == 0 else None,
            inc=(c == 15))
    wready = wms[:4] + [wo_sc_ms, consts_done]

    slot_free = [None] * 6
    pt_free = {"A": None, "B": None}
    blk_end_V = {}
    x1_dma = {}
    S_done = {}
    ab_free = {}
    rope_done = {}

    projd_all = {}
    tabs = {}
    ln1ev = {}

    group_src = [(WQ, 1024, 0), (WQ, 1024, 512), (WK, 1024, 0),
                 (WK, 1024, 512), (WV, 2048, 0), (WV, 2048, 512),
                 (WV, 2048, 1024), (WV, 2048, 1536), (WG, 2048, 0),
                 (WG, 2048, 512), (WG, 2048, 1024), (WG, 2048, 1536)]

    def emit_group(b, gi):
        Wsrc, wmul, coff = group_src[gi]
        pb = b % 2
        lnb = ln1[:, pb * D:(pb + 1) * D]
        lnlb = ("ln1", pb * D, (pb + 1) * D)
        proj_ms = projd_all[b]
        si = gi % 6
        wl = [ln1ev[b]]
        if b == 0 and gi == 0:
            wl = wl + wready
        if slot_free[si] is not None:
            wl.append(slot_free[si])
        ms = None
        for c in range(8):
            ms = p.op("T", lambda t, si=si, c=c, Wsrc=Wsrc, wmul=wmul,
                      coff=coff, lnb=lnb: t.matmul(
                          slot(si), lnb[:, c * 128:(c + 1) * 128],
                          Wsrc[:, c * wmul + coff:c * wmul + coff + 512],
                          start=(c == 0), stop=(c == 7)),
                      reads=(lnlb, "WBIG"), writes=("PS",),
                      wait=wl if c == 0 else None, inc=(c == 7))
        proj_ms[gi] = ms
        if gi >= 4:
            if gi < 8:
                dstn, dst, func = "vtm", vtm[:, (gi - 4) * 512:
                                             (gi - 3) * 512], AF.Identity
                ew = [ms]
            else:
                dstn, dst, func = "gsil", gsil[:, (gi - 8) * 512:
                                               (gi - 7) * 512], AF.Silu
                ew = [ms]
                if gi == 8 and og_gate.get(b - 1) is not None:
                    ew.append(og_gate[b - 1])
            ev = p.op("S", lambda s, si=si, dst=dst, func=func:
                      s.activation(dst, slot(si), func),
                      reads=("PS",), writes=(dstn,), wait=ew, inc=True)
            slot_free[si] = ev

    def front(b):
        pb = b % 2
        so = 32 * pb
        xb = xt[:, pb * D:(pb + 1) * D]
        lnb = ln1[:, pb * D:(pb + 1) * D]
        lnlb = ("ln1", pb * D, (pb + 1) * D)
        projd_all[b] = {}

        w = ([blk_end_V[b - 2], x1_dma[b - 2]] if b >= 2
             else ([consts_done] if b == 0 else None))
        x_ms = p.dma("SP", xb, din["x"][b * 128:(b + 1) * 128, :], wait=w,
                     sem="xd%d" % pb)
        tw0 = [rope_done[b - 1]] if b >= 1 else None
        ct_ms = p.dma("SP", ctab[:], din["costab"][b * 128:(b + 1) * 128, :],
                      wait=tw0, sem="td")
        st_ms = p.dma("SP", stab[:], din["sintab"][b * 128:(b + 1) * 128, :],
                      sem="td")
        tabs[b] = st_ms

        p.op("V", lambda v, xb=xb: v.bn_stats(bn6[:, 0:6], xb[:, 0:512]),
             reads=("xt",), writes=("bn6",), wait=[x_ms])
        p.op("V", lambda v, xb=xb: v.bn_stats(bn6[:, 6:12], xb[:, 512:1024]),
             reads=("xt",), writes=("bn6",))
        p.op("V", lambda v: v.bn_aggr(bn2[:], bn6[:]),
             reads=("bn6",), writes=("bn2",))
        ve = p.op("V", lambda v, so=so: v.tensor_scalar(
            stats[:, so:so + 1], bn2[:, 1:2], EPS, None, op0=ALU.add),
            reads=("bn2",), writes=(("stats", so, so + 1),), inc=True)
        pw = p.op("P", lambda g, so=so: g.tensor_tensor(
            stats[:, so + 1:so + 2], stats[:, so:so + 1], NEGH[:, 0:1],
            op=ALU.pow),
            reads=(("stats", so, so + 1), ("NEGH", 0, 1)),
            writes=(("stats", so + 1, so + 2),), wait=[ve], inc=True)
        bc = p.op("V", lambda v, so=so: v.scalar_tensor_tensor(
            stats[:, so + 2:so + 3], bn2[:, 0:1], -1.0,
            stats[:, so + 1:so + 2], op0=ALU.mult, op1=ALU.mult),
            reads=("bn2", ("stats", so + 1, so + 2)),
            writes=(("stats", so + 2, so + 3),), wait=[pw], inc=True)
        lw = [bc]
        if b >= 2 and projd_all.get(b - 2):
            lw.append(projd_all[b - 2][11])
        ln1_ms = p.op("S", lambda s, xb=xb, so=so, lnb=lnb: s.activation(
            lnb, xb, AF.Identity, bias=stats[:, so + 2:so + 3],
            scale=stats[:, so + 1:so + 2]),
            reads=("xt", ("stats", so, so + 3)), writes=(lnlb,), wait=lw,
            inc=True)

        tw = [ln1_ms]
        if pt_free["A"] is not None:
            tw.append(pt_free["A"])
        tms = None
        for c in range(8):
            tms = p.op("T", lambda t, c=c, lnb=lnb: t.transpose(
                ptA[:, c * 128:(c + 1) * 128], lnb[:, c * 128:(c + 1) * 128],
                IDENT[:]),
                reads=(lnlb, "IDENT"), writes=("PT",),
                wait=tw if c == 0 else None, inc=(c == 7))
        ln1fm_ms = p.op("S", lambda s, lnb=lnb: s.activation(lnb, ptA,
                                                             AF.Identity),
                        reads=("PT",), writes=(lnlb,), wait=[tms],
                        inc=True)
        pt_free["A"] = ln1fm_ms
        ln1ev[b] = ln1fm_ms

        for gi in range(6):
            emit_group(b, gi)

    def back1(b):
        pb = b % 2
        so = 32 * pb
        ctb = ctab[:]
        stb = stab[:]
        proj_ms = projd_all[b]

        def emit_rope(psrc, dst, with_dkv, wait_ms, nm_dst, nm_src="QKS"):
            s1_ = psrc.rearrange("p (h two d) -> p h two d", two=2, d=32)
            x1s_, x2s_ = s1_[:, :, 0, :], s1_[:, :, 1, :]
            o1 = dst.rearrange("p (h two d) -> p h two d", two=2, d=32)
            o1a, o2a = o1[:, :, 0, :], o1[:, :, 1, :]
            cv = ctb.rearrange("p (h d) -> p h d", d=32)
            sv = stb.rearrange("p (h d) -> p h d", d=32)
            m0 = ropem[:, 0:512].rearrange("p (h d) -> p h d", d=32)
            m1 = ropem[:, 512:1024].rearrange("p (h d) -> p h d", d=32)
            dkv = DKVEB.rearrange("p (h two d) -> p h two d", two=2, d=32)
            p.op("V", lambda v: v.tensor_mul(m0, x1s_, cv),
                 reads=(nm_src, "ctab"), writes=("ropem",), wait=wait_ms)
            p.op("V", lambda v: v.tensor_mul(m1, x2s_, sv),
                 reads=(nm_src, "stab"), writes=("ropem",))
            if with_dkv:
                p.op("V", lambda v: v.tensor_sub(m0, m0, m1),
                     reads=("ropem",), writes=("ropem",))
                p.op("V", lambda v: v.tensor_mul(o1a, m0, dkv[:, :, 0, :]),
                     reads=("ropem", "DKVEB"), writes=(nm_dst,))
            else:
                p.op("V", lambda v: v.tensor_sub(o1a, m0, m1),
                     reads=("ropem",), writes=(nm_dst,))
            p.op("V", lambda v: v.tensor_mul(m0, x1s_, sv),
                 reads=(nm_src, "stab"), writes=("ropem",))
            p.op("V", lambda v: v.tensor_mul(m1, x2s_, cv),
                 reads=(nm_src, "ctab"), writes=("ropem",))
            if with_dkv:
                p.op("V", lambda v: v.tensor_add(m0, m0, m1),
                     reads=("ropem",), writes=("ropem",))
                return p.op("V", lambda v: v.tensor_mul(
                    o2a, m0, dkv[:, :, 1, :]),
                    reads=("ropem", "DKVEB"), writes=(nm_dst,), inc=True)
            return p.op("V", lambda v: v.tensor_add(o2a, m0, m1),
                        reads=("ropem",), writes=(nm_dst,), inc=True)

        qkw = [proj_ms[0], proj_ms[1]]
        if b > 0:
            qkw.append(rope_done[b - 1])
        qe_ms = p.op("S", lambda s: s.activation(QKS[:, 0:1024],
                                                 PS[:, 0:1024],
                                                 AF.Identity),
                     reads=("PS",), writes=(("QKS", 0, 1024),), wait=qkw,
                     inc=True)
        rope_w_q = [qe_ms, tabs[b]]
        if b > 0:
            rope_w_q.append(S_done[b - 1])
        qr_ms = emit_rope(QKS[:, 0:1024], qrt, False, rope_w_q, "qrt")
        kr_ms = emit_rope(PS[:, 1024:2048], krt, True,
                          [proj_ms[2], proj_ms[3]], "krt", nm_src="PS")
        rope_done[b] = kr_ms
        slot_free[0] = slot_free[1] = qe_ms
        slot_free[2] = slot_free[3] = kr_ms
        for gi in range(6, 12):
            emit_group(b, gi)
        vg_ev = slot_free[5]

        if b > 0:
            sg_v = p.op("V", lambda v: v.tensor_mul(Sst[:], Sst[:], GCTB[:]),
                        reads=("Sst", "GCTB"), writes=("Sst",),
                        wait=[S_done[b - 1]], inc=True)
            sb_ms = p.op("S", lambda s: s.activation(Sbb[:], Sst[:],
                                                     AF.Identity),
                         reads=("Sst",), writes=("Sbb",), wait=[sg_v],
                         inc=True)
        else:
            sg_v = sb_ms = None

        ev_fm = {}
        for wv_i, (src, dst, nm_s, nm_d, h0, rms_) in enumerate([
                (qrt, qfm, "qrt", "qfm", 0, None),
                (qrt, qfm, "qrt", "qfm", 8, None),
                (krt, kfm, "krt", "kfm", 0, None),
                (krt, kfm, "krt", "kfm", 8, None)]):
            bk = "A" if wv_i % 2 == 0 else "B"
            bank = ptA if bk == "A" else ptB
            wl = [qr_ms if src is qrt else kr_ms]
            if pt_free[bk] is not None:
                wl.append(pt_free[bk])
            tms = None
            for hh in range(8):
                hx = h0 + hh
                tms = p.op("T", lambda t, bank=bank, hh=hh, hx=hx, src=src:
                           t.transpose(bank[0:64, hh * 128:(hh + 1) * 128],
                                       src[:, hx * 64:(hx + 1) * 64],
                                       IDENT[:]),
                           reads=(nm_s, "IDENT"), writes=("PT",),
                           wait=wl if hh == 0 else None, inc=(hh == 7))
            ev_fm[wv_i] = p.op(
                "S", lambda s, bank=bank, dst=dst, h0=h0: s.activation(
                    dst[0:64, h0 * 128:(h0 + 8) * 128], bank[0:64, 0:1024],
                    AF.Identity),
                reads=("PT",), writes=(nm_d,), wait=[tms], inc=True)
            pt_free[bk] = ev_fm[wv_i]

        mask_ms = {}
        o_ev = {}
        for wv_i in range(4):
            si = wv_i % 2
            reg = 0
            wl = [ev_fm[1], ev_fm[3]]
            if slot_free[si] is not None:
                wl.append(slot_free[si])
            ams = None
            for hh in range(4):
                hx = wv_i * 4 + hh
                ams = p.op("T", lambda t, si=si, hh=hh, hx=hx: t.matmul(
                    slot(si)[:, hh * 128:(hh + 1) * 128],
                    kfm[0:64, hx * 128:(hx + 1) * 128],
                    qfm[0:64, hx * 128:(hx + 1) * 128],
                    start=True, stop=True),
                    reads=("kfm", "qfm"), writes=("PS",),
                    wait=wl if hh == 0 else None, inc=(hh == 3))
            mwl = [ams]
            if ab_free.get(0) is not None:
                mwl.append(ab_free[0])
            msk = None
            for pr in range(2):
                msk = p.op("V", lambda v, si=si, reg=reg, pr=pr:
                           v.tensor_mul(
                               Ab[:, reg * 512 + pr * 256:
                                  reg * 512 + (pr + 1) * 256],
                               slot(si)[:, pr * 256:(pr + 1) * 256],
                               MASKB[:]),
                           reads=("PS", "MASKB"), writes=("Ab",),
                           wait=mwl if pr == 0 else None, inc=(pr == 1))
            mask_ms[wv_i] = msk
            slot_free[si] = msk

            osi = 2 + wv_i
            owl = [msk, vg_ev]
            if sb_ms is not None:
                owl.append(sb_ms)
            if slot_free[osi] is not None:
                owl.append(slot_free[osi])
            oms = None
            for hh in range(4):
                hx = wv_i * 4 + hh
                oms = p.op("T", lambda t, osi=osi, hh=hh, hx=hx, reg=reg,
                           sf=(b == 0): t.matmul(
                               slot(osi)[:, hh * 128:(hh + 1) * 128],
                               Ab[:, reg * 512 + hh * 128:
                                   reg * 512 + (hh + 1) * 128],
                               vtm[:, hx * 128:(hx + 1) * 128],
                               start=True, stop=sf),
                           reads=("Ab", "vtm"), writes=("PS",),
                           wait=owl if hh == 0 else None,
                           inc=(b == 0 and hh == 3))
                if b > 0:
                    oms = p.op("T", lambda t, osi=osi, hh=hh, hx=hx:
                               t.matmul(
                                   slot(osi)[:, hh * 128:(hh + 1) * 128],
                                   qfm[0:64, hx * 128:(hx + 1) * 128],
                                   Sbb[0:64, hx * 128:(hx + 1) * 128],
                                   start=False, stop=True),
                               reads=("qfm", "Sbb"), writes=("PS",),
                               inc=(hh == 3))
            ab_free[0] = oms
            o_ev[wv_i] = p.op("V", lambda v, osi=osi, wv_i=wv_i:
                              v.tensor_mul(
                                  otm[:, wv_i * 512:(wv_i + 1) * 512],
                                  slot(osi),
                                  DQ2B[:, wv_i * 512:(wv_i + 1) * 512]),
                              reads=("PS", "DQ2B"), writes=("otm",),
                              wait=[oms], inc=True)
            slot_free[osi] = o_ev[wv_i]

        kv_upd = []
        for wv_i in range(4):
            si = wv_i % 2
            wl = [kr_ms, vg_ev, slot_free[si]]
            kms = None
            for hh in range(4):
                hx = wv_i * 4 + hh
                kms = p.op("T", lambda t, si=si, hh=hh, hx=hx: t.matmul(
                    slot(si)[0:64, hh * 128:(hh + 1) * 128],
                    krt[:, hx * 64:(hx + 1) * 64],
                    vtm[:, hx * 128:(hx + 1) * 128],
                    start=True, stop=True),
                    reads=("krt", "vtm"), writes=("PS",),
                    wait=wl if hh == 0 else None, inc=(hh == 3))
            if b == 0:
                msv = p.op("V", lambda v, si=si, wv_i=wv_i: v.tensor_copy(
                    Sst[0:64, wv_i * 512:(wv_i + 1) * 512],
                    slot(si)[0:64, :]),
                    reads=("PS",), writes=("Sst",), wait=[kms], inc=True)
            else:
                msv = p.op("V", lambda v, si=si, wv_i=wv_i: v.tensor_add(
                    Sst[0:64, wv_i * 512:(wv_i + 1) * 512],
                    Sst[0:64, wv_i * 512:(wv_i + 1) * 512],
                    slot(si)[0:64, :]),
                    reads=("PS", "Sst"), writes=("Sst",),
                    wait=[kms, sg_v, sb_ms], inc=True)
            kv_upd.append(msv)
            slot_free[si] = msv
        S_done[b] = kv_upd[-1]
        back_state[b] = (o_ev, )

    def back2(b):
        pb = b % 2
        so = 32 * pb
        xb = xt[:, pb * D:(pb + 1) * D]
        x1b = xb
        (o_ev,) = back_state[b]

        p.op("V", lambda v: v.tensor_mul(og[:], otm[:], otm[:]),
             reads=("otm",), writes=("og",), wait=[o_ev[3]])
        p.op("V", lambda v, so=so: v.reduce_sum(
            stats[:, so + 8:so + 24].rearrange("p (h one) -> p h one",
                                               one=1),
            og[:].rearrange("p (h d) -> p h d", d=128), axis=AX.X),
            reads=("og",), writes=(("stats", so + 8, so + 24),))
        ve2 = p.op("V", lambda v, so=so: v.tensor_scalar(
            stats[:, so + 8:so + 24], stats[:, so + 8:so + 24], 1.0 / DV,
            EPS, op0=ALU.mult, op1=ALU.add),
            reads=(("stats", so + 8, so + 24),),
            writes=(("stats", so + 8, so + 24),), inc=True)
        pw2 = p.op("P", lambda g, so=so: g.tensor_tensor(
            stats[:, so + 8:so + 24], stats[:, so + 8:so + 24],
            NEGH[:, 0:16], op=ALU.pow),
            reads=(("stats", so + 8, so + 24), ("NEGH", 0, 16)),
            writes=(("stats", so + 8, so + 24),), wait=[ve2], inc=True)
        ogm = None
        for hx in range(16):
            ogm = p.op("V", lambda v, hx=hx, so=so: v.tensor_scalar_mul(
                og[:, hx * 128:(hx + 1) * 128],
                otm[:, hx * 128:(hx + 1) * 128],
                stats[:, so + 8 + hx:so + 9 + hx]),
                reads=("otm", ("stats", so + 8, so + 24)), writes=("og",),
                wait=[pw2] if hx == 0 else None, inc=(hx == 15))
        og_ms = p.op("V", lambda v: v.tensor_mul(og[:], og[:], gsil[:]),
                     reads=("og", "gsil"), writes=("og",), inc=True)
        og_gate[b] = og_ms

        ogfm_ev = {}
        for wv_i in range(2):
            bk = "A" if wv_i == 0 else "B"
            bank = ptA if wv_i == 0 else ptB
            wl = [og_ms]
            if pt_free[bk] is not None:
                wl.append(pt_free[bk])
            tms = None
            for hh in range(8):
                c = wv_i * 8 + hh
                tms = p.op("T", lambda t, bank=bank, hh=hh, c=c: t.transpose(
                    bank[:, hh * 128:(hh + 1) * 128],
                    og[:, c * 128:(c + 1) * 128], IDENT[:]),
                    reads=("og", "IDENT"), writes=("PT",),
                    wait=wl if hh == 0 else None, inc=(hh == 7))
            ogfm_ev[wv_i] = p.op(
                "S", lambda s, bank=bank, wv_i=wv_i: s.activation(
                    og[:, wv_i * 1024:(wv_i + 1) * 1024], bank[:, 0:1024],
                    AF.Identity),
                reads=("PT",), writes=("og",), wait=[tms], inc=True)
            pt_free[bk] = ogfm_ev[wv_i]

        wo_t = {}
        for g_i in range(2):
            si = 4 + g_i
            wl = [ogfm_ev[1]]
            if slot_free[si] is not None:
                wl.append(slot_free[si])
            wms2 = None
            for c in range(16):
                wms2 = p.op("T", lambda t, si=si, c=c, g_i=g_i: t.matmul(
                    slot(si), og[:, c * 128:(c + 1) * 128],
                    WO[:, c * 1024 + g_i * 512:c * 1024 + (g_i + 1) * 512],
                    start=(c == 0), stop=(c == 15)),
                    reads=("og", "WBIG"), writes=("PS",),
                    wait=wl if c == 0 else None, inc=(c == 15))
            wo_t[g_i] = wms2

        p.op("V", lambda v: v.bn_stats(bn6[:, 0:6], PS[:, 2048:2560]),
             reads=("PS",), writes=("bn6",), wait=[wo_t[0], wo_t[1]])
        p.op("V", lambda v: v.bn_stats(bn6[:, 6:12], PS[:, 2560:3072]),
             reads=("PS",), writes=("bn6",))
        p.op("V", lambda v: v.bn_aggr(bn2[:], bn6[:]),
             reads=("bn6",), writes=("bn2",))
        ve3 = p.op("V", lambda v, so=so: v.tensor_scalar(
            stats[:, so + 4:so + 5], bn2[:, 1:2], EPS, None, op0=ALU.add),
            reads=("bn2",), writes=(("stats", so + 4, so + 5),), inc=True)
        pw3 = p.op("P", lambda g, so=so: g.tensor_tensor(
            stats[:, so + 5:so + 6], stats[:, so + 4:so + 5], NEGH[:, 0:1],
            op=ALU.pow),
            reads=(("stats", so + 4, so + 5), ("NEGH", 0, 1)),
            writes=(("stats", so + 5, so + 6),), wait=[ve3], inc=True)
        bc3 = p.op("V", lambda v, so=so: v.scalar_tensor_tensor(
            stats[:, so + 6:so + 7], bn2[:, 0:1], -1.0,
            stats[:, so + 5:so + 6], op0=ALU.mult, op1=ALU.mult),
            reads=("bn2", ("stats", so + 5, so + 6)),
            writes=(("stats", so + 6, so + 7),), wait=[pw3], inc=True)
        ln2_ms = p.op("S", lambda s, so=so: s.activation(
            ln2b[:], PS[:, 2048:3072], AF.Identity,
            bias=stats[:, so + 6:so + 7], scale=stats[:, so + 5:so + 6]),
            reads=("PS", ("stats", so + 4, so + 7)), writes=("ln2b",),
            wait=[bc3], inc=True)
        slot_free[4] = slot_free[5] = ln2_ms
        xw = [ln2_ms]
        x1_ms = p.op("V", lambda v, x1b=x1b, xb=xb: v.tensor_add(
            x1b, xb, ln2b[:]),
            reads=(("xt", pb * D, (pb + 1) * D), "ln2b"),
            writes=(("xt", pb * D, (pb + 1) * D),), wait=xw, inc=True)
        blk_end_V[b] = x1_ms
        x1_dma[b] = p.dma("SP", dx1[b * 128:(b + 1) * 128, :], x1b,
                          wait=[x1_ms], sem="x1d")

    back_state = {}
    og_gate = {}
    front(0)
    for b in range(NB):
        back1(b)
        if b + 1 < NB:
            front(b + 1)
        back2(b)

    p.op("SP", lambda s, v=p.e["SP"].dma_val: s.wait_ge(sems["SPd"], v))
    p.op("P", lambda g, v=p.e["P"].dma_val: g.wait_ge(sems["Pd"], v))
    for k in ("xd0", "xd1", "td", "x1d"):
        p.op("SP", lambda sp, k=k, v=p.dsem.get(k, 0): sp.wait_ge(
            sems[k], v))

    with nc.Block() as blk1:
        p.run(blk1)
    ctx1.close()

    ctxp = ExitStack()
    s2_ = lambda nm, shape, dt=F32: ctxp.enter_context(
        nc.sbuf_tensor(nm, list(shape), dt)).ap()
    WSTG = s2_("WSTG", (128, 2 * 4096))
    WTERN = s2_("WTERN", (128, 4096), BF16)
    WTT = s2_("WTT", (128, 4096), BF16)
    WSUM = s2_("WSUM", (128, 64))
    UUW = s2_("UUW", (128, 1024))
    G1W = s2_("G1W", (128, 1024))
    G2W = s2_("G2W", (128, 1024))

    p = Plan(nc, sems, carry=p)
    slot_free = [None] * 6
    pt_free = {"A": None, "B": None}

    sum_last2 = [None, None]
    col = 0
    for dsrc, nchunk, width in [(din["w1"], 32, 1024), (din["w2"], 8, 4096)]:
        for r in range(nchunk):
            pbuf = WSTG[:, (r % 2) * 4096:(r % 2) * 4096 + width]
            wl = [sum_last2[r % 2]] if sum_last2[r % 2] is not None else None
            dm = p.dma("P", pbuf, dsrc[r * 128:(r + 1) * 128, :], wait=wl,
                       sem="wd%d" % (r % 2))
            ms = p.op("V", lambda v, pbuf=pbuf, col=col: v.reduce_sum(
                WSUM[:, col:col + 1], pbuf, axis=AX.X,
                apply_absolute_value=True),
                reads=("WSTG",), writes=("WSUM",), wait=[dm], inc=True)
            sum_last2[r % 2] = ms
            col += 1
    p.op("V", lambda v: v.reduce_sum(WSUM[:, 40:41], WSUM[:, 0:32],
                                     axis=AX.X),
         reads=("WSUM",), writes=("WSUM",),
         wait=[sum_last2[0], sum_last2[1]])
    t12 = p.op("V", lambda v: v.reduce_sum(WSUM[:, 41:42], WSUM[:, 32:40],
                                           axis=AX.X),
               reads=("WSUM",), writes=("WSUM",), inc=True)
    wsb = p.op("V", lambda v: v.tensor_copy(WTERN[:, 0:2], WSUM[:, 40:42]),
               reads=("WSUM",), writes=("WTERN",), wait=[t12], inc=True)
    tpm = p.op("T", lambda t: t.transpose(ptA[0:2, 0:128],
                                          WTERN[:, 0:2], IDENT[:]),
               reads=("WTERN", "IDENT"), writes=("PT",), wait=[wsb],
               inc=True)
    p.op("V", lambda v: v.reduce_sum(stats[0:2, 0:1], ptA[0:2, 0:128],
                                     axis=AX.X),
         reads=("PT",), writes=("stats",), wait=[tpm])
    p.op("V", lambda v: v.tensor_scalar(stats[0:2, 1:2], stats[0:2, 0:1],
                                        1.0 / (DH * D), 1e-5,
                                        op0=ALU.mult, op1=ALU.max),
         reads=("stats",), writes=("stats",))
    rc = p.op("V", lambda v: v.reciprocal(stats[0:2, 2:3], stats[0:2, 1:2]),
              reads=("stats",), writes=("stats",), inc=True)
    rcb = p.op("V", lambda v: v.tensor_copy(WTERN[0:2, 4:6],
                                            stats[0:2, 1:3]),
               reads=("stats",), writes=("WTERN",), wait=[rc], inc=True)
    tq1 = p.op("T", lambda t: t.transpose(ptA[0:1, 0:2], WTERN[0:2, 4:5],
                                          IDENT[0:2, 0:2]),
               reads=("WTERN", "IDENT"), writes=("PT",), wait=[rcb, tpm])
    tq2 = p.op("T", lambda t: t.transpose(ptA[0:1, 2:4], WTERN[0:2, 5:6],
                                          IDENT[0:2, 0:2]),
               reads=("WTERN", "IDENT"), writes=("PT",), inc=True)
    srow = p.op("V", lambda v: v.tensor_copy(WSUM[0:1, 48:52],
                                             ptA[0:1, 0:4]),
                reads=("PT",), writes=("WSUM",), wait=[tq2], inc=True)

    def stat_bcast(src_11, dst_col, wait_ms):
        mm = p.op("T", lambda t, src_11=src_11: t.matmul(
            slot(5)[:, 0:1], ONE1F[:], src_11, start=True, stop=True),
            reads=("ONE1F", "WSUM"), writes=("PS",), wait=wait_ms, inc=True)
        return p.op("V", lambda v, dst_col=dst_col: v.tensor_copy(
            dst_col, slot(5)[:, 0:1]),
            reads=("PS",), writes=("WSC",), wait=[mm], inc=True)

    b0 = stat_bcast(WSUM[0:1, 50:51], WSC[:, 0:1], [srow])
    b1 = stat_bcast(WSUM[0:1, 48:49], WSC[:, 1:2], [b0])
    b2 = stat_bcast(WSUM[0:1, 51:52], WSC[:, 2:3], [b1])
    b3 = stat_bcast(WSUM[0:1, 49:50], WSC[:, 3:4], [b2])

    cw_dm = p.dma("P", CW[:].rearrange("p (c k) -> p c k", k=3),
                  din["cw"].rearrange("(c p) k -> p c k", p=128), sem="cwd")
    cb_dm = p.dma("P", CB[:].rearrange("p (c one) -> p c one", one=1),
                  din["cb"].rearrange("(c p) one -> p c one", p=128),
                  slow=True, sem="cwd")
    p.op("V", lambda v: v.tensor_scalar_mul(CW[:], CW[:], WSC[:, 1:2]),
         reads=("CW", "WSC"), writes=("CW",), wait=[cb_dm, b1])

    pb_free = {}
    tern_T_last = tq2
    last_store = None
    for wi, (dsrc, ddst, nchunk, width, swc) in enumerate([
            (din["w1"], dw1t, 32, 1024, WSC[:, 0:1]),
            (din["w2"], dw2t, 8, 4096, WSC[:, 2:3])]):
        for r in range(nchunk):
            pbuf = WSTG[:, (r % 2) * 4096:(r % 2) * 4096 + width]
            wl = [b3]
            prev = pb_free.get((wi, r - 2))
            if prev is None and wi == 1 and r < 2:
                prev = pb_free.get((0, 30 + r))
            if prev is not None:
                wl.append(prev)
            dm = p.dma("P", pbuf, dsrc[r * 128:(r + 1) * 128, :], wait=wl,
                       sem="wd%d" % (r % 2))
            qt_ms = None
            for st0 in range(0, width, 1024):
                wwl = [dm]
                if tern_T_last is not None and st0 == 0:
                    wwl.append(tern_T_last)
                p.op("V", lambda v, pbuf=pbuf, st0=st0, swc=swc:
                     v.tensor_scalar_mul(UUW[:], pbuf[:, st0:st0 + 1024],
                                         swc),
                     reads=("WSTG", "WSC"), writes=("UUW",), wait=wwl)
                p.op("V", lambda v: v.tensor_single_scalar(
                    G1W[:], UUW[:], 0.5, op=ALU.is_ge),
                    reads=("UUW",), writes=("G1W",))
                p.op("V", lambda v: v.tensor_single_scalar(
                    G2W[:], UUW[:], -0.5, op=ALU.is_le),
                    reads=("UUW",), writes=("G2W",))
                qt_ms = p.op("V", lambda v, st0=st0: v.tensor_sub(
                    WTERN[:, st0:st0 + 1024], G1W[:], G2W[:]),
                    reads=("G1W", "G2W"), writes=("WTERN",),
                    inc=(st0 + 1024 >= width))
            pb_free[(wi, r)] = qt_ms
            ntc = width // 128
            ev_list = []
            for w0 in range(0, ntc, 8):
                bk = "A" if (w0 // 8) % 2 == 0 else "B"
                bank = ptA if bk == "A" else ptB
                nw = min(8, ntc - w0)
                wl2 = [qt_ms]
                if pt_free[bk] is not None:
                    wl2.append(pt_free[bk])
                tms = None
                for k in range(nw):
                    c = w0 + k
                    tms = p.op("T", lambda t, bank=bank, k=k, c=c:
                               t.transpose(bank[:, k * 128:(k + 1) * 128],
                                           WTERN[:, c * 128:(c + 1) * 128],
                                           IDENT[:]),
                               reads=("WTERN", "IDENT"), writes=("PT",),
                               wait=wl2 if k == 0 else None,
                               inc=(k == nw - 1))
                ewl = [tms]
                if last_store is not None and w0 == 0:
                    ewl.append(last_store)
                ev = p.op("S", lambda s, bank=bank, w0=w0, nw=nw:
                          s.activation(WTT[:, w0 * 128:(w0 + nw) * 128],
                                       bank[:, 0:nw * 128], AF.Identity),
                          reads=("PT",), writes=("WTT",), wait=ewl, inc=True)
                ev_list.append(ev)
                pt_free[bk] = ev
            tern_T_last = tms
            for ci in range(ntc):
                last_store = p.dma(
                    "SP",
                    ddst[ci * 128:(ci + 1) * 128, r * 128:(r + 1) * 128],
                    WTT[:, ci * 128:(ci + 1) * 128],
                    wait=[ev_list[ci // 8]] if ci % 8 == 0 else None)

    p.dma("SP", W1T.rearrange("p (c n) -> p c n", n=4096),
          dw1t.rearrange("(c p) n -> p c n", p=128), wait=[last_store])
    p.dma("SP", W2T.rearrange("p (c n) -> p c n", n=1024),
          dw2t.rearrange("(c p) n -> p c n", p=128))
    p.op("SP", lambda s, v=p.e["SP"].dma_val: s.wait_ge(sems["SPd"], v))
    p.op("P", lambda g, v=p.e["P"].dma_val: g.wait_ge(sems["Pd"], v))

    with nc.Block() as blk2:
        p.run(blk2)
    ctxp.close()

    ctx3 = ExitStack()
    s3_ = lambda nm, shape, dt=F32: ctx3.enter_context(
        nc.sbuf_tensor(nm, list(shape), dt)).ap()
    X1T = s3_("X1T", (128, 2 * 2048))
    LN3F = s3_("LN3F", (128, 1024))
    XQTM = s3_("XQTM", (128, 1024), BF16)
    XQFM = s3_("XQFM", (128, 2048), BF16)
    HT0 = s3_("HT0", (128, 32 * 258), BF16)
    HT1 = s3_("HT1", (128, 32 * 258), BF16)
    A0B = s3_("A0B", (128, 2 * 256))
    ABQ = s3_("ABQ", (128, 2 * 512), BF16)
    ABP = ABQ
    HQ = ABQ
    RMA = s3_("RMA", (128, 512), BF16)
    RMB = s3_("RMB", (128, 512), BF16)
    RMF = RMB[:, 0:256]
    SB2S = s3_("SB2S", (128, 2), BF16)
    UU = s3_("UU", (128, 2 * 512))
    IA = s3_("IA", (128, 2 * 512), I32)
    RR = IA
    SBCS = RMA
    SB2 = s3_("SB2", (2, 128), BF16)
    SROWB = s3_("SROWB", (1, 256), BF16)
    FFO = s3_("FFO", (128, 1024))
    LN4B = FFO

    p = Plan(nc, sems, carry=p)
    slot_free = [None] * 6
    pt_free = {"A": None, "B": None}

    HTS = [HT0, HT1]
    HTnm = ["HT0", "HT1"]

    def hmid(ht, c, a, bcol):
        return ht[:, c * 258 + a:c * 258 + bcol]

    mm1_ev = {}
    win_out_dma = {}
    ffo_done = {}
    halo_ms = {}
    xq_evs = {}
    ln3f_free = [None]
    m2done = {}
    evac_ms = {}

    mz_ms = None
    for c in range(32):
        mz_ms = p.op("V", lambda v, c=c: v.memset(hmid(HT0, c, 0, 1), 0.0),
                     writes=("HT0",), inc=(c == 31))
    halo_ms[0] = mz_ms

    WST = {}
    m2v1_last = {}

    def emit_quant(n):
        pb = n % 2
        pb = n % 2
        X1 = X1T[:, pb * 2048:(pb + 1) * 2048]
        HT = HTS[pb]
        wl = []
        if n >= 2:
            wl.append(win_out_dma.get(n - 2))
        x_ms = p.dma("SP", X1.rearrange("p (jj nn) -> p jj nn", nn=1024),
                     dx1[n * 256:(n + 1) * 256, :].rearrange(
                         "(jj pp) nn -> pp jj nn", pp=128),
                     wait=[m for m in wl if m is not None] or None,
                     sem="xd%d" % pb)
        tp_evs = []
        for jj in range(2):
            so = 48 * pb + 8 * jj
            x1lb = ("X1T", pb * 2048 + jj * 1024, pb * 2048 + (jj + 1) * 1024)
            xb = X1[:, jj * 1024:(jj + 1) * 1024]
            p.op("V", lambda v, xb=xb: v.bn_stats(bn6[:, 0:6],
                                                  xb[:, 0:512]),
                 reads=(x1lb,), writes=(("bn6", 0, 6),), wait=[x_ms])
            p.op("V", lambda v, xb=xb: v.bn_stats(bn6[:, 6:12],
                                                  xb[:, 512:1024]),
                 reads=(x1lb,), writes=(("bn6", 6, 12),))
            p.op("V", lambda v: v.bn_aggr(bn2[:], bn6[:]),
                 reads=("bn6",), writes=("bn2",))
            ve = p.op("V", lambda v, so=so: v.tensor_scalar(
                stats[:, so:so + 1], bn2[:, 1:2], EPS, None, op0=ALU.add),
                reads=("bn2",), writes=(("stats", so, so + 1),), inc=True)
            pw = p.op("P", lambda g, so=so: g.tensor_tensor(
                stats[:, so + 1:so + 2], stats[:, so:so + 1], NEGH[:, 0:1],
                op=ALU.pow),
                reads=(("stats", so, so + 1), ("NEGH", 0, 1)),
                writes=(("stats", so + 1, so + 2),), wait=[ve], inc=True)
            bc = p.op("V", lambda v, so=so: v.scalar_tensor_tensor(
                stats[:, so + 2:so + 3], bn2[:, 0:1], -1.0,
                stats[:, so + 1:so + 2], op0=ALU.mult, op1=ALU.mult),
                reads=("bn2", ("stats", so + 1, so + 2)),
                writes=(("stats", so + 2, so + 3),), wait=[pw], inc=True)
            lnm = p.op("S", lambda s, xb=xb, so=so: s.activation(
                LN3F[:], xb, AF.Identity, bias=stats[:, so + 2:so + 3],
                scale=stats[:, so + 1:so + 2]),
                reads=(x1lb, ("stats", so, so + 3)), writes=("LN3F",),
                wait=[bc, ln3f_free[0]] if ln3f_free[0] else [bc],
                inc=True)
            mx = p.op("V", lambda v, so=so: v.reduce_max(
                stats[:, so + 3:so + 4], LN3F[:], axis=AX.X,
                apply_absolute_value=True),
                reads=("LN3F",), writes=(("stats", so + 3, so + 4),),
                wait=[lnm], inc=True)
            mxc = p.op("V", lambda v, so=so: v.tensor_scalar_max(
                stats[:, so + 4:so + 5], stats[:, so + 3:so + 4], 1e-5),
                reads=(("stats", so + 3, so + 4),),
                writes=(("stats", so + 4, so + 5),), inc=True)
            p.op("P", lambda g, so=so: g.tensor_tensor(
                stats[:, so + 5:so + 6], stats[:, so + 4:so + 5],
                NEGH[:, 16:17], op=ALU.pow),
                reads=(("stats", so + 4, so + 5), ("NEGH", 16, 17)),
                writes=(("stats", so + 5, so + 6),), wait=[mxc])
            p.op("P", lambda g, so=so: g.tensor_tensor(
                stats[:, so + 6:so + 7], stats[:, so + 5:so + 6],
                NEGH[:, 17:18], op=ALU.mult),
                reads=(("stats", so + 5, so + 6), ("NEGH", 17, 18)),
                writes=(("stats", so + 6, so + 7),))
            sc_ms = p.op("P", lambda g, so=so: g.tensor_tensor(
                stats[:, so + 7:so + 8], stats[:, so + 4:so + 5],
                NEGH[:, 18:19], op=ALU.mult),
                reads=(("stats", so + 4, so + 5), ("NEGH", 18, 19)),
                writes=(("stats", so + 7, so + 8),), inc=True)
            xq_ms = None
            sl = lambda st: (("UU", st, st + 512), ("IA", st, st + 512))
            for step in range(5):
                for sp_ in range(2):
                    st = sp_ * 512
                    lf = LN3F[:, st:st + 512]
                    uu = UU[:, st:st + 512]
                    ia = IA[:, st:st + 512]
                    if step == 0:
                        uwl = None
                        if sp_ == 0:
                            uwl = [sc_ms]
                        p.op("V", lambda v, lf=lf, uu=uu, so=so:
                             v.tensor_scalar(
                                 uu, lf, stats[:, so + 6:so + 7], 0.5,
                                 op0=ALU.mult, op1=ALU.add),
                             reads=(("LN3F", st, st + 512),
                                    ("stats", so + 6, so + 7)),
                             writes=(("UU", st, st + 512),), wait=uwl)
                    elif step == 1:
                        p.op("V", lambda v, uu=uu, ia=ia:
                             v.tensor_copy(ia, uu),
                             reads=(("UU", st, st + 512),),
                             writes=(("IA", st, st + 512),))
                    elif step == 2:
                        p.op("V", lambda v, uu=uu, ia=ia: v.tensor_tensor(
                            uu, uu, ia, op=ALU.subtract),
                            reads=sl(st), writes=(("UU", st, st + 512),))
                    elif step == 3:
                        p.op("V", lambda v, uu=uu, ia=ia:
                             v.scalar_tensor_tensor(
                                 ia, uu, 0.0, ia, op0=ALU.is_ge,
                                 op1=ALU.add),
                             reads=sl(st), writes=(("IA", st, st + 512),))
                    else:
                        xwl = None
                        if sp_ == 0:
                            prev_key = (n, 0) if jj == 1 else (n - 1, 1)
                            if prev_key in xq_evs:
                                xwl = [xq_evs[prev_key]]
                        xq_ms = p.op("V", lambda v, ia=ia, st=st, so=so:
                                     v.tensor_scalar(
                                         XQTM[:, st:st + 512], ia, -1.0,
                                         stats[:, so + 7:so + 8],
                                         op0=ALU.add, op1=ALU.mult),
                                     reads=(("IA", st, st + 512),
                                            ("stats", so + 7, so + 8)),
                                     writes=(("XQTM", st, st + 512),),
                                     wait=xwl, inc=(sp_ == 1))
            ln3f_free[0] = xq_ms
            bk = "A" if jj == 0 else "B"
            bank = ptA if jj == 0 else ptB
            wl2 = [xq_ms]
            if pt_free[bk] is not None:
                wl2.append(pt_free[bk])
            tms = None
            for c in range(8):
                tms = p.op("T", lambda t, bank=bank, c=c: t.transpose(
                    bank[:, c * 128:(c + 1) * 128],
                    XQTM[:, c * 128:(c + 1) * 128], IDENT[:]),
                    reads=("XQTM", "IDENT"), writes=("PT",),
                    wait=wl2 if c == 0 else None, inc=(c == 7))
            ev = p.op("S", lambda s, bank=bank, jj=jj: s.activation(
                XQFM[:].rearrange("p (c j2 nn) -> p c j2 nn", j2=2, nn=128)
                [:, :, jj, :],
                bank[:, 0:1024].rearrange("p (c nn) -> p c nn", nn=128),
                AF.Identity),
                reads=("PT",), writes=("XQFM",), wait=[tms], inc=True)
            xq_evs[(n, jj)] = ev
            pt_free[bk] = ev
            tp_evs.append(ev)
        WST[n] = {"tp": tp_evs, "sil": {}, "add2": {}, "tree": {}}

    def emit_conv_group(n, g):
        pb = n % 2
        HT = HTS[pb]
        htn = HTnm[pb]
        htv = HT.rearrange("p (c e) -> p c e", e=258)
        st_ = WST[n]
        sil_ms = st_["sil"]
        add2_ms = st_["add2"]
        tree_ms = st_["tree"]
        nxtv = None
        if n + 1 < NW:
            nxtv = HTS[(n + 1) % 2].rearrange("p (c e) -> p c e", e=258)
        if nxtv is not None:
            p.op("V", lambda v, g=g, nxtv=nxtv, htv=htv: v.tensor_copy(
                htv[:, g * 8:(g + 1) * 8, 257:258],
                nxtv[:, g * 8:(g + 1) * 8, 1:2]),
                reads=(HTnm[(n + 1) % 2],), writes=(htn,),
                wait=[evac_ms[(n + 1, g * 8 + 7)]])
            hl = p.op("V", lambda v, g=g, nxtv=nxtv, htv=htv:
                      v.tensor_copy(
                          nxtv[:, g * 8:(g + 1) * 8, 0:1],
                          htv[:, g * 8:(g + 1) * 8, 256:257]),
                      reads=(htn,), writes=(HTnm[(n + 1) % 2],),
                      wait=[evac_ms[(n, g * 8 + 7)]], inc=(g == 3))
            if g == 3:
                halo_ms[n + 1] = hl
        elif g == 0:
            p.op("V", lambda v, htv=htv: v.memset(htv[:, :, 257:258],
                                                   0.0),
                 writes=(htn,))
        for c in range(g * 8, g * 8 + 8):
            s4 = c % 2
            a0 = A0B[:, s4 * 256:(s4 + 1) * 256]
            l0 = ("A0B", s4 * 256, (s4 + 1) * 256)
            swl = []
            if c == 0:
                swl.append(halo_ms[n])
            if c >= 2:
                swl.append(add2_ms[c - 2])
            t0 = p.op("S", lambda s, c=c, a0=a0, HT=HT: s.activation(
                a0, hmid(HT, c, 0, 256), AF.Identity,
                scale=CW[:, c * 3:c * 3 + 1], bias=CB[:, c:c + 1]),
                reads=((htn, c * 258, c * 258 + 256), "CW", "CB"),
                writes=(l0,), wait=swl or None, inc=True)
            p.op("V", lambda v, c=c, a0=a0, HT=HT: v.scalar_tensor_tensor(
                a0, hmid(HT, c, 1, 257), CW[:, c * 3 + 1:c * 3 + 2], a0,
                op0=ALU.mult, op1=ALU.add),
                reads=((htn, c * 258 + 1, c * 258 + 257), "CW", l0),
                writes=(l0,), wait=[t0])
            a_ms = p.op("V", lambda v, c=c, a0=a0, HT=HT:
                        v.scalar_tensor_tensor(
                            a0, hmid(HT, c, 2, 258),
                            CW[:, c * 3 + 2:c * 3 + 3], a0,
                            op0=ALU.mult, op1=ALU.add),
                        reads=((htn, c * 258 + 2, c * 258 + 258), "CW", l0),
                        writes=(l0,), inc=True)
            add2_ms[c] = a_ms
            sil_ms[c] = p.op("S", lambda s, c=c, a0=a0, HT=HT: s.activation(
                hmid(HT, c, 1, 257), a0, AF.Silu),
                reads=(l0,),
                writes=((htn, c * 258 + 1, c * 258 + 257),),
                wait=[a_ms], inc=True)

        for pr in range(4 * g, 4 * g + 4):
            s2 = pr % 2
            abp = ABP[:, s2 * 512:(s2 + 1) * 512]
            lab = ("ABQ", s2 * 512, (s2 + 1) * 512)
            awl = []
            if pr >= 2:
                awl.append(tree_ms[pr - 2])
            elif m2done.get(n - 1) is not None:
                awl.append(m2done[n - 1])
            ab = p.op("S", lambda s, pr=pr, abp=abp: s.activation(
                abp, htv[:, 2 * pr:2 * pr + 2, 1:257], AF.Abs),
                reads=((htn, 2 * pr * 258, (2 * pr + 2) * 258),),
                writes=(lab,), wait=awl or None, inc=True)
            lane = pr % 2
            dst = RMA if lane == 0 else RMB
            dn = "RMA" if lane == 0 else "RMB"
            if pr < 2:
                tree_ms[pr] = p.op("V", lambda v, dst=dst, abp=abp:
                                   v.tensor_copy(dst[:], abp),
                                   reads=(lab,), writes=(dn,),
                                   wait=[ab], inc=True)
            else:
                tree_ms[pr] = p.op("V", lambda v, dst=dst, abp=abp:
                                   v.tensor_tensor(dst[:], dst[:], abp,
                                                   op=ALU.max),
                                   reads=(lab, dn), writes=(dn,),
                                   wait=[ab], inc=True)

    def emit_pipe(n):
        pb = n % 2
        so = 48 * pb + 16
        st_ = WST[n]
        tree_ms = st_["tree"]
        p.op("V", lambda v: v.tensor_tensor(RMA[:], RMA[:], RMB[:],
                                            op=ALU.max),
             reads=("RMA", "RMB"), writes=("RMA",),
             wait=[tree_ms[14], tree_ms[15]])
        rb = p.op("V", lambda v: v.tensor_tensor(
            RMF[:], RMA[:, 0:256], RMA[:, 256:512], op=ALU.max),
            reads=("RMA",), writes=(("RMB", 0, 256),), inc=True)

        wl2 = [rb]
        if pt_free["A"] is not None:
            wl2.append(pt_free["A"])
        tms = None
        for jj in range(2):
            tms = p.op("T", lambda t, jj=jj: t.transpose(
                ptA[:, jj * 128:(jj + 1) * 128],
                RMF[:, jj * 128:(jj + 1) * 128], IDENT[:]),
                reads=(("RMB", 0, 256), "IDENT"), writes=("PT",),
                wait=wl2 if jj == 0 else None, inc=(jj == 1))
        mxc = p.op("V", lambda v, so=so: v.reduce_max(
            stats[:, so:so + 2].rearrange("p (j2 one) -> p j2 one", one=1),
            ptA[:, 0:256].rearrange("p (j2 nn) -> p j2 nn", nn=128),
            axis=AX.X),
            reads=("PT",), writes=(("stats", so, so + 2),), wait=[tms],
            inc=True)
        pt_free["A"] = mxc
        clip = p.op("V", lambda v, so=so: v.tensor_scalar_max(
            stats[:, so + 2:so + 4], stats[:, so:so + 2], 1e-5),
            reads=(("stats", so, so + 2),),
            writes=(("stats", so + 2, so + 4),), inc=True)
        p.op("P", lambda g, so=so: g.tensor_tensor(
            stats[:, so + 4:so + 6], stats[:, so + 2:so + 4],
            NEGH[:, 19:21], op=ALU.pow),
            reads=(("stats", so + 2, so + 4), ("NEGH", 19, 21)),
            writes=(("stats", so + 4, so + 6),), wait=[clip])
        s127p = p.op("P", lambda g, so=so: g.tensor_tensor(
            stats[:, so + 6:so + 8], stats[:, so + 4:so + 6],
            NEGH[:, 21:23], op=ALU.mult),
            reads=(("stats", so + 4, so + 6), ("NEGH", 21, 23)),
            writes=(("stats", so + 6, so + 8),), inc=True)
        p.op("P", lambda g, so=so: g.tensor_tensor(
            stats[:, so + 8:so + 10], stats[:, so + 2:so + 4],
            NEGH[:, 23:25], op=ALU.mult),
            reads=(("stats", so + 2, so + 4), ("NEGH", 23, 25)),
            writes=(("stats", so + 8, so + 10),))
        c2a = p.op("P", lambda g, so=so: g.tensor_tensor(
            stats[:, so + 10:so + 11], stats[:, so + 8:so + 9],
            WSC[:, 3:4], op=ALU.mult),
            reads=(("stats", so + 8, so + 9), "WSC"),
            writes=(("stats", so + 10, so + 11),), inc=True)
        c2_ms = p.op("P", lambda g, so=so: g.tensor_tensor(
            stats[:, so + 11:so + 12], stats[:, so + 9:so + 10],
            WSC[:, 3:4], op=ALU.mult),
            reads=(("stats", so + 9, so + 10), "WSC"),
            writes=(("stats", so + 11, so + 12),), inc=True)
        sb2_ms = p.op("S", lambda s, so=so: s.activation(
            SB2S[:], stats[:, so + 6:so + 8], AF.Identity),
            reads=(("stats", so + 6, so + 8),), writes=("SB2S",),
            wait=[s127p], inc=True)
        wl3 = [sb2_ms]
        if pt_free["B"] is not None:
            wl3.append(pt_free["B"])
        st2 = p.op("T", lambda t: t.transpose(ptB[0:2, 0:128], SB2S[:],
                                              IDENT[:]),
                   reads=("SB2S", "IDENT"), writes=("PT",), wait=wl3,
                   inc=True)
        sb2ev = p.op("S", lambda s: s.activation(SB2[:], ptB[0:2, 0:128],
                                                 AF.Identity),
                     reads=("PT",), writes=("SB2",), wait=[st2], inc=True)
        pt_free["B"] = sb2ev
        sr_dma = p.dma("SP", SROWB[0:1, :], SB2[:], wait=[sb2ev],
                       sem="srd")
        bwl = [sr_dma]
        if ffo_done.get(n - 1) is not None:
            bwl.append(ffo_done[n - 1])
        if slot_free[2] is not None:
            bwl.append(slot_free[2])
        p.op("T", lambda t: t.matmul(
            slot(2)[:, 0:256], ONE1[:], SROWB[0:1, :], start=True,
            stop=True),
            reads=("ONE1", "SROWB"), writes=("PS",), wait=bwl)
        bc_mm = p.op("T", lambda t: t.matmul(
            slot(2)[:, 256:512], ONE1[:], SROWB[0:1, :], start=True,
            stop=True),
            reads=("ONE1", "SROWB"), writes=("PS",), inc=True)
        sbc_ms = p.op("S", lambda s: s.activation(SBCS[:], slot(2)[:, :],
                                                  AF.Identity),
                      reads=("PS",), writes=("RMA",), wait=[bc_mm],
                      inc=True)
        slot_free[2] = sbc_ms
        st_["sbc"] = sbc_ms
        st_["c2"] = c2_ms

    def emit_mm2_out(n):
        pb = n % 2
        HT = HTS[pb]
        htn = HTnm[pb]
        htv = HT.rearrange("p (c e) -> p c e", e=258)
        X1 = X1T[:, pb * 2048:(pb + 1) * 2048]
        so = 48 * pb + 16
        lb = 48 * pb + 28
        st_ = WST[n]
        sil_ms = st_["sil"]
        sbc_ms = st_["sbc"]
        c2_ms = st_["c2"]
        m2slots = [3, 4, 5, 2]
        mm_last = {}
        hq_ms = {}
        for pg in range(8):
            for step in range(5):
                for k in range(2):
                    pr = 2 * pg + k
                    st = k * 512
                    uu = UU[:, st:st + 512]
                    ia = IA[:, st:st + 512]
                    luu = ("UU", st, st + 512)
                    lia = ("IA", st, st + 512)
                    lhq = ("ABQ", st, st + 512)
                    if step == 0:
                        wl = [sil_ms[2 * pr + 1]]
                        if pr == 0:
                            wl.append(sbc_ms)
                        v1 = p.op("V", lambda v, pr=pr, uu=uu:
                                  v.tensor_tensor(
                                      uu, htv[:, 2 * pr:2 * pr + 2, 1:257],
                                      SBCS[:], op=ALU.mult),
                                  reads=((htn, 2 * pr * 258,
                                          (2 * pr + 2) * 258), "RMA"),
                                  writes=(luu,), wait=wl,
                                  inc=(pr == 15))
                        if pr == 15:
                            m2v1_last[n] = v1
                    elif step == 1:
                        p.op("V", lambda v, uu=uu, ia=ia: v.tensor_scalar(
                            ia, uu, 0.5, None, op0=ALU.add),
                            reads=(luu,), writes=(lia,))
                    elif step == 2:
                        p.op("V", lambda v, uu=uu, ia=ia:
                             v.scalar_tensor_tensor(
                                 uu, uu, 0.5, ia, op0=ALU.add,
                                 op1=ALU.subtract),
                             reads=(luu, lia), writes=(luu,))
                    elif step == 3:
                        p.op("V", lambda v, uu=uu, ia=ia:
                             v.scalar_tensor_tensor(
                                 ia, uu, 0.0, ia, op0=ALU.is_ge,
                                 op1=ALU.add),
                             reads=(luu, lia), writes=(lia,))
                    else:
                        hwl = [mm_last[pr - 2]] if pr >= 2 else None
                        hq_ms[pr] = p.op("V", lambda v, ia=ia, st=st:
                                         v.tensor_scalar(
                                             HQ[:, st:st + 512], ia, -1.0,
                                             None, op0=ALU.add),
                                         reads=(lia,), writes=(lhq,),
                                         wait=hwl, inc=True)
            for k in range(2):
                pr = 2 * pg + k
                st = k * 512
                lhq = ("ABQ", st, st + 512)
                last = None
                for ci in range(2):
                    c = 2 * pr + ci
                    for jj2 in range(2):
                        for g_i in range(2):
                            si = m2slots[jj2 * 2 + g_i]
                            wlm = None
                            if ci == 0 and jj2 == 0 and g_i == 0:
                                wlm = [hq_ms[pr]]
                                if c == 0:
                                    if ffo_done.get(n - 1) is not None:
                                        wlm.append(ffo_done[n - 1])
                                    for s_ in (3, 4, 5):
                                        if slot_free[s_] is not None:
                                            wlm.append(slot_free[s_])
                            if c == 0 and si == 2 and wlm is None:
                                wlm = [sbc_ms]
                            last = p.op("T", lambda t, si=si, c=c, jj2=jj2,
                                        g_i=g_i, st=st: t.matmul(
                                            slot(si),
                                            HQ[:, st + (c % 2) * 256 +
                                               jj2 * 128:
                                               st + (c % 2) * 256 +
                                               (jj2 + 1) * 128],
                                            W2T[:, c * 1024 + g_i * 512:
                                                c * 1024 + (g_i + 1) * 512],
                                            start=(c == 0), stop=(c == 31)),
                                        reads=(lhq, "WBIG"), writes=("PS",),
                                        wait=wlm,
                                        inc=(ci == 1 and jj2 == 1
                                             and g_i == 1))
                mm_last[pr] = last
        m2done[n] = mm_last[15]

        prev_out = None
        out_ms = []
        for jj2 in range(2):
            l4 = lb + 4 * jj2
            evs = []
            for g_i in range(2):
                si = m2slots[jj2 * 2 + g_i]
                evw = [mm_last[15], c2_ms]
                if jj2 == 0 and g_i == 0 and ffo_done.get(n - 1) is not None:
                    evw.append(ffo_done[n - 1])
                if jj2 == 1 and g_i == 0:
                    evw.append(prev_out)
                ev = p.op("S", lambda s, si=si, g_i=g_i, jj2=jj2, so=so:
                          s.activation(
                              FFO[:, g_i * 512:(g_i + 1) * 512], slot(si),
                              AF.Identity,
                              scale=stats[:, so + 10 + jj2:so + 11 + jj2]),
                          reads=("PS", ("stats", so + 10, so + 12)),
                          writes=(("FFO", g_i * 512, (g_i + 1) * 512),),
                          wait=evw, inc=True)
                evs.append(ev)
                slot_free[si] = ev
            p.op("V", lambda v: v.bn_stats(bn6[:, 0:6], FFO[:, 0:512]),
                 reads=(("FFO", 0, 512),), writes=(("bn6", 0, 6),),
                 wait=evs)
            p.op("V", lambda v: v.bn_stats(bn6[:, 6:12], FFO[:, 512:1024]),
                 reads=(("FFO", 512, 1024),), writes=(("bn6", 6, 12),))
            p.op("V", lambda v: v.bn_aggr(bn2[:], bn6[:]),
                 reads=("bn6",), writes=("bn2",))
            ve = p.op("V", lambda v, l4=l4: v.tensor_scalar(
                stats[:, l4:l4 + 1], bn2[:, 1:2], EPS, None, op0=ALU.add),
                reads=("bn2",), writes=(("stats", l4, l4 + 1),), inc=True)
            pw = p.op("P", lambda g, l4=l4: g.tensor_tensor(
                stats[:, l4 + 1:l4 + 2], stats[:, l4:l4 + 1],
                NEGH[:, 0:1], op=ALU.pow),
                reads=(("stats", l4, l4 + 1), ("NEGH", 0, 1)),
                writes=(("stats", l4 + 1, l4 + 2),), wait=[ve], inc=True)
            bc = p.op("V", lambda v, l4=l4: v.scalar_tensor_tensor(
                stats[:, l4 + 2:l4 + 3], bn2[:, 0:1], -1.0,
                stats[:, l4 + 1:l4 + 2], op0=ALU.mult, op1=ALU.mult),
                reads=("bn2", ("stats", l4 + 1, l4 + 2)),
                writes=(("stats", l4 + 2, l4 + 3),), wait=[pw], inc=True)
            lnm = p.op("S", lambda s, l4=l4: s.activation(
                FFO[:], FFO[:], AF.Identity,
                bias=stats[:, l4 + 2:l4 + 3],
                scale=stats[:, l4 + 1:l4 + 2]),
                reads=("FFO", ("stats", l4, l4 + 3)), writes=("FFO",),
                wait=[bc], inc=True)
            om = p.op("V", lambda v, jj2=jj2, X1=X1: v.tensor_tensor(
                X1[:, jj2 * 1024:(jj2 + 1) * 1024],
                X1[:, jj2 * 1024:(jj2 + 1) * 1024], FFO[:], op=ALU.add),
                reads=(("X1T", pb * 2048 + jj2 * 1024,
                        pb * 2048 + (jj2 + 1) * 1024), "FFO"),
                writes=(("X1T", pb * 2048 + jj2 * 1024,
                         pb * 2048 + (jj2 + 1) * 1024),),
                wait=[lnm], inc=True)
            out_ms.append(om)
            prev_out = om
        ffo_done[n] = out_ms[-1]
        win_out_dma[n] = p.dma(
            "SP", dy[n * 256:(n + 1) * 256, :].rearrange(
                "(jj pp) nn -> pp jj nn", pp=128),
            X1.rearrange("p (jj nn) -> p jj nn", nn=1024),
            wait=out_ms, sem="od")

    for n in range(NW + 2):
        if n >= 2:
            emit_mm2_out(n - 2)
        if n <= NW - 1:
            emit_quant(n)
            pb = n % 2
            HT = HTS[pb]
            tp_evs = WST[n]["tp"]
            lastev = None
            for hfc in range(32):
                si = hfc % 4
                wl = list(tp_evs) if hfc < 2 else []
                if slot_free[si] is not None:
                    wl.append(slot_free[si])
                mms = None
                for c in range(8):
                    mms = p.op("T", lambda t, si=si, c=c, hfc=hfc:
                               t.matmul(
                                   slot(si)[:, 0:256],
                                   W1T[:, c * 4096 + hfc * 128:
                                       c * 4096 + (hfc + 1) * 128],
                                   XQFM[:, c * 256:(c + 1) * 256],
                                   start=(c == 0), stop=(c == 7)),
                               reads=("WBIG", "XQFM"),
                               writes=("PS",),
                               wait=(wl or None) if c == 0 else None,
                               inc=(c == 7))
                ewl = [mms]
                if hfc == 0 and (n - 2) in m2v1_last:
                    ewl.append(m2v1_last[n - 2])
                lastev = p.op("S", lambda s, si=si, hfc=hfc, HT=HT:
                              s.activation(
                                  hmid(HT, hfc, 1, 257),
                                  slot(si)[:, 0:256], AF.Identity),
                              reads=("PS",),
                              writes=((HTnm[pb], hfc * 258 + 1,
                                       hfc * 258 + 257),),
                              wait=ewl, inc=True)
                slot_free[si] = lastev
                evac_ms[(n, hfc)] = lastev
                if hfc % 8 == 7 and n >= 1:
                    emit_conv_group(n - 1, hfc // 8)
            mm1_ev[n] = lastev
        elif n == NW:
            for g in range(4):
                emit_conv_group(NW - 1, g)
        if 1 <= n <= NW:
            emit_pipe(n - 1)

    p.op("SP", lambda s, v=p.e["SP"].dma_val: s.wait_ge(sems["SPd"], v))
    p.op("P", lambda g, v=p.e["P"].dma_val: g.wait_ge(sems["Pd"], v))
    for k in ("xd0", "xd1", "td", "x1d", "wd0", "wd1", "cwd", "srd", "od"):
        p.op("SP", lambda sp, k=k, v=p.dsem.get(k, 0): sp.wait_ge(
            sems[k], v))

    with nc.Block() as blk3:
        p.run(blk3)
    ctx3.close()

    return nc


def kernel(L0_attn_x, L0_attn_Wq, L0_attn_Wk, L0_attn_Wv, L0_attn_Wg,
           L0_attn_Wo, L0_attn_g_norm_w, L0_ffn_w1, L0_ffn_conv_w,
           L0_ffn_conv_b, L0_ffn_w2):
    x = np.ascontiguousarray(np.asarray(L0_attn_x, dtype=np.float32))
    B, T, D_ = x.shape
    assert (B, D_) == (NCORES, D)

    consts = make_consts(T)
    nc = build_program(T)

    common = {
        "wq": np.ascontiguousarray(L0_attn_Wq, np.float32),
        "wk": np.ascontiguousarray(L0_attn_Wk, np.float32),
        "wv": np.ascontiguousarray(L0_attn_Wv, np.float32),
        "wg": np.ascontiguousarray(L0_attn_Wg, np.float32),
        "wo": np.ascontiguousarray(L0_attn_Wo, np.float32),
        "gnw": np.ascontiguousarray(
            np.asarray(L0_attn_g_norm_w, np.float32).reshape(DV, 1)),
        "w1": np.ascontiguousarray(L0_ffn_w1, np.float32),
        "cw": np.ascontiguousarray(
            np.asarray(L0_ffn_conv_w, np.float32).reshape(DH, 3)),
        "cb": np.ascontiguousarray(
            np.asarray(L0_ffn_conv_b, np.float32).reshape(DH, 1)),
        "w2": np.ascontiguousarray(L0_ffn_w2, np.float32),
        **consts,
    }
    in_maps = []
    for i in range(NCORES):
        m = dict(common)
        m["x"] = np.ascontiguousarray(x[i])
        in_maps.append(m)

    res = run_bass_kernel_spmd(nc, in_maps, core_ids=list(range(NCORES)))
    out = np.stack([np.asarray(r["y"], np.float32) for r in res.results])
    return out
